# version 04
# baseline (speedup 1.0000x reference)
import numpy as np
from contextlib import ExitStack

import concourse.bass as bass
import concourse.mybir as mybir
from concourse.bass_utils import run_bass_kernel_spmd

F32 = mybir.dt.float32
BF16 = mybir.dt.bfloat16
DR = mybir.MatmulPerfMode.DoubleRow
OP = mybir.AluOpType
AF = mybir.ActivationFunctionType
AX = mybir.AxisListType

DM = 1024
S = 2048
DI = 2048
NST = 16
DTR = 64
FF = 4096
NE = 8
NCORE = 8
CH = 512
NCH = S // CH
NT_DM = DM // 128
NT_DI = DI // 128
NT_F = FF // 128

_N_SUB = 4
_DEBUG_TAPS = False
_DEBUG_FFN = False
_FFN_BARRIERS = (0, 0, 0, 0, 0)
_FFN_PARANOID = (0, 0, 0)
_DEBUG_ATT = False

_FCONSTS = [1e-5, -0.5, 0.03125, 1.0, 0.0, 0.5, -1.0]


def _host_prep(inp):
    d = {}
    f32 = np.float32

    def c(x):
        return np.ascontiguousarray(x, dtype=f32)

    d["W_inT"] = c(inp["L0_attn_a_W_in"].T)
    for sfx, tag in (("", "f"), ("_b", "b")):
        cw = np.asarray(inp[f"L0_attn_a_conv_w{sfx}"], f32)[:, 0, :]
        d[f"convw_{tag}"] = c(cw.reshape(NT_DI, 128, 4).transpose(1, 0, 2).reshape(128, NT_DI * 4))
        d[f"convb_{tag}"] = c(np.asarray(inp[f"L0_attn_a_conv_b{sfx}"], f32).reshape(NT_DI, 128).T)
        wx = np.asarray(inp[f"L0_attn_a_Wx{sfx}"], f32)
        wxp = np.zeros((DI, 128), f32)
        wxp[:, 0:16] = wx[DTR:DTR + 16].T
        wxp[:, 16:32] = wx[DTR + 16:DTR + 32].T
        wxp[:, 64:128] = wx[0:DTR].T
        d[f"WxTp_{tag}"] = c(wxp)
        d[f"WdtT_{tag}"] = c(np.asarray(inp[f"L0_attn_a_Wdt{sfx}"], f32).T)
        d[f"bdt_{tag}"] = c(np.asarray(inp[f"L0_attn_a_bdt{sfx}"], f32).reshape(NT_DI, 128).T)
        al = np.asarray(inp["L0_attn_a_A_log" if tag == "f" else "L0_attn_a_A_b_log"], f32)
        d[f"Alog_{tag}"] = c(al.reshape(NT_DI, 128, NST).transpose(1, 0, 2).reshape(128, NT_DI * NST))
        dd = np.asarray(inp["L0_attn_a_D" if tag == "f" else "L0_attn_a_D_b"], f32)
        d[f"D_{tag}"] = c(dd.reshape(NT_DI, 128).T)
    d["W_outT"] = c(inp["L0_attn_a_W_out"].T)

    d["W_q"] = c(inp["L0_attn_b_W_q"])
    d["W_k"] = c(inp["L0_attn_b_W_k"])
    d["W_v"] = c(inp["L0_attn_b_W_v"])
    d["bq"] = c(np.asarray(inp["L0_attn_b_b_q"], f32).reshape(NT_DM, 128).T)
    d["bk"] = c(np.asarray(inp["L0_attn_b_b_k"], f32).reshape(NT_DM, 128).T)
    d["bv_r"] = c(np.asarray(inp["L0_attn_b_b_v"], f32).reshape(1, DM))

    d["gw1T"] = c(inp["L0_ffn_gate_w1"].T)
    d["gb1"] = c(np.asarray(inp["L0_ffn_gate_b1"], f32).reshape(NT_DM, 128).T)
    d["gw2T"] = c(inp["L0_ffn_gate_w2"].T)
    d["gb2"] = c(np.asarray(inp["L0_ffn_gate_b2"], f32).reshape(NE, 1))

    W1 = np.asarray(inp["L0_ffn_W1"], f32)
    TW1 = np.asarray(inp["L0_ffn_TW1"], f32)
    t1 = TW1.transpose(2, 0, 1).reshape(16, 64, NE, FF)
    d["TW1_strips"] = c(t1.transpose(0, 2, 1, 3).reshape(16, NE * 64, FF))
    d["W1_strips"] = c(W1.T.reshape(16, 64, FF))
    W2 = np.asarray(inp["L0_ffn_W2"], f32)
    TW2 = np.asarray(inp["L0_ffn_TW2"], f32)
    t2 = TW2.transpose(2, 0, 1).reshape(64, 64, NE, DM)
    d["TW2_strips"] = c(t2.transpose(0, 2, 1, 3).reshape(64, NE * 64, DM))
    d["W2_strips"] = c(W2.T.reshape(64, 64, DM))
    d["Tb1_aug"] = c(np.concatenate([np.asarray(inp["L0_ffn_b1"], f32)[None],
                                     np.asarray(inp["L0_ffn_Tb1"], f32)], 0))
    d["Tb2_aug"] = c(np.concatenate([np.asarray(inp["L0_ffn_b2"], f32)[None],
                                     np.asarray(inp["L0_ffn_Tb2"], f32)], 0))

    d["I128"] = np.eye(128, dtype=f32)
    ind32 = np.zeros((32, 32 * 128), f32)
    for j in range(32):
        ind32[j, j * 128:(j + 1) * 128] = 1.0
    d["ind32"] = ind32
    d["I64t2"] = c(np.concatenate([np.eye(64), np.eye(64)], 0))
    d["I64"] = np.eye(64, dtype=f32)
    indp = np.zeros((9, 4 * 128), f32)
    for j in range(4):
        indp[1 + 2 * j, j * 128:j * 128 + 64] = 1.0
        indp[2 + 2 * j, j * 128 + 64:(j + 1) * 128] = 1.0
    d["indpair"] = indp
    d["onesLN"] = np.full((128, 128), 1.0 / DM, f32)
    d["ones1"] = np.ones((1, 128), f32)
    d["fconsts"] = np.tile(np.array(_FCONSTS, f32)[None, :], (128, 1))

    x = np.asarray(inp["L0_attn_a_hidden"], f32)
    xs = [np.ascontiguousarray(x[b].T) for b in range(NCORE)]
    return d, xs


class P:
    def __init__(self, nc, es):
        self.nc = nc
        self.es = es
        self.blk = es.enter_context(nc.Block())
        self.sems = {}
        self.cnt = {}
        for name in ("sq", "gq", "pe", "dv", "ac", "po"):
            self.sems[name] = es.enter_context(nc.semaphore(name))
            self.cnt[name] = 0
        self.t = {}

    def mark(self, instr, s, k=1):
        instr.then_inc(self.sems[s], k)
        self.cnt[s] += k
        return self.cnt[s]

    def v(self, s):
        return self.cnt[s]

    def w(self, eng, s, val):
        if val and val > 0:
            eng.wait_ge(self.sems[s], val)

    def barrier(self):
        tgt = {s: self.v(s) for s in self.cnt}
        for ename in ("tensor", "vector", "scalar", "gpsimd", "sync"):
            @getattr(self.blk, ename)
            def _(e, tgt=tgt):
                for s, v in tgt.items():
                    if v > 0:
                        e.wait_ge(self.sems[s], v)


_UID = [0]


def _sb(p, es, name, shape, dtype=F32):
    _UID[0] += 1
    return es.enter_context(p.nc.sbuf_tensor(f"{name}_{_UID[0]}", shape, dtype))


def _ps(p, es, name, shape, dtype=F32):
    _UID[0] += 1
    return es.enter_context(p.nc.psum_tensor(f"{name}_{_UID[0]}", shape, dtype))


def _build_program():
    nc = bass.Bass("TRN2", target_bir_lowering=False, debug=False)
    dr = {}

    def din(name, shape):
        dr[name] = nc.dram_tensor(name, list(shape), F32, kind="ExternalInput").ap()

    din("x_in", [DM, S])
    din("W_inT", [DM, 2 * DI])
    for t in ("f", "b"):
        din(f"convw_{t}", [128, NT_DI * 4]); din(f"convb_{t}", [128, NT_DI])
        din(f"WxTp_{t}", [DI, 128]); din(f"WdtT_{t}", [DTR, DI])
        din(f"bdt_{t}", [128, NT_DI]); din(f"Alog_{t}", [128, NT_DI * NST])
        din(f"D_{t}", [128, NT_DI])
    din("W_outT", [DI, DM])
    din("W_q", [DM, DM]); din("W_k", [DM, DM]); din("W_v", [DM, DM])
    din("bq", [128, NT_DM]); din("bk", [128, NT_DM]); din("bv_r", [1, DM])
    din("gw1T", [DM, DM]); din("gb1", [128, NT_DM]); din("gw2T", [DM, NE]); din("gb2", [NE, 1])
    din("TW1_strips", [16, NE * 64, FF]); din("W1_strips", [16, 64, FF])
    din("TW2_strips", [64, NE * 64, DM]); din("W2_strips", [64, 64, DM])
    din("Tb1_aug", [9, FF]); din("Tb2_aug", [9, DM])
    din("I128", [128, 128]); din("ind32", [32, 32 * 128]); din("I64t2", [128, 64])
    din("I64", [64, 64]); din("indpair", [9, 4 * 128])
    din("onesLN", [128, 128]); din("ones1", [1, 128])
    din("fconsts", [128, len(_FCONSTS)])

    dr["out"] = nc.dram_tensor("out", [DM, S], F32, kind="ExternalOutput").ap()

    def scr(name, shape, dtype=F32):
        dr[name] = nc.dram_tensor(name, list(shape), dtype, kind="Internal").ap()

    scr("res_a", [DM, S]); scr("res_b", [DM, S])
    scr("sp_xzx", [DI, S]); scr("sp_sz", [DI, S]); scr("sp_xsil", [DI, S])
    scr("sp_of", [DI, S], BF16); scr("sp_ofb", [DI, S], BF16)
    scr("sp_xln", [DM, S], BF16)
    scr("g_scr", [16]); scr("mb1_scr", [FF]); scr("mb2_scr", [DM])
    if _DEBUG_TAPS:
        for nm in ("tp_dt", "tp_dtx", "tp_B", "tp_C", "tp_dA", "tp_h", "tp_pn", "tp_y0", "tp_yf"):
            dr[nm] = nc.dram_tensor(nm, [128, S], F32, kind="ExternalOutput").ap()
    if _DEBUG_FFN:
        dr["tq_g"] = nc.dram_tensor("tq_g", [16, 1], F32, kind="ExternalOutput").ap()
        dr["tq_gp"] = nc.dram_tensor("tq_gp", [128, 4], F32, kind="ExternalOutput").ap()
        dr["tq_mb1"] = nc.dram_tensor("tq_mb1", [128, NT_F], F32, kind="ExternalOutput").ap()
        dr["tq_mb2"] = nc.dram_tensor("tq_mb2", [128, NT_DM], F32, kind="ExternalOutput").ap()
        dr["tq_w1"] = nc.dram_tensor("tq_w1", [128, FF], BF16, kind="ExternalOutput").ap()
        dr["tq_w2"] = nc.dram_tensor("tq_w2", [128, DM], BF16, kind="ExternalOutput").ap()
        dr["tq_hg"] = nc.dram_tensor("tq_hg", [128, NT_DM * CH], BF16, kind="ExternalOutput").ap()
        dr["tq_u"] = nc.dram_tensor("tq_u", [128, 256], BF16, kind="ExternalOutput").ap()
        dr["tq_w1f"] = nc.dram_tensor("tq_w1f", [DM, FF], BF16, kind="ExternalOutput").ap()
        dr["tq_w2f"] = nc.dram_tensor("tq_w2f", [FF, DM], BF16, kind="ExternalOutput").ap()
    if _DEBUG_ATT:
        dr["ta_nm"] = nc.dram_tensor("ta_nm", [128, 1], F32, kind="ExternalOutput").ap()
        dr["ta_rs"] = nc.dram_tensor("ta_rs", [128, 1], F32, kind="ExternalOutput").ap()
        dr["ta_rsr"] = nc.dram_tensor("ta_rsr", [128, 1], F32, kind="ExternalOutput").ap()
        dr["ta_pbf"] = nc.dram_tensor("ta_pbf", [128, S], BF16, kind="ExternalOutput").ap()
        dr["ta_pt"] = nc.dram_tensor("ta_pt", [128, 128], BF16, kind="ExternalOutput").ap()
        dr["ta_qt"] = nc.dram_tensor("ta_qt", [128, S], BF16, kind="ExternalOutput").ap()
        dr["ta_kt"] = nc.dram_tensor("ta_kt", [128, S], BF16, kind="ExternalOutput").ap()
        dr["ta_vt"] = nc.dram_tensor("ta_vt", [128, DM], BF16, kind="ExternalOutput").ap()

    with ExitStack() as es:
        p = P(nc, es)
        _emit(p, dr, es)
    return nc


def _emit(p, dr, es):
    T = p.t

    T["I128"] = _sb(p, es, "c_I128", [128, 128], BF16)
    T["I64t2"] = _sb(p, es, "c_I64t2", [128, 64], F32)
    T["I64"] = _sb(p, es, "c_I64", [64, 64], BF16)
    T["indp"] = _sb(p, es, "c_indp", [9, 4 * 128], BF16)
    T["onesLN"] = _sb(p, es, "c_onesLN", [128, 128], BF16)
    T["ones1"] = _sb(p, es, "c_ones1", [1, 128], BF16)
    for t in ("f", "b"):
        T[f"convw_{t}"] = _sb(p, es, f"m_convw_{t}", [128, NT_DI * 4])
        T[f"convb_{t}"] = _sb(p, es, f"m_convb_{t}", [128, NT_DI])
        T[f"bdt_{t}"] = _sb(p, es, f"m_bdt_{t}", [128, NT_DI])
        T[f"Aneg_{t}"] = _sb(p, es, f"m_Aneg_{t}", [128, NT_DI * NST])
        T[f"D_{t}"] = _sb(p, es, f"m_D_{t}", [128, NT_DI])
    T["fconsts"] = _sb(p, es, "c_fconsts", [128, len(_FCONSTS)])
    for j, val in enumerate(_FCONSTS):
        p.nc.const_aps.register(F32, float(val), T["fconsts"][:, j:j + 1])
    T["gb2"] = _sb(p, es, "a_gb2", [NE, 1])
    T["bq"] = _sb(p, es, "a_bq", [128, NT_DM])
    T["bk"] = _sb(p, es, "a_bk", [128, NT_DM])
    T["gb1"] = _sb(p, es, "a_gb1", [128, NT_DM])

    @p.blk.sync
    def _(sync):
        for name in ("fconsts", "convw_f", "convw_b", "convb_f", "convb_b", "bdt_f", "bdt_b",
                     "D_f", "D_b", "gb2", "bq", "bk", "gb1"):
            p.mark(sync.dma_start(out=T[name][:], in_=dr[name][:]), "sq", 16)
        p.mark(sync.dma_start(out=T["Aneg_f"][:], in_=dr["Alog_f"][:]), "sq", 16)
        p.mark(sync.dma_start(out=T["Aneg_b"][:], in_=dr["Alog_b"][:]), "sq", 16)

    @p.blk.gpsimd
    def _(g):
        for name, src in (("I128", "I128"), ("I64", "I64"),
                          ("indp", "indpair"), ("onesLN", "onesLN"), ("ones1", "ones1")):
            p.mark(g.dma_start(out=T[name][:], in_=dr[src][:]), "gq", 16)
        p.mark(g.dma_start(out=T["I64t2"][:], in_=dr["I64t2"][:]), "gq", 16)

    ld_small = p.v("sq")

    @p.blk.scalar
    def _(sc):
        p.w(sc, "sq", ld_small)
        sc.activation(T["Aneg_f"][:], T["Aneg_f"][:], AF.Exp)
        p.mark(sc.activation(T["Aneg_b"][:], T["Aneg_b"][:], AF.Exp), "ac")

    @p.blk.vector
    def _(ve):
        p.w(ve, "ac", p.v("ac"))
        ve.tensor_scalar(T["Aneg_f"][:], T["Aneg_f"][:], -1.0, None, OP.mult)
        p.mark(ve.tensor_scalar(T["Aneg_b"][:], T["Aneg_b"][:], -1.0, None, OP.mult), "dv")

    res = {0: dr["x_in"], 1: dr["res_a"], 2: dr["res_b"], 3: dr["res_a"]}

    def dst(k):
        return dr["out"] if k == _N_SUB else res[k]

    _ln(p, dr, res[0])
    _mamba(p, dr, res[0], dst(1))
    if _N_SUB >= 2:
        _ln(p, dr, res[1])
        _ffn(p, dr, res[1], dst(2))
    if _N_SUB >= 3:
        _ln(p, dr, res[2])
        _attn(p, dr, res[2], dst(3))
    if _N_SUB >= 4:
        _ln(p, dr, res[3])
        _ffn(p, dr, res[3], dr["out"])

    @p.blk.sync
    def _(sync):
        sync.wait_ge(p.sems["sq"], p.v("sq"))
        sync.wait_ge(p.sems["gq"], p.v("gq"))


def _ln(p, dr, res_src):
    p.barrier()
    with ExitStack() as es:
        x_f = [_sb(p, es, f"ln_x{k}", [128, S]) for k in range(NT_DM)]
        x_bf = [_sb(p, es, f"ln_xb{k}", [128, S], BF16) for k in range(NT_DM)]
        sq_bf = [_sb(p, es, f"ln_sq{k}", [128, S], BF16) for k in range(NT_DM)]
        mm_sq = _sb(p, es, "ln_mmsq", [128, S])
        rstd = _sb(p, es, "ln_rstd", [128, S])
        mr = _sb(p, es, "ln_mr", [128, S])
        t1 = _sb(p, es, "ln_t1", [128, S])
        xo = [_sb(p, es, f"ln_xo{i}", [128, S], BF16) for i in range(2)]
        p_m = _ps(p, es, "ln_pm", [128, S])
        p_s = _ps(p, es, "ln_ps", [128, S])

        @p.blk.sync
        def _(sync):
            for k in range(NT_DM):
                p.mark(sync.dma_start(out=x_f[k][:], in_=res_src[k * 128:(k + 1) * 128, :]), "sq", 16)
        ld = p.v("sq")

        @p.blk.vector
        def _(ve):
            p.w(ve, "sq", ld)
            for k in range(NT_DM):
                i = ve.tensor_copy(x_bf[k][:], x_f[k][:])
            p.mark(i, "dv")
        v_cast = p.v("dv")

        @p.blk.scalar
        def _(sc):
            p.w(sc, "sq", ld)
            for k in range(NT_DM):
                i = sc.activation(sq_bf[k][:], x_f[k][:], AF.Square)
            p.mark(i, "ac")
        a_sq = p.v("ac")

        @p.blk.tensor
        def _(te):
            p.w(te, "dv", v_cast)
            p.w(te, "ac", a_sq)
            ones = p.t["onesLN"]
            for ch in range(NCH):
                for k in range(NT_DM):
                    te.matmul(p_m[:, ch * CH:(ch + 1) * CH], ones[:], x_bf[k][:, ch * CH:(ch + 1) * CH],
                              start=(k == 0), stop=(k == NT_DM - 1))
            for ch in range(NCH):
                for k in range(NT_DM):
                    i = te.matmul(p_s[:, ch * CH:(ch + 1) * CH], ones[:], sq_bf[k][:, ch * CH:(ch + 1) * CH],
                                  start=(k == 0), stop=(k == NT_DM - 1))
            p.mark(i, "pe")
        pe_st = p.v("pe")

        @p.blk.scalar
        def _(sc):
            p.w(sc, "pe", pe_st)
            p.mark(sc.activation(mm_sq[:], p_m[:], AF.Square), "ac")
        a_mm = p.v("ac")

        @p.blk.vector
        def _(ve):
            p.w(ve, "ac", a_mm)
            p.mark(ve.scalar_tensor_tensor(t1[:], p_s[:], 1.0, mm_sq[:], OP.mult, OP.subtract), "dv")
        v_var = p.v("dv")

        @p.blk.scalar
        def _(sc):
            p.w(sc, "dv", v_var)
            v = p.mark(sc.activation(t1[:], t1[:], AF.Ln, bias=1e-5), "ac")
            sc.wait_ge(p.sems["ac"], v)
            p.mark(sc.activation(rstd[:], t1[:], AF.Exp, scale=-0.5), "ac")
        a_rs = p.v("ac")

        dv0 = p.v("dv")
        sq0 = p.v("sq")

        @p.blk.vector
        def _(ve):
            p.w(ve, "ac", a_rs)
            ve.tensor_tensor(mr[:], p_m[:], rstd[:], OP.mult)
            for k in range(NT_DM):
                if k >= 2:
                    p.w(ve, "sq", sq0 + 16 * (k - 1))
                ve.tensor_tensor(t1[:], x_f[k][:], rstd[:], OP.mult)
                p.mark(ve.tensor_tensor(xo[k % 2][:], t1[:], mr[:], OP.subtract), "dv")

        @p.blk.sync
        def _(sync):
            for k in range(NT_DM):
                p.w(sync, "dv", dv0 + k + 1)
                p.mark(sync.dma_start(out=dr["sp_xln"][k * 128:(k + 1) * 128, :], in_=xo[k % 2][:]), "sq", 16)
    p.barrier()


def _mamba(p, dr, res_src, res_dst):
    with ExitStack() as mes:
        for tg in ("f", "b"):
            p.t[f"BCb_{tg}"] = _sb(p, mes, f"BCb_{tg}", [32, S], BF16)
            p.t[f"dtr_{tg}"] = _sb(p, mes, f"dtr_{tg}", [DTR, S], BF16)
        _mamba_stageA(p, dr, "f", rev=False)
        _mamba_stageB(p, dr, "f", rev=False)
        _mamba_stageA(p, dr, "b", rev=True)
        _mamba_stageB(p, dr, "b", rev=True)
        _mamba_outproj(p, dr, res_src, res_dst)
    for tg in ("f", "b"):
        del p.t[f"BCb_{tg}"], p.t[f"dtr_{tg}"]


def _mamba_stageA(p, dr, tag, rev):
    p.barrier()
    if f"BCb_{tag}" not in p.t:
        p.t[f"BCb_{tag}"] = _sb(p, p.es, f"BCb_{tag}", [32, S], BF16)
        p.t[f"dtr_{tag}"] = _sb(p, p.es, f"dtr_{tag}", [DTR, S], BF16)
    with ExitStack() as es:
        xzx = _sb(p, es, "ma_xzx", [128, NT_DI, CH + 3])
        stash = _sb(p, es, "ma_stash", [128, NT_DI, 3])
        sz_a = _sb(p, es, "ma_sz", [128, NT_DI, CH])
        xsil_bf = _sb(p, es, "ma_xsb", [128, NT_DI, CH], BF16)
        xdbl = _sb(p, es, "ma_xdbl", [128, S])
        cvt = [_sb(p, es, f"ma_cvt{i}", [128, CH]) for i in range(2)]
        wxp = _sb(p, es, "ma_wxp", [128, NT_DI, 128], BF16)

        @p.blk.gpsimd
        def _(g):
            p.mark(g.dma_start(out=wxp[:], in_=dr[f"WxTp_{tag}"].rearrange("(t p) c -> p t c", p=128)), "gq", 16)
        ld_wxp = p.v("gq")
        ps_in = [_ps(p, es, f"ma_pin{i}", [128, CH]) for i in range(2)]
        ps_xd = _ps(p, es, "ma_pxd", [128, CH])
        if not rev:
            w_in = [_sb(p, es, f"ma_win{k}", [128, 2 * DI], BF16) for k in range(NT_DM)]
            xln_ch = [_sb(p, es, f"ma_xln{i}", [128, NT_DM, CH], BF16) for i in range(2)]

            @p.blk.gpsimd
            def _(g):
                for k in range(NT_DM):
                    p.mark(g.dma_start(out=w_in[k][:], in_=dr["W_inT"][k * 128:(k + 1) * 128, :]), "gq", 16)
            ld_w = p.v("gq")
        else:
            ld_x = [_sb(p, es, f"ma_ldx{i}", [128, NT_DI, CH]) for i in range(2)]

        convw, convb = p.t[f"convw_{tag}"], p.t[f"convb_{tag}"]
        sp_xln_r = dr["sp_xln"].rearrange("(k p) s -> p k s", p=128)
        sp_xzx_r = dr["sp_xzx"].rearrange("(t p) s -> p t s", p=128)
        sp_sz_r = dr["sp_sz"].rearrange("(t p) s -> p t s", p=128)
        sp_xsil_r = dr["sp_xsil"].rearrange("(t p) s -> p t s", p=128)

        fillm = {}
        fill_s = "ac" if not rev else "dv"
        silm = {}
        cvm = {}
        xsbm = {}
        inpe = {}
        spx = {}
        szsp = {}
        sps = {}
        xdm = {}
        pech = {}
        stm = {}

        for ch in range(NCH):
            if not rev:
                @p.blk.sync
                def _(sync, ch=ch):
                    if ch >= 2:
                        p.w(sync, "pe", pech[ch - 2])
                    p.mark(sync.dma_start(out=xln_ch[ch % 2][:],
                                          in_=sp_xln_r[:, :, ch * CH:(ch + 1) * CH]), "sq", 16)
                ld_ch = p.v("sq")

                ac0 = p.v("ac")

                @p.blk.tensor
                def _(te, ch=ch, ld_ch=ld_ch, ac0=ac0):
                    p.w(te, "sq", ld_ch)
                    if ch == 0:
                        p.w(te, "gq", ld_w)
                    for dz in range(2 * NT_DI):
                        ig = ch * 2 * NT_DI + dz
                        if dz < 2:
                            if ig >= 2:
                                p.w(te, "ac", fillm[("ig", ig - 2)])
                        else:
                            p.w(te, "ac", ac0 + dz - 1)
                        for kk in range(NT_DM):
                            i = te.matmul(ps_in[dz % 2][:, :], w_in[kk][:, dz * 128:(dz + 1) * 128],
                                          xln_ch[ch % 2][:, kk, :], start=(kk == 0), stop=(kk == NT_DM - 1))
                        inpe[ig] = p.mark(i, "pe")
                    pech[ch] = p.v("pe")

                @p.blk.scalar
                def _(sc, ch=ch):
                    if ch > 0:
                        p.w(sc, "gq", sps[ch - 1])
                        p.w(sc, "dv", stm[ch - 1])
                    for dz in range(2 * NT_DI):
                        ig = ch * 2 * NT_DI + dz
                        p.w(sc, "pe", inpe[ig])
                        if dz < NT_DI:
                            i = sc.copy(xzx[:, dz, 3:CH + 3], ps_in[dz % 2][:, :])
                        else:
                            i = sc.activation(sz_a[:, dz - NT_DI, :], ps_in[dz % 2][:, :], AF.Silu)
                        fillm[("ig", ig)] = p.mark(i, "ac")
                        if dz < NT_DI:
                            fillm[(ch, dz)] = fillm[("ig", ig)]
                fill_done = p.v("ac")
            else:
                o0 = S - (ch + 1) * CH

                @p.blk.sync
                def _(sync, ch=ch, o0=o0):
                    if ch >= 2:
                        p.w(sync, "dv", fillm[(ch - 2, NT_DI - 1)])
                    p.mark(sync.dma_start(out=ld_x[ch % 2][:], in_=sp_xzx_r[:, :, o0:o0 + CH]), "sq", 16)
                ld_ch = p.v("sq")

                @p.blk.vector
                def _(ve, ch=ch, ld_ch=ld_ch):
                    p.w(ve, "sq", ld_ch)
                    if ch > 0:
                        p.w(ve, "ac", silm[ch * NT_DI - 1])
                        p.w(ve, "gq", sps[ch - 1])
                    for t in range(NT_DI):
                        i = ve.tensor_copy(xzx[:, t, 3:CH + 3], ld_x[ch % 2][:, t, ::-1])
                        fillm[(ch, t)] = p.mark(i, "dv")
                fill_done = p.v("dv")

            @p.blk.vector
            def _(ve, ch=ch, fill_done=fill_done):
                if ch == 0:
                    v = p.mark(ve.memset(xzx[:, :, 0:3], 0.0), "dv")
                else:
                    v = p.mark(ve.tensor_copy(xzx[:, :, 0:3], stash[:]), "dv")
                ve.wait_ge(p.sems["dv"], v)
                p.w(ve, fill_s, fill_done)
                i = ve.tensor_copy(stash[:], xzx[:, :, CH:CH + 3])
                stm[ch] = p.mark(i, "dv")

            if not rev:
                @p.blk.gpsimd
                def _(g, ch=ch, fill_done=fill_done):
                    p.w(g, "ac", fill_done)
                    p.mark(g.dma_start(out=sp_xzx_r[:, :, ch * CH:(ch + 1) * CH],
                                       in_=xzx[:, :, 3:CH + 3]), "gq", 16)
                    spx[ch] = p.v("gq")
                    p.mark(g.dma_start(out=sp_sz_r[:, :, ch * CH:(ch + 1) * CH],
                                       in_=sz_a[:, :, :]), "gq", 16)
                    szsp[ch] = p.v("gq")

            for t in range(NT_DI):
                isil = ch * NT_DI + t

                @p.blk.vector
                def _(ve, ch=ch, t=t, isil=isil):
                    if isil >= 2:
                        p.w(ve, "ac", silm[isil - 2])
                    p.w(ve, fill_s, fillm[(ch, t)])
                    p.w(ve, "dv", stm[ch])
                    cv = cvt[t % 2]
                    ve.tensor_scalar(cv[:], xzx[:, t, 0:CH], convw[:, 4 * t:4 * t + 1], None, OP.mult)
                    for kk in range(1, 4):
                        i = ve.scalar_tensor_tensor(cv[:], xzx[:, t, kk:CH + kk], convw[:, 4 * t + kk:4 * t + kk + 1],
                                                    cv[:], OP.mult, OP.add)
                    cvm[isil] = p.mark(i, "dv")

                @p.blk.scalar
                def _(sc, ch=ch, t=t, isil=isil):
                    p.w(sc, "dv", cvm[isil])
                    if not rev:
                        p.w(sc, "gq", spx[ch])
                    i = sc.activation(xzx[:, t, 3:CH + 3], cvt[t % 2][:], AF.Silu, bias=convb[:, t:t + 1])
                    silm[isil] = p.mark(i, "ac")

            @p.blk.vector
            def _(ve, ch=ch):
                for t in range(NT_DI):
                    p.w(ve, "ac", silm[ch * NT_DI + t])
                    i = ve.tensor_copy(xsil_bf[:, t, :], xzx[:, t, 3:CH + 3])
                    xsbm[(ch, t)] = p.mark(i, "dv")

            @p.blk.gpsimd
            def _(g, ch=ch):
                p.w(g, "ac", silm[ch * NT_DI + NT_DI - 1])
                p.mark(g.dma_start(out=sp_xsil_r[:, :, ch * CH:(ch + 1) * CH],
                                   in_=xzx[:, :, 3:CH + 3]), "gq", 16)
                sps[ch] = p.v("gq")

            @p.blk.tensor
            def _(te, ch=ch):
                p.w(te, "dv", xsbm[(ch, NT_DI - 1)])
                p.w(te, "gq", ld_wxp)
                if ch > 0:
                    p.w(te, "dv", xdm[ch - 1])
                wx = wxp
                for t in range(NT_DI):
                    i = te.matmul(ps_xd[:, :], wx[:, t, :], xsil_bf[:, t, :],
                                  start=(t == 0), stop=(t == NT_DI - 1))
                p.mark(i, "pe")
            pe_xd = p.v("pe")

            @p.blk.vector
            def _(ve, ch=ch, pe_xd=pe_xd):
                p.w(ve, "pe", pe_xd)
                i = ve.tensor_copy(xdbl[:, ch * CH:(ch + 1) * CH], ps_xd[:, :])
                xdm[ch] = p.mark(i, "dv")

        @p.blk.vector
        def _(ve):
            p.w(ve, "dv", xdm[NCH - 1])
            ve.tensor_copy(p.t[f"BCb_{tag}"][:, :], xdbl[0:32, :])
            p.mark(ve.tensor_copy(p.t[f"dtr_{tag}"][:, :], xdbl[64:128, :]), "dv")
    p.barrier()


def _mamba_stageB(p, dr, tag, rev):
    HF = S // 2
    NBK = NT_DI // 2
    with ExitStack() as es:
        dt_b = _sb(p, es, "mb_dt", [128, 2, S])
        dtx_b = _sb(p, es, "mb_dtx", [128, 2, S])
        xs_b = _sb(p, es, "mb_xs", [128, 2, S])
        szb = [_sb(p, es, f"mb_sz{i}", [128, 2, HF]) for i in range(2)]
        Bbc = [_sb(p, es, f"mb_Bbc{i}", [128, HF], BF16) for i in range(2)]
        Cbc = [_sb(p, es, f"mb_Cbc{i}", [128, HF], BF16) for i in range(2)]
        dA = [_sb(p, es, f"mb_dA{i}", [128, HF]) for i in range(2)]
        dp = [_sb(p, es, f"mb_dp{i}", [128, HF]) for i in range(2)]
        pnb = [_sb(p, es, f"mb_pn{i}", [128, HF], BF16) for i in range(2)]
        hsc = _sb(p, es, "mb_h", [128, HF], BF16)
        hl = _sb(p, es, "mb_hl", [128, 32])
        sp_full = _sb(p, es, "mb_spf", [128, S])
        t1 = _sb(p, es, "mb_t1", [128, HF])
        of_t = [_sb(p, es, f"mb_of{i}", [128, HF], BF16) for i in range(2)]
        WdtT = _sb(p, es, "mb_wdt", [DTR, DI], BF16)
        ind32 = _sb(p, es, "mb_ind32", [32, 32 * 128], BF16)
        ps_y = [_ps(p, es, f"mb_psy{i}", [128, HF]) for i in range(2)]
        ps_dt = [_ps(p, es, f"mb_psdt{i}", [128, CH]) for i in range(2)]
        ps_bc = [_ps(p, es, f"mb_psbc{i}", [128, CH]) for i in range(2)]

        @p.blk.gpsimd
        def _(g):
            p.mark(g.dma_start(out=WdtT[:], in_=dr[f"WdtT_{tag}"][:]), "gq", 16)
            p.mark(g.dma_start(out=ind32[:], in_=dr["ind32"][:]), "gq", 16)
        ld_wdt = p.v("gq")

        Aneg, Dv = p.t[f"Aneg_{tag}"], p.t[f"D_{tag}"]
        bdt = p.t[f"bdt_{tag}"]
        BCb, dtr = p.t[f"BCb_{tag}"], p.t[f"dtr_{tag}"]
        I128 = p.t["I128"]
        sp_xsil_r = dr["sp_xsil"].rearrange("(t p) s -> p t s", p=128)
        sp_sz_r = dr["sp_sz"].rearrange("(t p) s -> p t s", p=128)
        sp_of_r = dr["sp_of" if not rev else "sp_ofb"].rearrange("(t p) s -> p t s", p=128)

        def bcast_mm(te, st, jj, base):
            row = st if jj < 2 else 16 + st
            c0 = base + (jj % 2) * CH
            return te.matmul(ps_bc[jj % 2][:, :], ind32[:, row * 128:(row + 1) * 128],
                             BCb[:, c0:c0 + CH], start=True, stop=True)

        def bc_evac(sc, st, jj):
            dstt = Bbc if jj < 2 else Cbc
            c0 = (jj % 2) * CH
            return sc.copy(dstt[st % 2][:, c0:c0 + CH], ps_bc[jj % 2][:, :])

        bcev = {}
        xlm = {}
        dtln = {}
        dtxm = {}
        ofm = {}
        ofs = {}
        szm = {}
        scm = {}
        pnm = {}
        dbm = {}
        acm = {}
        damk = {}
        dtq_last = None
        ln_guard = {}

        ac_seed = p.v("ac")

        @p.blk.tensor
        def _(te):
            p.w(te, "gq", ld_wdt)
            for jj in range(4):
                if jj >= 2:
                    p.w(te, "ac", ac_seed + jj - 1)
                p.mark(bcast_mm(te, 0, jj, 0), "pe")
        seed_pe0 = p.v("pe") - 4

        @p.blk.scalar
        def _(sc):
            for jj in range(4):
                p.w(sc, "pe", seed_pe0 + jj + 1)
                p.mark(bc_evac(sc, 0, jj), "ac")
        bcev[0] = p.v("ac")
        assert bcev[0] == ac_seed + 4
        prev_e23 = (ac_seed + 3, ac_seed + 4)

        for blk in range(NBK):
            for t4 in range(2):
                t = blk * 2 + t4

                @p.blk.sync
                def _(sync, blk=blk, t4=t4, t=t):
                    if blk > 0:
                        p.w(sync, "dv", ofm[(blk - 1, 1, t4)])
                    p.mark(sync.dma_start(out=xs_b[:, t4, :], in_=sp_xsil_r[:, t, :]), "sq", 16)
                    xlm[(blk, t4)] = p.v("sq")

                pe0_dt = p.v("pe")
                ac0_dt = p.v("ac")

                @p.blk.tensor
                def _(te, blk=blk, t4=t4, t=t, ac0_dt=ac0_dt, dtq_last=dtq_last):
                    for q in range(NCH):
                        if q >= 2:
                            p.w(te, "ac", ac0_dt + q - 1)
                        elif dtq_last is not None:
                            p.w(te, "ac", dtq_last[q])
                        p.mark(te.matmul(ps_dt[q % 2][:, :], WdtT[:, t * 128:(t + 1) * 128],
                                         dtr[:, q * CH:(q + 1) * CH], start=True, stop=True), "pe")

                @p.blk.scalar
                def _(sc, blk=blk, t4=t4, t=t, pe0_dt=pe0_dt):
                    if blk > 0:
                        p.w(sc, "ac", ln_guard[t4])
                    for q in range(NCH):
                        p.w(sc, "pe", pe0_dt + q + 1)
                        i = sc.activation(sp_full[:, q * CH:(q + 1) * CH], ps_dt[q % 2][:, :],
                                          AF.Exp, bias=bdt[:, t:t + 1])
                        p.mark(i, "ac")
                    i = sc.activation(dt_b[:, t4, :], sp_full[:], AF.Ln, bias=1.0)
                    dtln[(blk, t4)] = p.mark(i, "ac")
                assert p.v("ac") == ac0_dt + NCH + 1
                dtq_last = (ac0_dt + 3, ac0_dt + 4)

                @p.blk.vector
                def _(ve, blk=blk, t4=t4):
                    p.w(ve, "ac", dtln[(blk, t4)])
                    p.w(ve, "sq", xlm[(blk, t4)])
                    if blk > 0:
                        p.w(ve, "dv", dbm[(NST - 1, t4)])
                    i = ve.tensor_tensor(dtx_b[:, t4, :], dt_b[:, t4, :], xs_b[:, t4, :], OP.mult)
                    dtxm[(blk, t4)] = p.mark(i, "dv")

            for h in range(2):
                base = h * HF
                fwd_off = base if not rev else S - base - HF

                @p.blk.sync
                def _(sync, blk=blk, h=h, fwd_off=fwd_off):
                    if blk > 0:
                        p.w(sync, "dv", ofm[(blk - 1, h, 1)])
                    for t4 in range(2):
                        t = blk * 2 + t4
                        p.mark(sync.dma_start(out=szb[h][:, t4, :],
                                              in_=sp_sz_r[:, t, fwd_off:fwd_off + HF]), "sq", 16)
                    szm[(blk, h)] = p.v("sq")

                for n in range(NST):
                    nn = (n + 1) % NST
                    last_gi = (blk == NBK - 1 and h == 1 and n == NST - 1)
                    if nn == 0:
                        if h == 0:
                            nbase = HF
                        else:
                            nbase = 0
                    else:
                        nbase = base

                    ac0 = p.v("ac")
                    dv0 = p.v("dv")

                    if not last_gi:
                        @p.blk.tensor
                        def _(te, nn=nn, nbase=nbase, ac0=ac0, prev_e23=prev_e23):
                            for jj in range(4):
                                if jj >= 2:
                                    p.w(te, "ac", ac0 + jj + 1)
                                else:
                                    p.w(te, "ac", prev_e23[jj])
                                p.mark(bcast_mm(te, nn, jj, nbase), "pe")
                    bc_pe0 = p.v("pe") - (0 if last_gi else 4)

                    @p.blk.scalar
                    def _(sc, blk=blk, h=h, n=n, nn=nn, last_gi=last_gi, bc_pe0=bc_pe0):
                        for t4 in range(2):
                            t = blk * 2 + t4
                            p.w(sc, "ac", dtln[(blk, t4)])
                            pk = (n - 1, t4) if n > 0 else (NST - 1, t4)
                            if pk in scm:
                                p.w(sc, "dv", scm[pk])
                            i = sc.activation(dA[t4][:], dt_b[:, t4, base:base + HF], AF.Exp,
                                              scale=Aneg[:, t * NST + n:t * NST + n + 1])
                            damk[(n, t4)] = p.mark(i, "ac")
                        if not last_gi:
                            pk = (n - 1, 1) if n > 0 else (NST - 1, 1)
                            if pk in pnm:
                                p.w(sc, "dv", pnm[pk])
                                p.w(sc, "dv", dbm[pk])
                            for jj in range(4):
                                p.w(sc, "pe", bc_pe0 + jj + 1)
                                p.mark(bc_evac(sc, nn, jj), "ac")
                    if not last_gi:
                        bcev[nn] = ac0 + 6
                        prev_e23 = (ac0 + 5, ac0 + 6)
                    assert p.v("ac") == ac0 + (2 if last_gi else 6)

                    @p.blk.vector
                    def _(ve, blk=blk, h=h, n=n, dv0=dv0):
                        nb = n % 2
                        for t4 in range(2):
                            p.w(ve, "ac", bcev[n])
                            p.w(ve, "ac", damk[(n, t4)])
                            pk = (n - 1, t4) if n > 0 else (NST - 1, t4)
                            if pk in acm:
                                p.w(ve, "pe", acm[pk])
                            i = ve.tensor_tensor(dp[t4][:], dtx_b[:, t4, base:base + HF],
                                                 Bbc[nb][:], OP.mult)
                            dbm[(n, t4)] = p.mark(i, "dv")
                            col = n * 2 + t4
                            init = 0.0 if h == 0 else hl[:, col:col + 1]
                            i = ve.tensor_tensor_scan(hsc[:], dA[t4][:], dp[t4][:], init,
                                                      OP.mult, OP.add)
                            scm[(n, t4)] = p.mark(i, "dv")
                            if h == 0:
                                p.mark(ve.tensor_copy(hl[:, col:col + 1], hsc[:, HF - 1:HF]), "dv")
                            i = ve.tensor_tensor(pnb[t4][:], hsc[:], Cbc[nb][:], OP.mult)
                            pnm[(n, t4)] = p.mark(i, "dv")

                    @p.blk.tensor
                    def _(te, blk=blk, h=h, n=n):
                        for t4 in range(2):
                            p.w(te, "dv", pnm[(n, t4)])
                            if n == 0 and (blk > 0 or h > 0):
                                ph = (blk, h - 1, t4) if h > 0 else (blk - 1, 1, t4)
                                p.w(te, "dv", ofm[ph])
                            for cq in range(2):
                                i = te.matmul(ps_y[t4][:, cq * CH:(cq + 1) * CH], I128[:, :],
                                              pnb[t4][:, cq * CH:(cq + 1) * CH],
                                              start=(n == 0), stop=(n == NST - 1))
                            acm[(n, t4)] = p.mark(i, "pe")

                for t4 in range(2):
                    t = blk * 2 + t4

                    @p.blk.vector
                    def _(ve, blk=blk, h=h, t4=t4, t=t, base=base):
                        p.w(ve, "pe", acm[(NST - 1, t4)])
                        p.w(ve, "sq", szm[(blk, h)])
                        pk = (blk, h - 1, t4) if h > 0 else \
                             ((blk - 1, 1, t4) if blk > 0 else None)
                        if pk is not None:
                            p.w(ve, "gq", ofs[pk])
                        i = ve.scalar_tensor_tensor(t1[:], xs_b[:, t4, base:base + HF],
                                                    Dv[:, t:t + 1], ps_y[t4][:, :],
                                                    OP.mult, OP.add)
                        v = p.mark(i, "dv")
                        ve.wait_ge(p.sems["dv"], v)
                        if not rev:
                            i = ve.tensor_tensor(of_t[t4][:], t1[:], szb[h][:, t4, :], OP.mult)
                        else:
                            i = ve.tensor_tensor(of_t[t4][:], t1[:, ::-1], szb[h][:, t4, :], OP.mult)
                        ofm[(blk, h, t4)] = p.mark(i, "dv")

                    @p.blk.gpsimd
                    def _(gg, blk=blk, h=h, t4=t4, t=t, fwd_off=fwd_off):
                        p.w(gg, "dv", ofm[(blk, h, t4)])
                        p.mark(gg.dma_start(out=sp_of_r[:, t, fwd_off:fwd_off + HF],
                                            in_=of_t[t4][:]), "gq", 16)
                        ofs[(blk, h, t4)] = p.v("gq")
            for t4 in range(2):
                ln_guard[t4] = damk[(NST - 1, t4)]
    p.barrier()


def _mamba_outproj(p, dr, res_src, res_dst):
    with ExitStack() as es:
        wo_f = [_sb(p, es, f"mo_wf{i}", [128, DM]) for i in range(2)]
        wo_bf = [_sb(p, es, f"mo_wb{k}", [128, DM], BF16) for k in range(NT_DI)]
        of_f = [_sb(p, es, f"mo_off{i}", [128, NT_DI, CH], BF16) for i in range(2)]
        of_b = [_sb(p, es, f"mo_ofb{i}", [128, NT_DI, CH], BF16) for i in range(2)]
        res_t = [_sb(p, es, f"mo_res{k}", [128, S]) for k in range(NT_DM)]
        ps_o = [_ps(p, es, f"mo_ps{i}", [128, CH]) for i in range(2)]

        sp_of_r = dr["sp_of"].rearrange("(t p) s -> p t s", p=128)
        sp_ofb_r = dr["sp_ofb"].rearrange("(t p) s -> p t s", p=128)

        dv0 = p.v("dv")
        sq0 = p.v("sq")

        @p.blk.sync
        def _(sync):
            for k in range(NT_DI):
                if k >= 2:
                    p.w(sync, "dv", dv0 + k - 1)
                p.mark(sync.dma_start(out=wo_f[k % 2][:], in_=dr["W_outT"][k * 128:(k + 1) * 128, :]), "sq", 16)

        @p.blk.vector
        def _(ve):
            for k in range(NT_DI):
                p.w(ve, "sq", sq0 + 16 * (k + 1))
                p.mark(ve.tensor_scalar(wo_bf[k][:], wo_f[k % 2][:], 0.5, None, OP.mult), "dv")
        v_w = p.v("dv")

        pe0 = p.v("pe")
        dvA0 = p.v("dv")
        sq_r0 = p.v("sq")

        @p.blk.sync
        def _(sync):
            for k in range(NT_DM):
                p.mark(sync.dma_start(out=res_t[k][:], in_=res_src[k * 128:(k + 1) * 128, :]), "sq", 16)
            for chn in range(NCH):
                if chn >= 2:
                    p.w(sync, "pe", pe0 + (chn - 2) * NT_DM + NT_DM)
                p.mark(sync.dma_start(out=of_f[chn % 2][:], in_=sp_of_r[:, :, chn * CH:(chn + 1) * CH]), "sq", 16)
                p.mark(sync.dma_start(out=of_b[chn % 2][:], in_=sp_ofb_r[:, :, chn * CH:(chn + 1) * CH]), "sq", 16)

        @p.blk.tensor
        def _(te):
            p.w(te, "dv", v_w)
            for chn in range(NCH):
                p.w(te, "sq", sq_r0 + 16 * NT_DM + 32 * (chn + 1))
                for m in range(NT_DM):
                    fl = chn * NT_DM + m
                    if fl >= 2:
                        p.w(te, "dv", dvA0 + fl - 1)
                    for k in range(NT_DI):
                        te.matmul(ps_o[fl % 2][:, :], wo_bf[k][:, m * 128:(m + 1) * 128],
                                  of_f[chn % 2][:, k, :], start=(k == 0), stop=False)
                    for k in range(NT_DI):
                        i = te.matmul(ps_o[fl % 2][:, :], wo_bf[k][:, m * 128:(m + 1) * 128],
                                      of_b[chn % 2][:, k, :], start=False, stop=(k == NT_DI - 1))
                    p.mark(i, "pe")

        @p.blk.vector
        def _(ve):
            p.w(ve, "sq", sq_r0 + 16 * NT_DM)
            for chn in range(NCH):
                for m in range(NT_DM):
                    fl = chn * NT_DM + m
                    p.w(ve, "pe", pe0 + fl + 1)
                    i = ve.tensor_tensor(res_t[m][:, chn * CH:(chn + 1) * CH],
                                         res_t[m][:, chn * CH:(chn + 1) * CH], ps_o[fl % 2][:, :], OP.add)
                    p.mark(i, "dv")

        @p.blk.sync
        def _(sync):
            for k in range(NT_DM):
                p.w(sync, "dv", dvA0 + (NCH - 1) * NT_DM + k + 1)
                p.mark(sync.dma_start(out=res_dst[k * 128:(k + 1) * 128, :], in_=res_t[k][:]), "sq", 16)
    p.barrier()


def _ffn(p, dr, res_src, res_dst):
    p.barrier()
    with ExitStack() as es:
        mW1T = _sb(p, es, "ff_w1", [128, NT_DM, FF], BF16)
        mW2T = _sb(p, es, "ff_w2", [128, NT_F, DM], BF16)
        xln_ch = _sb(p, es, "ff_xln", [128, NT_DM, CH], BF16)
        g_sb = _sb(p, es, "ff_g", [NE, 1])
        g_aug = _sb(p, es, "ff_gaug", [16, 1])
        g_rf = _sb(p, es, "ff_grf", [16, 512])
        g_hi = _sb(p, es, "ff_ghi", [16, 512], BF16)
        g_lo4 = _sb(p, es, "ff_glo4", [16, 512])
        g_lo = _sb(p, es, "ff_glo", [16, 512], BF16)
        g_pair = _sb(p, es, "ff_gpair", [128, 4])
        lhsp = _sb(p, es, "ff_lhsp", [128, 4, 64], BF16)
        mb1c = _sb(p, es, "ff_mb1c", [128, NT_F])
        mb2c = _sb(p, es, "ff_mb2c", [128, NT_DM])
        sp_xln_r = dr["sp_xln"].rearrange("(k p) s -> p k s", p=128)
        I64t2, I64, indp = p.t["I64t2"], p.t["I64"], p.t["indp"]

        with ExitStack() as ges:
            gw1 = _sb(p, ges, "ff_gw1", [128, NT_DM, DM], BF16)
            gw2 = _sb(p, ges, "ff_gw2", [128, NT_DM, NE], BF16)
            hg_ch = _sb(p, ges, "ff_hg", [128, NT_DM, CH], BF16)
            gsum = _sb(p, ges, "ff_gsum", [NE, 1])
            ps_g1 = [_ps(p, ges, f"ff_psg{i}", [128, CH]) for i in range(2)]
            ps_g2 = _ps(p, ges, "ff_psg2", [NE, S])

            @p.blk.gpsimd
            def _(g):
                p.mark(g.dma_start(out=gw1[:], in_=dr["gw1T"].rearrange("(k p) d -> p k d", p=128)), "gq", 16)
                p.mark(g.dma_start(out=gw2[:], in_=dr["gw2T"].rearrange("(k p) e -> p k e", p=128)), "gq", 16)
            ld_gw = p.v("gq")

            g1pe = {}
            g1ac = {}
            g2pe = {}
            for ch in range(NCH):
                @p.blk.sync
                def _(sync, ch=ch):
                    if ch > 0:
                        p.w(sync, "pe", g2pe[ch - 1])
                    p.mark(sync.dma_start(out=xln_ch[:], in_=sp_xln_r[:, :, ch * CH:(ch + 1) * CH]), "sq", 16)
                ldx = p.v("sq")

                ac0_ch = p.v("ac")
                for o in range(NT_DM):
                    @p.blk.tensor
                    def _(te, ch=ch, o=o, ldx=ldx, ac0_ch=ac0_ch):
                        if o == 0:
                            p.w(te, "sq", ldx)
                            if ch == 0:
                                p.w(te, "gq", ld_gw)
                        if o >= 2:
                            p.w(te, "ac", ac0_ch + o - 1)
                        elif ch > 0:
                            p.w(te, "ac", g1ac[(ch - 1, NT_DM - 2 + o)])
                        for k in range(NT_DM):
                            i = te.matmul(ps_g1[o % 2][:, :], gw1[:, k, o * 128:(o + 1) * 128],
                                          xln_ch[:, k, :], start=(k == 0), stop=(k == NT_DM - 1))
                        g1pe[(ch, o)] = p.mark(i, "pe")

                @p.blk.scalar
                def _(sc, ch=ch):
                    for o in range(NT_DM):
                        p.w(sc, "pe", g1pe[(ch, o)])
                        i = sc.activation(hg_ch[:, o, :], ps_g1[o % 2][:, :],
                                          AF.Relu, bias=p.t["gb1"][:, o:o + 1])
                        g1ac[(ch, o)] = p.mark(i, "ac")
                assert p.v("ac") == ac0_ch + NT_DM

                if _DEBUG_FFN and ch == 0:
                    @p.blk.gpsimd
                    def _(g):
                        p.w(g, "ac", g1ac[(0, NT_DM - 1)])
                        p.mark(g.dma_start(out=dr["tq_hg"][:], in_=hg_ch[:].rearrange("p k c -> p (k c)")), "gq", 16)
                    hgtap = p.v("gq")

                    @p.blk.sync
                    def _(sync):
                        p.w(sync, "gq", hgtap)

                @p.blk.tensor
                def _(te, ch=ch):
                    p.w(te, "ac", g1ac[(ch, NT_DM - 1)])
                    for k in range(NT_DM):
                        i = te.matmul(ps_g2[:, ch * CH:(ch + 1) * CH], gw2[:, k, :],
                                      hg_ch[:, k, :], start=(k == 0), stop=(k == NT_DM - 1))
                    g2pe[ch] = p.mark(i, "pe")

            @p.blk.vector
            def _(ve):
                p.w(ve, "pe", g2pe[NCH - 1])
                v = p.mark(ve.tensor_reduce(gsum[:], ps_g2[:], AX.X, OP.add), "dv")
                ve.wait_ge(p.sems["dv"], v)
                p.mark(ve.tensor_scalar(g_sb[:], gsum[:], 1.0 / S, p.t["gb2"][:], OP.mult, OP.add), "dv")
            v_g = p.v("dv")
            gate_pe_end = p.v("pe")

        if _FFN_BARRIERS[0]:
            p.barrier()
        @p.blk.vector
        def _(ve):
            ve.memset(g_rf[:], 1.0)
            p.mark(ve.memset(g_aug[:], 1.0), "dv")
        v_ga0 = p.v("dv")

        @p.blk.gpsimd
        def _(g):
            p.w(g, "dv", v_g)
            p.w(g, "dv", v_ga0)
            p.mark(g.dma_start(out=g_aug[1:9, :], in_=g_sb[0:8, :]), "gq", 16)
        gq_ga = p.v("gq")

        @p.blk.vector
        def _(ve):
            p.w(ve, "gq", gq_ga)
            v = p.mark(ve.tensor_scalar(g_rf[:], g_rf[:], g_aug[:, 0:1], None, OP.mult), "dv")
            ve.wait_ge(p.sems["dv"], v)
            v = p.mark(ve.tensor_copy(g_hi[:], g_rf[:]), "dv")
            ve.wait_ge(p.sems["dv"], v)
            v = p.mark(ve.tensor_tensor(g_lo4[:], g_rf[:], g_hi[:], OP.subtract), "dv")
            ve.wait_ge(p.sems["dv"], v)
            p.mark(ve.tensor_copy(g_lo[:], g_lo4[:]), "dv")
        v_ghl = p.v("dv")

        with ExitStack() as pes:
            mbrow = _sb(p, pes, "ff_mbrow", [1, FF])
            ps_gp = _ps(p, pes, "ff_psgp", [128, 512])
            ps_mb = _ps(p, pes, "ff_psmb", [128, 512])

            gp_pe = {}
            gp_ac = {}
            for j in range(4):
                @p.blk.tensor
                def _(te, j=j):
                    if j == 0:
                        p.w(te, "dv", v_ghl)
                    else:
                        p.w(te, "ac", gp_ac[j - 1])
                    te.matmul(ps_gp[:, :], indp[:, j * 128:(j + 1) * 128], g_hi[0:9, :],
                              start=True, stop=False)
                    i = te.matmul(ps_gp[:, :], indp[:, j * 128:(j + 1) * 128], g_lo[0:9, :],
                                  start=False, stop=True)
                    gp_pe[j] = p.mark(i, "pe")

                @p.blk.scalar
                def _(sc, j=j):
                    p.w(sc, "pe", gp_pe[j])
                    gp_ac[j] = p.mark(sc.copy(g_pair[:, j:j + 1], ps_gp[:, 0:1]), "ac")
            a_gp = p.v("ac")

            @p.blk.vector
            def _(ve):
                p.w(ve, "ac", a_gp)
                for j in range(4):
                    i = ve.tensor_scalar(lhsp[:, j, :], I64t2[:], g_pair[:, j:j + 1], None, OP.mult)
                p.mark(i, "dv")
            v_lh = p.v("dv")

            for nm, W, scrn, dst_t in (("Tb1_aug", FF, "mb1_scr", mb1c),
                                       ("Tb2_aug", DM, "mb2_scr", mb2c)):
                with ExitStack() as bes:
                    tb = _sb(p, bes, f"ff_tb_{nm}", [16, W], BF16)
                    mtt = _sb(p, bes, f"ff_mtt_{nm}", [128, 128], BF16)
                    ps_mt = _ps(p, bes, f"ff_psmt_{nm}", [128, 128], BF16)

                    @p.blk.gpsimd
                    def _(g, nm=nm, tb=tb):
                        p.w(g, "pe", gate_pe_end)
                        p.mark(g.dma_start(out=tb[0:9, :], in_=dr[nm][:]), "gq", 16)
                    ld_tb = p.v("gq")

                    mbpe = {}
                    mbac = {}
                    for c in range(W // 512):
                        @p.blk.tensor
                        def _(te, tb=tb, c=c, ld_tb=ld_tb):
                            if c == 0:
                                p.w(te, "gq", ld_tb)
                                p.w(te, "dv", v_ghl)
                            else:
                                p.w(te, "ac", mbac[c - 1])
                            te.matmul(ps_mb[:, :], g_hi[0:9, 0:128], tb[0:9, c * 512:(c + 1) * 512],
                                      start=True, stop=False)
                            i = te.matmul(ps_mb[:, :], g_lo[0:9, 0:128], tb[0:9, c * 512:(c + 1) * 512],
                                          start=False, stop=True)
                            mbpe[c] = p.mark(i, "pe")

                        @p.blk.scalar
                        def _(sc, c=c):
                            p.w(sc, "pe", mbpe[c])
                            mbac[c] = p.mark(sc.copy(mbrow[:, c * 512:(c + 1) * 512], ps_mb[0:1, :]), "ac")

                    @p.blk.gpsimd
                    def _(g, scrn=scrn, W=W, mtt=mtt):
                        p.w(g, "ac", mbac[W // 512 - 1])
                        v = p.mark(g.dma_start(out=dr[scrn].rearrange("(o w) -> o w", o=1), in_=mbrow[0:1, 0:W]), "gq", 16)
                        g.wait_ge(p.sems["gq"], v)
                        p.mark(g.dma_start(out=mtt[0:W // 128, :], in_=dr[scrn].rearrange("(t f) -> t f", f=128)), "gq", 16)
                    gq_mt = p.v("gq")

                    @p.blk.tensor
                    def _(te, W=W, mtt=mtt, ps_mt=ps_mt, gq_mt=gq_mt):
                        p.w(te, "gq", gq_mt)
                        p.mark(te.transpose(ps_mt[:, :], mtt[:, :], p.t["I128"][:, :]), "pe")
                    pe_mt = p.v("pe")

                    @p.blk.vector
                    def _(ve, dst_t=dst_t, ps_mt=ps_mt, pe_mt=pe_mt, W=W):
                        p.w(ve, "pe", pe_mt)
                        p.mark(ve.tensor_copy(dst_t[:], ps_mt[:, 0:W // 128]), "dv")
        mb_done = p.v("gq")
        if _DEBUG_FFN:
            @p.blk.gpsimd
            def _(g):
                p.w(g, "dv", v_g)
                p.mark(g.dma_start(out=dr["tq_g"][0:8, :], in_=g_sb[:]), "gq", 16)
                p.w(g, "dv", v_lh)
                p.mark(g.dma_start(out=dr["tq_gp"][:], in_=g_pair[:]), "gq", 16)
                p.w(g, "gq", mb_done)
                p.mark(g.dma_start(out=dr["tq_mb1"][:], in_=mb1c[:]), "gq", 16)
                p.mark(g.dma_start(out=dr["tq_mb2"][:], in_=mb2c[:]), "gq", 16)
        pre_build = {"pe": p.v("pe"), "dv": p.v("dv"), "ac": p.v("ac")}

        def build(which, pre):
            nstrip, W = (16, FF) if which == 1 else (64, DM)
            tw_r = dr["TW1_strips" if which == 1 else "TW2_strips"].rearrange(
                "s (j p) f -> s p j f", p=128)
            w_name = "W1_strips" if which == 1 else "W2_strips"
            dst = mW1T if which == 1 else mW2T
            nchk = W // 512
            with ExitStack() as bes:
                twb = _sb(p, bes, f"ff_twb{which}", [128, 5, W], BF16)
                ps_b = _ps(p, bes, f"ff_psb{which}", [64, W])
                cons = {}
                evac = {}
                for s in range(nstrip):
                    if _FFN_PARANOID[1]:
                        p.barrier()

                    @p.blk.gpsimd
                    def _(g, s=s):
                        if s == 0:
                            for sem, v in pre.items():
                                p.w(g, sem, v)
                        for j in range(5):
                            if s >= 1:
                                p.w(g, "pe", cons[(s - 1, j)])
                            if j < 4:
                                p.mark(g.dma_start(out=twb[:, j, :], in_=tw_r[s, :, j, :]), "gq", 16)
                            else:
                                p.mark(g.dma_start(out=twb[0:64, 4, :], in_=dr[w_name][s]), "gq", 16)
                    gq_s = p.v("gq")

                    @p.blk.tensor
                    def _(te, s=s, gq_s=gq_s):
                        p.w(te, "gq", gq_s)
                        p.w(te, "dv", v_lh)
                        if s == 0:
                            for sem, v in pre.items():
                                p.w(te, sem, v)
                        if s >= 1:
                            p.w(te, "dv", evac[s - 1])
                        for j in range(5):
                            for c in range(nchk):
                                if j < 4:
                                    i = te.matmul(ps_b[:, c * 512:(c + 1) * 512],
                                                  lhsp[:, j, :], twb[:, j, c * 512:(c + 1) * 512],
                                                  start=(j == 0), stop=False)
                                else:
                                    i = te.matmul(ps_b[:, c * 512:(c + 1) * 512],
                                                  I64[:, :], twb[0:64, 4, c * 512:(c + 1) * 512],
                                                  start=False, stop=True)
                            cons[(s, j)] = p.mark(i, "pe")

                    @p.blk.vector
                    def _(ve, s=s):
                        p.w(ve, "pe", cons[(s, 4)])
                        t_i, pb = s // 2, (s % 2) * 64
                        for c in range(nchk):
                            i = ve.tensor_copy(dst[pb:pb + 64, t_i, c * 512:(c + 1) * 512],
                                               ps_b[:, c * 512:(c + 1) * 512])
                        evac[s] = p.mark(i, "dv")
                return evac[nstrip - 1], {"pe": p.v("pe"), "dv": p.v("dv"), "ac": p.v("ac")}

        ev1, post1 = build(1, pre_build)
        if _FFN_BARRIERS[2]:
            p.barrier()
        ev2, post2 = build(2, post1)
        if _FFN_BARRIERS[3]:
            p.barrier()
        if _DEBUG_FFN:
            @p.blk.gpsimd
            def _(g):
                p.w(g, "ac", ev2)
                p.mark(g.dma_start(out=dr["tq_w1"][:], in_=mW1T[0][:]), "gq", 16)
                p.mark(g.dma_start(out=dr["tq_w2"][:], in_=mW2T[0][:]), "gq", 16)
                for k in range(NT_DM):
                    p.mark(g.dma_start(out=dr["tq_w1f"][k * 128:(k + 1) * 128, :], in_=mW1T[k][:]), "gq", 16)
                for k in range(NT_F):
                    p.mark(g.dma_start(out=dr["tq_w2f"][k * 128:(k + 1) * 128, :], in_=mW2T[k][:]), "gq", 16)

        p.barrier()
        UC = 256
        with ExitStack() as ues:
            u_sb = _sb(p, ues, "ff_u", [128, NT_F, UC], BF16)
            res_c = _sb(p, ues, "ff_res", [128, NT_DM, UC])
            xln_u = _sb(p, ues, "ff_xlnu", [128, NT_DM, UC], BF16)
            ps_u = [_ps(p, ues, f"ff_psu{i}", [128, UC]) for i in range(4)]
            ps_d = [_ps(p, ues, f"ff_psd{i}", [128, UC]) for i in range(4)]
            res_r = res_src.rearrange("(k p) s -> p k s", p=128)
            dst_r = res_dst.rearrange("(k p) s -> p k s", p=128)
            upe = {}
            uac = {}
            dpe = {}
            adddv = {}
            dnlast = {}
            for c in range(S // UC):
                @p.blk.sync
                def _(sync, c=c):
                    if c > 0:
                        p.w(sync, "pe", dnlast[c - 1])
                        p.w(sync, "dv", adddv[(c - 1, NT_DM - 1)])
                    p.mark(sync.dma_start(out=xln_u[:], in_=sp_xln_r[:, :, c * UC:(c + 1) * UC]), "sq", 16)
                    p.mark(sync.dma_start(out=res_c[:], in_=res_r[:, :, c * UC:(c + 1) * UC]), "sq", 16)
                ld_u = p.v("sq")

                ac0_c = p.v("ac")
                for ft in range(NT_F):
                    @p.blk.tensor
                    def _(te, c=c, ft=ft, ld_u=ld_u, ac0_c=ac0_c):
                        if ft == 0:
                            p.w(te, "sq", ld_u)
                            if c == 0:
                                p.w(te, "dv", ev1)
                        if ft >= 4:
                            p.w(te, "ac", ac0_c + ft - 3)
                        elif c > 0:
                            p.w(te, "ac", uac[(c - 1, NT_F - 4 + ft)])
                        for k in range(NT_DM):
                            i = te.matmul(ps_u[ft % 4][:, :], mW1T[:, k, ft * 128:(ft + 1) * 128],
                                          xln_u[:, k, :], start=(k == 0), stop=(k == NT_DM - 1))
                        upe[(c, ft)] = p.mark(i, "pe")

                dv0_c = p.v("dv")
                for m in range(NT_DM):
                    @p.blk.tensor
                    def _(te, c=c, m=m, ac0_c=ac0_c, dv0_c=dv0_c):
                        if m == 0:
                            p.w(te, "ac", ac0_c + NT_F)
                            if c == 0:
                                p.w(te, "dv", ev2)
                        if m >= 4:
                            p.w(te, "dv", dv0_c + m - 3)
                        elif c > 0:
                            p.w(te, "dv", adddv[(c - 1, NT_DM - 4 + m)])
                        for k in range(NT_F):
                            i = te.matmul(ps_d[m % 4][:, :], mW2T[:, k, m * 128:(m + 1) * 128],
                                          u_sb[:, k, :], start=(k == 0), stop=(k == NT_F - 1))
                        dpe[(c, m)] = p.mark(i, "pe")
                        if m == NT_DM - 1:
                            dnlast[c] = dpe[(c, m)]

                @p.blk.scalar
                def _(sc, c=c):
                    if c > 0:
                        p.w(sc, "pe", dnlast[c - 1])
                    for ft in range(NT_F):
                        p.w(sc, "pe", upe[(c, ft)])
                        i = sc.activation(u_sb[:, ft, :], ps_u[ft % 4][:, :],
                                          AF.Gelu_apprx_tanh, bias=mb1c[:, ft:ft + 1])
                        uac[(c, ft)] = p.mark(i, "ac")
                assert p.v("ac") == ac0_c + NT_F

                @p.blk.vector
                def _(ve, c=c, ld_u=ld_u):
                    for m in range(NT_DM):
                        p.w(ve, "pe", dpe[(c, m)])
                        if m == 0:
                            p.w(ve, "sq", ld_u)
                        i = ve.scalar_tensor_tensor(res_c[:, m, :], ps_d[m % 4][:, :],
                                                    mb2c[:, m:m + 1], res_c[:, m, :],
                                                    OP.add, OP.add)
                        adddv[(c, m)] = p.mark(i, "dv")
                assert p.v("dv") == dv0_c + NT_DM

                @p.blk.sync
                def _(sync, c=c):
                    p.w(sync, "dv", adddv[(c, NT_DM - 1)])
                    p.mark(sync.dma_start(out=dst_r[:, :, c * UC:(c + 1) * UC], in_=res_c[:]), "sq", 16)
    p.barrier()


def _attn(p, dr, res_src, res_dst):
    p.barrier()
    with ExitStack() as es:
        QT = _sb(p, es, "at_qt", [128, NT_DM, S], BF16)
        KT = _sb(p, es, "at_kt", [128, NT_DM, S], BF16)
        VT = _sb(p, es, "at_vt", [128, 16, DM], BF16)
        bvbc = _sb(p, es, "at_bvbc", [128, DM])
        sp_xln_r = dr["sp_xln"].rearrange("(k p) s -> p k s", p=128)
        res_r = res_src.rearrange("(k p) s -> p k s", p=128)
        dst_r = res_dst.rearrange("(k p) s -> p k s", p=128)
        I128, ones1 = p.t["I128"], p.t["ones1"]

        with ExitStack() as bes:
            @p.blk.sync
            def _(sync):
                p.mark(sync.dma_start(out=bvbc[:], in_=bass.AP(dr["bv_r"].tensor, 0, [[0, 128], [1, DM]])), "sq", 16)
            a_bv = ("sq", p.v("sq"))

            wq = [_sb(p, bes, f"at_w{i}", [128, NT_DM, DM], BF16) for i in range(2)]
            xln_ch = _sb(p, bes, "at_xln", [128, NT_DM, CH], BF16)
            ps_q = [_ps(p, bes, f"at_psq{i}", [128, CH]) for i in range(2)]

            @p.blk.gpsimd
            def _(g):
                p.mark(g.dma_start(out=wq[0][:], in_=dr["W_q"].rearrange("(k p) d -> p k d", p=128)), "gq", 16)
                p.mark(g.dma_start(out=wq[1][:], in_=dr["W_k"].rearrange("(k p) d -> p k d", p=128)), "gq", 16)
            ld_wq = p.v("gq")
            ld_wv = {}

            prpe = {}
            prac = {}
            for proj in range(3):
                wt = wq[proj % 2]
                for ch in range(NCH):
                    @p.blk.sync
                    def _(sync, proj=proj, ch=ch):
                        if proj > 0 or ch > 0:
                            prev = prpe.get((proj, ch - 1)) if ch > 0 else prpe[(proj - 1, NCH - 1)]
                            p.w(sync, "pe", prev)
                        p.mark(sync.dma_start(out=xln_ch[:], in_=sp_xln_r[:, :, ch * CH:(ch + 1) * CH]), "sq", 16)
                    ldx = p.v("sq")

                    for o in range(NT_DM):
                        @p.blk.tensor
                        def _(te, proj=proj, ch=ch, o=o, ldx=ldx, wt=wt):
                            if o == 0:
                                p.w(te, "sq", ldx)
                                if proj == 0 and ch == 0:
                                    p.w(te, "gq", ld_wq)
                                if proj == 2 and ch == 0:
                                    p.w(te, "gq", ld_wv[0])
                            if (ch * NT_DM + o) >= 2 or proj > 0:
                                prv = prac.get((proj, ch, o - 2)) or prac.get((proj, ch - 1, NT_DM - 2 + o)) \
                                    or (prac[(proj - 1, NCH - 1, NT_DM - 2 + o)] if proj > 0 and ch == 0 and o < 2 else None)
                                if prv is not None:
                                    p.w(te, prv[0], prv[1])
                            if proj < 2:
                                for k in range(NT_DM):
                                    i = te.matmul(ps_q[o % 2][:, :], wt[:, k, o * 128:(o + 1) * 128],
                                                  xln_ch[:, k, :], start=(k == 0), stop=(k == NT_DM - 1))
                            else:
                                for k in range(NT_DM):
                                    i = te.matmul(ps_q[o % 2][:, :],
                                                  xln_ch[:, k, (o % 4) * 128:(o % 4) * 128 + 128],
                                                  wt[:, k, (o // 4) * 512:(o // 4) * 512 + 512],
                                                  start=(k == 0), stop=(k == NT_DM - 1))
                            prpe[(proj, ch)] = p.mark(i, "pe")

                        if proj < 2:
                            @p.blk.scalar
                            def _(sc, proj=proj, ch=ch, o=o):
                                p.w(sc, "pe", prpe[(proj, ch)])
                                dst = QT if proj == 0 else KT
                                bias = p.t["bq"] if proj == 0 else p.t["bk"]
                                i = sc.activation(dst[:, o, ch * CH:(ch + 1) * CH], ps_q[o % 2][:, :],
                                                  AF.Identity, bias=bias[:, o:o + 1])
                                prac[(proj, ch, o)] = ("ac", p.mark(i, "ac"))
                        else:
                            @p.blk.vector
                            def _(ve, proj=proj, ch=ch, o=o):
                                p.w(ve, "pe", prpe[(proj, ch)])
                                p.w(ve, a_bv[0], a_bv[1])
                                ssub = ch * 4 + (o % 4)
                                dvc = o // 4
                                i = ve.tensor_tensor(VT[:, ssub, dvc * 512:(dvc + 1) * 512],
                                                     ps_q[o % 2][:, :], bvbc[:, dvc * 512:(dvc + 1) * 512],
                                                     OP.add)
                                prac[(proj, ch, o)] = ("dv", p.mark(i, "dv"))

                if proj == 0:
                    @p.blk.gpsimd
                    def _(g):
                        p.w(g, "pe", prpe[(0, NCH - 1)])
                        p.mark(g.dma_start(out=wq[0][:], in_=dr["W_v"].rearrange("(k p) d -> p k d", p=128)), "gq", 16)
                        ld_wv[0] = p.v("gq")

        with ExitStack() as s2:
          PT = _sb(p, s2, "at_pt", [128, 16, S], BF16)
          with ExitStack() as ses:
            Ssb = _sb(p, ses, "at_ssb", [128, S])
            Pf = _sb(p, ses, "at_pf", [128, S])
            nm32 = _sb(p, ses, "at_nm", [128, 1])
            rs = _sb(p, ses, "at_rs", [128, 1])
            rsr = _sb(p, ses, "at_rsr", [128, 1])
            Pbf = _sb(p, ses, "at_pbf", [128, S], BF16)
            ps_sc = _ps(p, ses, "at_pssc", [128, S])
            ps_tr = _ps(p, ses, "at_pstr", [128, 128], BF16)
            scpe = {}
            smdv = {}
            smac = {}
            trdv = {}
            qk_done = prac[(1, NCH - 1, NT_DM - 1)][1]
            evsc = {}
            for sq_t in range(16):
                @p.blk.tensor
                def _(te, sq_t=sq_t):
                    if sq_t == 0:
                        p.w(te, "ac", qk_done)
                        p.w(te, "dv", prac[(2, NCH - 1, NT_DM - 1)][1])
                    if sq_t > 0:
                        p.w(te, "ac", evsc[sq_t - 1])
                    for ch in range(NCH):
                        for k in range(NT_DM):
                            i = te.matmul(ps_sc[:, ch * CH:(ch + 1) * CH],
                                          QT[:, k, sq_t * 128:(sq_t + 1) * 128],
                                          KT[:, k, ch * CH:(ch + 1) * CH],
                                          start=(k == 0), stop=(k == NT_DM - 1))
                    scpe[sq_t] = p.mark(i, "pe")

                @p.blk.scalar
                def _(sc, sq_t=sq_t):
                    p.w(sc, "pe", scpe[sq_t])
                    if sq_t > 0:
                        p.w(sc, "ac", smac[sq_t - 1])
                    evsc[sq_t] = p.mark(sc.copy(Ssb[:], ps_sc[:]), "ac")

                @p.blk.vector
                def _(ve, sq_t=sq_t):
                    p.w(ve, "ac", evsc[sq_t])
                    v = p.mark(ve.tensor_reduce(nm32[:], Ssb[:], AX.X, OP.max, negate=True), "dv")
                    ve.wait_ge(p.sems["dv"], v)
                    p.mark(ve.tensor_scalar(nm32[:], nm32[:], 0.03125, None, OP.mult), "dv")
                smdv[sq_t] = p.v("dv")

                @p.blk.scalar
                def _(sc, sq_t=sq_t):
                    p.w(sc, "dv", smdv[sq_t])
                    i = sc.activation(Pf[:], Ssb[:], AF.Exp, scale=0.03125, bias=nm32[:],
                                      accum_out=rs[:])
                    smac[sq_t] = p.mark(i, "ac")

                @p.blk.vector
                def _(ve, sq_t=sq_t):
                    p.w(ve, "ac", smac[sq_t])
                    v = p.mark(ve.reciprocal(rsr[:], rs[:]), "dv")
                    ve.wait_ge(p.sems["dv"], v)
                    p.mark(ve.tensor_scalar(Pbf[:], Pf[:], rsr[:], None, OP.mult), "dv")
                v_pn = p.v("dv")

                if _DEBUG_ATT and sq_t == 0:
                    @p.blk.gpsimd
                    def _(g):
                        p.w(g, "dv", v_pn)
                        p.mark(g.dma_start(out=dr["ta_nm"][:], in_=nm32[:]), "gq", 16)
                        p.mark(g.dma_start(out=dr["ta_rs"][:], in_=rs[:]), "gq", 16)
                        p.mark(g.dma_start(out=dr["ta_rsr"][:], in_=rsr[:]), "gq", 16)
                        p.mark(g.dma_start(out=dr["ta_pbf"][:], in_=Pbf[:]), "gq", 16)
                    tapA = p.v("gq")

                    @p.blk.vector
                    def _(ve, tapA=tapA):
                        p.w(ve, "gq", tapA)

                for k in range(16):
                    @p.blk.tensor
                    def _(te, sq_t=sq_t, k=k, v_pn=v_pn):
                        if k == 0:
                            p.w(te, "dv", v_pn)
                        if (sq_t, k) != (0, 0):
                            prev = trdv.get((sq_t, k - 1)) or trdv[(sq_t - 1, 15)]
                            p.w(te, "dv", prev)
                        p.mark(te.transpose(ps_tr[:], Pbf[:, k * 128:(k + 1) * 128], I128[:]), "pe")

                    @p.blk.vector
                    def _(ve, sq_t=sq_t, k=k):
                        p.w(ve, "pe", p.v("pe"))
                        i = ve.tensor_copy(PT[:, k, sq_t * 128:(sq_t + 1) * 128], ps_tr[:])
                        trdv[(sq_t, k)] = p.mark(i, "dv")

          if _DEBUG_ATT:
            @p.blk.gpsimd
            def _(g):
                p.w(g, "dv", trdv[(15, 15)])
                p.mark(g.dma_start(out=dr["ta_pt"][:], in_=PT[:, 0, 0:128]), "gq", 16)
                p.mark(g.dma_start(out=dr["ta_qt"][:], in_=QT[:, 0, :]), "gq", 16)
                p.mark(g.dma_start(out=dr["ta_kt"][:], in_=KT[:, 0, :]), "gq", 16)
                p.mark(g.dma_start(out=dr["ta_vt"][:], in_=VT[:, 0, :]), "gq", 16)
          p.barrier()
          with ExitStack() as aes:
            res_c = _sb(p, aes, "at_res", [128, NT_DM, CH])
            tda = [_sb(p, aes, f"at_td{i}", [128, CH]) for i in range(2)]
            ps_av = [_ps(p, aes, f"at_psav{i}", [128, CH]) for i in range(2)]
            avpe = {}
            avac = {}
            avdv = {}
            for ch in range(NCH):
                @p.blk.sync
                def _(sync, ch=ch):
                    if ch > 0:
                        p.w(sync, "dv", avdv[(ch - 1, NT_DM - 1)])
                    p.mark(sync.dma_start(out=res_c[:], in_=res_r[:, :, ch * CH:(ch + 1) * CH]), "sq", 16)
                ld_r = p.v("sq")

                for m in range(NT_DM):
                    @p.blk.tensor
                    def _(te, ch=ch, m=m, ld_r=ld_r):
                        fl = ch * NT_DM + m
                        if fl >= 2:
                            p.w(te, "ac", avac[(ch, m - 2)] if m >= 2 else avac[(ch - 1, NT_DM - 2 + m)])
                        if fl == 0:
                            p.w(te, "dv", trdv[(15, 15)])
                            p.w(te, "ac", smac[15])
                            p.w(te, "dv", prac[(2, NCH - 1, NT_DM - 1)][1])
                        for k in range(16):
                            i = te.matmul(ps_av[m % 2][:, :], VT[:, k, m * 128:(m + 1) * 128],
                                          PT[:, k, ch * CH:(ch + 1) * CH],
                                          start=(k == 0), stop=(k == 15))
                        avpe[(ch, m)] = p.mark(i, "pe")

                    @p.blk.scalar
                    def _(sc, ch=ch, m=m):
                        p.w(sc, "pe", avpe[(ch, m)])
                        if (ch, m) >= (0, 2):
                            prv = avdv.get((ch, m - 2)) if m >= 2 else avdv.get((ch - 1, NT_DM - 2 + m))
                            p.w(sc, "dv", prv)
                        i = sc.copy(tda[m % 2][:], ps_av[m % 2][:])
                        avac[(ch, m)] = p.mark(i, "ac")

                    @p.blk.vector
                    def _(ve, ch=ch, m=m, ld_r=ld_r):
                        p.w(ve, "ac", avac[(ch, m)])
                        if m == 0:
                            p.w(ve, "sq", ld_r)
                        i = ve.tensor_tensor(res_c[:, m, :], res_c[:, m, :], tda[m % 2][:], OP.add)
                        avdv[(ch, m)] = p.mark(i, "dv")

                @p.blk.sync
                def _(sync, ch=ch):
                    p.w(sync, "dv", avdv[(ch, NT_DM - 1)])
                    p.mark(sync.dma_start(out=dst_r[:, :, ch * CH:(ch + 1) * CH], in_=res_c[:]), "sq", 16)
    p.barrier()


def kernel(**inputs):
    shared, xs = _host_prep(inputs)
    nc = _build_program()
    in_maps = []
    for b in range(NCORE):
        m = dict(shared)
        m["x_in"] = xs[b]
        in_maps.append(m)
    res = run_bass_kernel_spmd(nc, in_maps, core_ids=list(range(NCORE)))
    out = np.stack([np.asarray(r["out"], np.float32).T for r in res.results], 0)
    return np.ascontiguousarray(out)


# version 07 (final)
# speedup vs baseline: 1.0050x; 1.0050x over previous
import numpy as np
import ml_dtypes
from contextlib import ExitStack

import concourse.bass as bass
import concourse.mybir as mybir
from concourse.bass_utils import run_bass_kernel_spmd

F32 = mybir.dt.float32
BF16 = mybir.dt.bfloat16
DR = mybir.MatmulPerfMode.DoubleRow
OP = mybir.AluOpType
AF = mybir.ActivationFunctionType
AX = mybir.AxisListType

DM = 1024
S = 2048
DI = 2048
NST = 16
DTR = 64
FF = 4096
NE = 8
NCORE = 8
CH = 512
NCH = S // CH
NT_DM = DM // 128
NT_DI = DI // 128
NT_F = FF // 128

_N_SUB = 4
_DEBUG_TAPS = False
_DEBUG_FFN = False
_FFN_BARRIERS = (0, 0, 0, 0, 0)
_FFN_PARANOID = (0, 0, 0)
_DEBUG_ATT = False

_FCONSTS = [1e-5, -0.5, 0.03125, 1.0, 0.0, 0.5, -1.0]


def _host_prep(inp):
    d = {}
    f32 = np.float32

    def c(x):
        return np.ascontiguousarray(x, dtype=f32)

    d["W_inT"] = c(inp["L0_attn_a_W_in"].T)
    for sfx, tag in (("", "f"), ("_b", "b")):
        cw = np.asarray(inp[f"L0_attn_a_conv_w{sfx}"], f32)[:, 0, :]
        d[f"convw_{tag}"] = c(cw.reshape(NT_DI, 128, 4).transpose(1, 0, 2).reshape(128, NT_DI * 4))
        d[f"convb_{tag}"] = c(np.asarray(inp[f"L0_attn_a_conv_b{sfx}"], f32).reshape(NT_DI, 128).T)
        wx = np.asarray(inp[f"L0_attn_a_Wx{sfx}"], f32)
        wxp = np.zeros((DI, 128), f32)
        wxp[:, 0:16] = wx[DTR:DTR + 16].T
        wxp[:, 16:32] = wx[DTR + 16:DTR + 32].T
        wxp[:, 64:128] = wx[0:DTR].T
        d[f"WxTp_{tag}"] = c(wxp)
        d[f"WdtT_{tag}"] = c(np.asarray(inp[f"L0_attn_a_Wdt{sfx}"], f32).T)
        d[f"bdt_{tag}"] = c(np.asarray(inp[f"L0_attn_a_bdt{sfx}"], f32).reshape(NT_DI, 128).T)
        al = np.asarray(inp["L0_attn_a_A_log" if tag == "f" else "L0_attn_a_A_b_log"], f32)
        d[f"Alog_{tag}"] = c(al.reshape(NT_DI, 128, NST).transpose(1, 0, 2).reshape(128, NT_DI * NST))
        dd = np.asarray(inp["L0_attn_a_D" if tag == "f" else "L0_attn_a_D_b"], f32)
        d[f"D_{tag}"] = c(dd.reshape(NT_DI, 128).T)
    d["W_outT"] = c(inp["L0_attn_a_W_out"].T)

    d["W_q"] = c(inp["L0_attn_b_W_q"])
    d["W_k"] = c(inp["L0_attn_b_W_k"])
    d["W_v"] = c(inp["L0_attn_b_W_v"])
    d["bq"] = c(np.asarray(inp["L0_attn_b_b_q"], f32).reshape(NT_DM, 128).T)
    d["bk"] = c(np.asarray(inp["L0_attn_b_b_k"], f32).reshape(NT_DM, 128).T)
    d["bv_r"] = c(np.asarray(inp["L0_attn_b_b_v"], f32).reshape(1, DM))

    d["gw1T"] = c(inp["L0_ffn_gate_w1"].T)
    d["gb1"] = c(np.asarray(inp["L0_ffn_gate_b1"], f32).reshape(NT_DM, 128).T)
    d["gw2T"] = c(inp["L0_ffn_gate_w2"].T)
    d["gb2"] = c(np.asarray(inp["L0_ffn_gate_b2"], f32).reshape(NE, 1))

    W1 = np.asarray(inp["L0_ffn_W1"], f32)
    TW1 = np.asarray(inp["L0_ffn_TW1"], f32)
    t1 = TW1.transpose(2, 0, 1).reshape(16, 64, NE, FF)
    d["TW1_strips"] = c(t1.transpose(0, 2, 1, 3).reshape(16, NE * 64, FF))
    d["W1_strips"] = c(W1.T.reshape(16, 64, FF))
    W2 = np.asarray(inp["L0_ffn_W2"], f32)
    TW2 = np.asarray(inp["L0_ffn_TW2"], f32)
    t2 = TW2.transpose(2, 0, 1).reshape(64, 64, NE, DM)
    d["TW2_strips"] = c(t2.transpose(0, 2, 1, 3).reshape(64, NE * 64, DM))
    d["W2_strips"] = c(W2.T.reshape(64, 64, DM))
    tb1 = np.concatenate([np.asarray(inp["L0_ffn_b1"], f32)[None],
                          np.asarray(inp["L0_ffn_Tb1"], f32)], 0)
    tb2 = np.concatenate([np.asarray(inp["L0_ffn_b2"], f32)[None],
                          np.asarray(inp["L0_ffn_Tb2"], f32)], 0)
    d["TbT1"] = c(tb1.reshape(9, FF // 128, 128).transpose(2, 0, 1))
    d["TbT2"] = c(tb2.reshape(9, DM // 128, 128).transpose(2, 0, 1))

    d["I128"] = np.eye(128, dtype=f32)
    ind32 = np.zeros((32, 32 * 128), f32)
    for j in range(32):
        ind32[j, j * 128:(j + 1) * 128] = 1.0
    d["ind32"] = np.ascontiguousarray(ind32, dtype=ml_dtypes.bfloat16)
    d["I64t2"] = c(np.concatenate([np.eye(64), np.eye(64)], 0))
    d["I64"] = np.eye(64, dtype=f32)
    indp = np.zeros((9, 4 * 128), f32)
    for j in range(4):
        indp[1 + 2 * j, j * 128:j * 128 + 64] = 1.0
        indp[2 + 2 * j, j * 128 + 64:(j + 1) * 128] = 1.0
    d["indpair"] = indp
    indsel = np.zeros((9, 9 * 128), f32)
    for e in range(9):
        indsel[e, e * 128:(e + 1) * 128] = 1.0
    d["indsel"] = indsel
    d["onesLN"] = np.full((128, 128), 1.0 / DM, f32)
    d["ones1"] = np.ones((1, 128), f32)
    d["fconsts"] = np.tile(np.array(_FCONSTS, f32)[None, :], (128, 1))

    x = np.asarray(inp["L0_attn_a_hidden"], f32)
    xs = [np.ascontiguousarray(x[b].T) for b in range(NCORE)]
    return d, xs


class P:
    def __init__(self, nc, es):
        self.nc = nc
        self.es = es
        self.blk = es.enter_context(nc.Block())
        self.sems = {}
        self.cnt = {}
        for name in ("sq", "gq", "pe", "dv", "ac", "po"):
            self.sems[name] = es.enter_context(nc.semaphore(name))
            self.cnt[name] = 0
        self.t = {}

    def mark(self, instr, s, k=1):
        instr.then_inc(self.sems[s], k)
        self.cnt[s] += k
        return self.cnt[s]

    def v(self, s):
        return self.cnt[s]

    def w(self, eng, s, val):
        if val and val > 0:
            eng.wait_ge(self.sems[s], val)

    def barrier(self):
        tgt = {s: self.v(s) for s in self.cnt}
        for ename in ("tensor", "vector", "scalar", "gpsimd", "sync"):
            @getattr(self.blk, ename)
            def _(e, tgt=tgt):
                for s, v in tgt.items():
                    if v > 0:
                        e.wait_ge(self.sems[s], v)


_UID = [0]


def _sb(p, es, name, shape, dtype=F32):
    _UID[0] += 1
    return es.enter_context(p.nc.sbuf_tensor(f"{name}_{_UID[0]}", shape, dtype))


def _ps(p, es, name, shape, dtype=F32):
    _UID[0] += 1
    return es.enter_context(p.nc.psum_tensor(f"{name}_{_UID[0]}", shape, dtype))


def _build_program():
    nc = bass.Bass("TRN2", target_bir_lowering=False, debug=False)
    dr = {}

    def din(name, shape, dtype=F32):
        dr[name] = nc.dram_tensor(name, list(shape), dtype, kind="ExternalInput").ap()

    din("x_in", [DM, S])
    din("W_inT", [DM, 2 * DI])
    for t in ("f", "b"):
        din(f"convw_{t}", [128, NT_DI * 4]); din(f"convb_{t}", [128, NT_DI])
        din(f"WxTp_{t}", [DI, 128]); din(f"WdtT_{t}", [DTR, DI])
        din(f"bdt_{t}", [128, NT_DI]); din(f"Alog_{t}", [128, NT_DI * NST])
        din(f"D_{t}", [128, NT_DI])
    din("W_outT", [DI, DM])
    din("W_q", [DM, DM]); din("W_k", [DM, DM]); din("W_v", [DM, DM])
    din("bq", [128, NT_DM]); din("bk", [128, NT_DM]); din("bv_r", [1, DM])
    din("gw1T", [DM, DM]); din("gb1", [128, NT_DM]); din("gw2T", [DM, NE]); din("gb2", [NE, 1])
    din("TW1_strips", [16, NE * 64, FF]); din("W1_strips", [16, 64, FF])
    din("TW2_strips", [64, NE * 64, DM]); din("W2_strips", [64, 64, DM])
    din("TbT1", [128, 9, NT_F]); din("TbT2", [128, 9, NT_DM])
    din("I128", [128, 128]); din("ind32", [32, 32 * 128], BF16); din("I64t2", [128, 64])
    din("I64", [64, 64]); din("indpair", [9, 4 * 128]); din("indsel", [9, 9 * 128])
    din("onesLN", [128, 128]); din("ones1", [1, 128])
    din("fconsts", [128, len(_FCONSTS)])

    dr["out"] = nc.dram_tensor("out", [DM, S], F32, kind="ExternalOutput").ap()

    def scr(name, shape, dtype=F32):
        dr[name] = nc.dram_tensor(name, list(shape), dtype, kind="Internal").ap()

    scr("res_a", [DM, S]); scr("res_b", [DM, S])
    scr("sp_xzx", [DI, S]); scr("sp_sz", [DI, S]); scr("sp_xsil", [DI, S])
    scr("sp_of", [DI, S], BF16); scr("sp_ofb", [DI, S], BF16)
    scr("sp_xln", [DM, S], BF16)
    scr("g_scr", [16]); scr("mb1_scr", [FF]); scr("mb2_scr", [DM])
    scr("TW1bf", [16, NE * 64, FF], BF16); scr("TW2bf", [64, NE * 64, DM], BF16)
    if _DEBUG_TAPS:
        for nm in ("tp_dt", "tp_dtx", "tp_B", "tp_C", "tp_dA", "tp_h", "tp_pn", "tp_y0", "tp_yf"):
            dr[nm] = nc.dram_tensor(nm, [128, S], F32, kind="ExternalOutput").ap()
    if _DEBUG_FFN:
        dr["tq_g"] = nc.dram_tensor("tq_g", [16, 1], F32, kind="ExternalOutput").ap()
        dr["tq_gp"] = nc.dram_tensor("tq_gp", [128, 4], F32, kind="ExternalOutput").ap()
        dr["tq_mb1"] = nc.dram_tensor("tq_mb1", [128, NT_F], F32, kind="ExternalOutput").ap()
        dr["tq_mb2"] = nc.dram_tensor("tq_mb2", [128, NT_DM], F32, kind="ExternalOutput").ap()
        dr["tq_w1"] = nc.dram_tensor("tq_w1", [128, FF], BF16, kind="ExternalOutput").ap()
        dr["tq_w2"] = nc.dram_tensor("tq_w2", [128, DM], BF16, kind="ExternalOutput").ap()
        dr["tq_hg"] = nc.dram_tensor("tq_hg", [128, NT_DM * CH], BF16, kind="ExternalOutput").ap()
        dr["tq_u"] = nc.dram_tensor("tq_u", [128, 256], BF16, kind="ExternalOutput").ap()
        dr["tq_w1f"] = nc.dram_tensor("tq_w1f", [DM, FF], BF16, kind="ExternalOutput").ap()
        dr["tq_w2f"] = nc.dram_tensor("tq_w2f", [FF, DM], BF16, kind="ExternalOutput").ap()
    if _DEBUG_ATT:
        dr["ta_nm"] = nc.dram_tensor("ta_nm", [128, 1], F32, kind="ExternalOutput").ap()
        dr["ta_rs"] = nc.dram_tensor("ta_rs", [128, 1], F32, kind="ExternalOutput").ap()
        dr["ta_rsr"] = nc.dram_tensor("ta_rsr", [128, 1], F32, kind="ExternalOutput").ap()
        dr["ta_pbf"] = nc.dram_tensor("ta_pbf", [128, S], BF16, kind="ExternalOutput").ap()
        dr["ta_pt"] = nc.dram_tensor("ta_pt", [128, 128], BF16, kind="ExternalOutput").ap()
        dr["ta_qt"] = nc.dram_tensor("ta_qt", [128, S], BF16, kind="ExternalOutput").ap()
        dr["ta_kt"] = nc.dram_tensor("ta_kt", [128, S], BF16, kind="ExternalOutput").ap()
        dr["ta_vt"] = nc.dram_tensor("ta_vt", [128, DM], BF16, kind="ExternalOutput").ap()

    with ExitStack() as es:
        p = P(nc, es)
        _emit(p, dr, es)
    return nc


def _emit(p, dr, es):
    T = p.t

    T["I128"] = _sb(p, es, "c_I128", [128, 128], BF16)
    T["I64t2"] = _sb(p, es, "c_I64t2", [128, 64], F32)
    T["I64"] = _sb(p, es, "c_I64", [64, 64], BF16)
    T["indp"] = _sb(p, es, "c_indp", [9, 4 * 128], BF16)
    T["indsel"] = _sb(p, es, "c_indsel", [9, 9 * 128], BF16)
    T["onesLN"] = _sb(p, es, "c_onesLN", [128, 128], BF16)
    T["ones1"] = _sb(p, es, "c_ones1", [1, 128], BF16)
    for t in ("f", "b"):
        T[f"convw_{t}"] = _sb(p, es, f"m_convw_{t}", [128, NT_DI * 4])
        T[f"convb_{t}"] = _sb(p, es, f"m_convb_{t}", [128, NT_DI])
        T[f"bdt_{t}"] = _sb(p, es, f"m_bdt_{t}", [128, NT_DI])
        T[f"Aneg_{t}"] = _sb(p, es, f"m_Aneg_{t}", [128, NT_DI * NST])
        T[f"D_{t}"] = _sb(p, es, f"m_D_{t}", [128, NT_DI])
    T["fconsts"] = _sb(p, es, "c_fconsts", [128, len(_FCONSTS)])
    for j, val in enumerate(_FCONSTS):
        p.nc.const_aps.register(F32, float(val), T["fconsts"][:, j:j + 1])
    T["WdtT_f"] = _sb(p, es, "wdtT_f", [DTR, DI], BF16)
    T["WdtT_b"] = _sb(p, es, "wdtT_b", [DTR, DI], BF16)
    T["gb2"] = _sb(p, es, "a_gb2", [NE, 1])
    T["bq"] = _sb(p, es, "a_bq", [128, NT_DM])
    T["bk"] = _sb(p, es, "a_bk", [128, NT_DM])
    T["gb1"] = _sb(p, es, "a_gb1", [128, NT_DM])

    @p.blk.sync
    def _(sync):
        for name in ("fconsts", "convw_f", "convw_b", "convb_f", "convb_b", "bdt_f", "bdt_b",
                     "D_f", "D_b", "gb2", "bq", "bk", "gb1"):
            p.mark(sync.dma_start(out=T[name][:], in_=dr[name][:]), "sq", 16)
        p.mark(sync.dma_start(out=T["Aneg_f"][:], in_=dr["Alog_f"][:]), "sq", 16)
        p.mark(sync.dma_start(out=T["Aneg_b"][:], in_=dr["Alog_b"][:]), "sq", 16)

    @p.blk.gpsimd
    def _(g):
        for name, src in (("I128", "I128"), ("I64", "I64"), ("indp", "indpair"),
                          ("indsel", "indsel"), ("onesLN", "onesLN"), ("ones1", "ones1")):
            p.mark(g.dma_start(out=T[name][:], in_=dr[src][:]), "gq", 16)
        p.mark(g.dma_start(out=T["I64t2"][:], in_=dr["I64t2"][:]), "gq", 16)

    ld_small = p.v("sq")

    @p.blk.scalar
    def _(sc):
        p.w(sc, "sq", ld_small)
        sc.activation(T["Aneg_f"][:], T["Aneg_f"][:], AF.Exp)
        p.mark(sc.activation(T["Aneg_b"][:], T["Aneg_b"][:], AF.Exp), "ac")

    @p.blk.vector
    def _(ve):
        p.w(ve, "ac", p.v("ac"))
        ve.tensor_scalar(T["Aneg_f"][:], T["Aneg_f"][:], -1.0, None, OP.mult)
        p.mark(ve.tensor_scalar(T["Aneg_b"][:], T["Aneg_b"][:], -1.0, None, OP.mult), "dv")

    res = {0: dr["x_in"], 1: dr["res_a"], 2: dr["res_b"], 3: dr["res_a"]}

    def dst(k):
        return dr["out"] if k == _N_SUB else res[k]

    _ln(p, dr, res[0])
    _mamba(p, dr, res[0], dst(1))
    if _N_SUB >= 2:
        _ln(p, dr, res[1])
        _ffn(p, dr, res[1], dst(2))
    if _N_SUB >= 3:
        _ln(p, dr, res[2])
        _attn(p, dr, res[2], dst(3))
    if _N_SUB >= 4:
        _ln(p, dr, res[3])
        _ffn(p, dr, res[3], dr["out"])

    @p.blk.sync
    def _(sync):
        sync.wait_ge(p.sems["sq"], p.v("sq"))
        sync.wait_ge(p.sems["gq"], p.v("gq"))


def _ln(p, dr, res_src):
    p.barrier()
    with ExitStack() as es:
        x_f = [_sb(p, es, f"ln_x{k}", [128, S]) for k in range(NT_DM)]
        x_bf = [_sb(p, es, f"ln_xb{k}", [128, S], BF16) for k in range(NT_DM)]
        sq_bf = [_sb(p, es, f"ln_sq{k}", [128, S], BF16) for k in range(NT_DM)]
        mm_sq = _sb(p, es, "ln_mmsq", [128, S])
        rstd = _sb(p, es, "ln_rstd", [128, S])
        mr = _sb(p, es, "ln_mr", [128, S])
        t1 = _sb(p, es, "ln_t1", [128, S])
        xo = [_sb(p, es, f"ln_xo{i}", [128, S], BF16) for i in range(2)]
        p_m = _ps(p, es, "ln_pm", [128, S])
        p_s = _ps(p, es, "ln_ps", [128, S])

        @p.blk.sync
        def _(sync):
            for k in range(NT_DM):
                p.mark(sync.dma_start(out=x_f[k][:], in_=res_src[k * 128:(k + 1) * 128, :]), "sq", 16)
        ld = p.v("sq")

        @p.blk.vector
        def _(ve):
            p.w(ve, "sq", ld)
            for k in range(NT_DM):
                i = ve.tensor_copy(x_bf[k][:], x_f[k][:])
            p.mark(i, "dv")
        v_cast = p.v("dv")

        @p.blk.scalar
        def _(sc):
            p.w(sc, "sq", ld)
            for k in range(NT_DM):
                i = sc.activation(sq_bf[k][:], x_f[k][:], AF.Square)
            p.mark(i, "ac")
        a_sq = p.v("ac")

        @p.blk.tensor
        def _(te):
            p.w(te, "dv", v_cast)
            p.w(te, "ac", a_sq)
            ones = p.t["onesLN"]
            for ch in range(NCH):
                for k in range(NT_DM):
                    te.matmul(p_m[:, ch * CH:(ch + 1) * CH], ones[:], x_bf[k][:, ch * CH:(ch + 1) * CH],
                              start=(k == 0), stop=(k == NT_DM - 1))
            for ch in range(NCH):
                for k in range(NT_DM):
                    i = te.matmul(p_s[:, ch * CH:(ch + 1) * CH], ones[:], sq_bf[k][:, ch * CH:(ch + 1) * CH],
                                  start=(k == 0), stop=(k == NT_DM - 1))
            p.mark(i, "pe")
        pe_st = p.v("pe")

        @p.blk.scalar
        def _(sc):
            p.w(sc, "pe", pe_st)
            p.mark(sc.activation(mm_sq[:], p_m[:], AF.Square), "ac")
        a_mm = p.v("ac")

        @p.blk.vector
        def _(ve):
            p.w(ve, "ac", a_mm)
            p.mark(ve.scalar_tensor_tensor(t1[:], p_s[:], 1.0, mm_sq[:], OP.mult, OP.subtract), "dv")
        v_var = p.v("dv")

        @p.blk.scalar
        def _(sc):
            p.w(sc, "dv", v_var)
            v = p.mark(sc.activation(t1[:], t1[:], AF.Ln, bias=1e-5), "ac")
            sc.wait_ge(p.sems["ac"], v)
            p.mark(sc.activation(rstd[:], t1[:], AF.Exp, scale=-0.5), "ac")
        a_rs = p.v("ac")

        dv0 = p.v("dv")
        sq0 = p.v("sq")

        @p.blk.vector
        def _(ve):
            p.w(ve, "ac", a_rs)
            ve.tensor_tensor(mr[:], p_m[:], rstd[:], OP.mult)
            for k in range(NT_DM):
                if k >= 2:
                    p.w(ve, "sq", sq0 + 16 * (k - 1))
                ve.tensor_tensor(t1[:], x_f[k][:], rstd[:], OP.mult)
                p.mark(ve.tensor_tensor(xo[k % 2][:], t1[:], mr[:], OP.subtract), "dv")

        @p.blk.sync
        def _(sync):
            for k in range(NT_DM):
                p.w(sync, "dv", dv0 + k + 1)
                p.mark(sync.dma_start(out=dr["sp_xln"][k * 128:(k + 1) * 128, :], in_=xo[k % 2][:]), "sq", 16)
    p.barrier()


def _mamba(p, dr, res_src, res_dst):
    with ExitStack() as mes:
        for tg in ("f", "b"):
            p.t[f"BCb_{tg}"] = _sb(p, mes, f"BCb_{tg}", [32, S], BF16)
            p.t[f"dtr_{tg}"] = _sb(p, mes, f"dtr_{tg}", [DTR, S], BF16)
        _mamba_stageA(p, dr, "f", rev=False)
        _mamba_stageB(p, dr, "f", rev=False)
        _mamba_stageA(p, dr, "b", rev=True)
        _mamba_stageB(p, dr, "b", rev=True)
        _mamba_outproj(p, dr, res_src, res_dst)
    for tg in ("f", "b"):
        del p.t[f"BCb_{tg}"], p.t[f"dtr_{tg}"]


def _mamba_stageA(p, dr, tag, rev):
    p.barrier()
    if f"BCb_{tag}" not in p.t:
        p.t[f"BCb_{tag}"] = _sb(p, p.es, f"BCb_{tag}", [32, S], BF16)
        p.t[f"dtr_{tag}"] = _sb(p, p.es, f"dtr_{tag}", [DTR, S], BF16)
    with ExitStack() as es:
        xzx = _sb(p, es, "ma_xzx", [128, NT_DI, CH + 3])
        stash = _sb(p, es, "ma_stash", [128, NT_DI, 3])
        sz_a = _sb(p, es, "ma_sz", [128, NT_DI, CH])
        xsil_bf = _sb(p, es, "ma_xsb", [128, NT_DI, CH], BF16)
        xdbl = _sb(p, es, "ma_xdbl", [128, S])
        cvt = [_sb(p, es, f"ma_cvt{i}", [128, CH]) for i in range(2)]
        wxp = _sb(p, es, "ma_wxp", [128, NT_DI, 128], BF16)

        @p.blk.gpsimd
        def _(g):
            p.mark(g.dma_start(out=wxp[:], in_=dr[f"WxTp_{tag}"].rearrange("(t p) c -> p t c", p=128)), "gq", 16)
        ld_wxp = p.v("gq")
        ps_in = [_ps(p, es, f"ma_pin{i}", [128, CH]) for i in range(2)]
        ps_xd = _ps(p, es, "ma_pxd", [128, CH])
        if not rev:
            w_in = [_sb(p, es, f"ma_win{k}", [128, 2 * DI], BF16) for k in range(NT_DM)]
            xln_ch = [_sb(p, es, f"ma_xln{i}", [128, NT_DM, CH], BF16) for i in range(2)]

            @p.blk.gpsimd
            def _(g):
                for k in range(NT_DM):
                    p.mark(g.dma_start(out=w_in[k][:], in_=dr["W_inT"][k * 128:(k + 1) * 128, :]), "gq", 16)
            ld_w = p.v("gq")

            @p.blk.gpsimd
            def _(g):
                p.mark(g.dma_start(out=p.t["WdtT_f"][:], in_=dr["WdtT_f"][:]), "gq", 16)
                p.mark(g.dma_start(out=p.t["WdtT_b"][:], in_=dr["WdtT_b"][:]), "gq", 16)
            p.t["ld_wdt"] = p.v("gq")

            @p.blk.gpsimd
            def _(g):
                for s in range(16):
                    p.mark(g.dma_start(out=dr["TW1bf"][s], in_=dr["TW1_strips"][s]), "gq", 16)
            p.t["conv1"] = p.v("gq")

            @p.blk.gpsimd
            def _(g):
                for s in range(64):
                    p.mark(g.dma_start(out=dr["TW2bf"][s], in_=dr["TW2_strips"][s]), "gq", 16)
            p.t["conv2"] = p.v("gq")
        else:
            ld_x = [_sb(p, es, f"ma_ldx{i}", [128, NT_DI, CH]) for i in range(2)]

        convw, convb = p.t[f"convw_{tag}"], p.t[f"convb_{tag}"]
        sp_xln_r = dr["sp_xln"].rearrange("(k p) s -> p k s", p=128)
        sp_xzx_r = dr["sp_xzx"].rearrange("(t p) s -> p t s", p=128)
        sp_sz_r = dr["sp_sz"].rearrange("(t p) s -> p t s", p=128)
        sp_xsil_r = dr["sp_xsil"].rearrange("(t p) s -> p t s", p=128)

        fillm = {}
        fill_s = "ac" if not rev else "dv"
        silm = {}
        cvm = {}
        xsbm = {}
        inpe = {}
        spx = {}
        szsp = {}
        sps = {}
        xdm = {}
        pech = {}
        stm = {}

        for ch in range(NCH):
            if not rev:
                @p.blk.sync
                def _(sync, ch=ch):
                    if ch >= 2:
                        p.w(sync, "pe", pech[ch - 2])
                    p.mark(sync.dma_start(out=xln_ch[ch % 2][:],
                                          in_=sp_xln_r[:, :, ch * CH:(ch + 1) * CH]), "sq", 16)
                ld_ch = p.v("sq")

                ac0 = p.v("ac")

                @p.blk.tensor
                def _(te, ch=ch, ld_ch=ld_ch, ac0=ac0):
                    p.w(te, "sq", ld_ch)
                    if ch == 0:
                        p.w(te, "gq", ld_w)
                    for dz in range(2 * NT_DI):
                        ig = ch * 2 * NT_DI + dz
                        if dz < 2:
                            if ig >= 2:
                                p.w(te, "ac", fillm[("ig", ig - 2)])
                        else:
                            p.w(te, "ac", ac0 + dz - 1)
                        for kk in range(NT_DM):
                            i = te.matmul(ps_in[dz % 2][:, :], w_in[kk][:, dz * 128:(dz + 1) * 128],
                                          xln_ch[ch % 2][:, kk, :], start=(kk == 0), stop=(kk == NT_DM - 1))
                        inpe[ig] = p.mark(i, "pe")
                    pech[ch] = p.v("pe")

                @p.blk.scalar
                def _(sc, ch=ch):
                    if ch > 0:
                        p.w(sc, "gq", sps[ch - 1])
                        p.w(sc, "dv", stm[ch - 1])
                    for dz in range(2 * NT_DI):
                        ig = ch * 2 * NT_DI + dz
                        p.w(sc, "pe", inpe[ig])
                        if dz < NT_DI:
                            i = sc.copy(xzx[:, dz, 3:CH + 3], ps_in[dz % 2][:, :])
                        else:
                            i = sc.activation(sz_a[:, dz - NT_DI, :], ps_in[dz % 2][:, :], AF.Silu)
                        fillm[("ig", ig)] = p.mark(i, "ac")
                        if dz < NT_DI:
                            fillm[(ch, dz)] = fillm[("ig", ig)]
                fill_done = p.v("ac")
            else:
                o0 = S - (ch + 1) * CH

                @p.blk.sync
                def _(sync, ch=ch, o0=o0):
                    if ch >= 2:
                        p.w(sync, "dv", fillm[(ch - 2, NT_DI - 1)])
                    p.mark(sync.dma_start(out=ld_x[ch % 2][:], in_=sp_xzx_r[:, :, o0:o0 + CH]), "sq", 16)
                ld_ch = p.v("sq")

                @p.blk.vector
                def _(ve, ch=ch, ld_ch=ld_ch):
                    p.w(ve, "sq", ld_ch)
                    if ch > 0:
                        p.w(ve, "ac", silm[ch * NT_DI - 1])
                        p.w(ve, "gq", sps[ch - 1])
                    for t in range(NT_DI):
                        i = ve.tensor_copy(xzx[:, t, 3:CH + 3], ld_x[ch % 2][:, t, ::-1])
                        fillm[(ch, t)] = p.mark(i, "dv")
                fill_done = p.v("dv")

            @p.blk.vector
            def _(ve, ch=ch, fill_done=fill_done):
                if ch == 0:
                    v = p.mark(ve.memset(xzx[:, :, 0:3], 0.0), "dv")
                else:
                    v = p.mark(ve.tensor_copy(xzx[:, :, 0:3], stash[:]), "dv")
                ve.wait_ge(p.sems["dv"], v)
                p.w(ve, fill_s, fill_done)
                i = ve.tensor_copy(stash[:], xzx[:, :, CH:CH + 3])
                stm[ch] = p.mark(i, "dv")

            if not rev:
                @p.blk.gpsimd
                def _(g, ch=ch, fill_done=fill_done):
                    p.w(g, "ac", fill_done)
                    p.mark(g.dma_start(out=sp_xzx_r[:, :, ch * CH:(ch + 1) * CH],
                                       in_=xzx[:, :, 3:CH + 3]), "gq", 16)
                    spx[ch] = p.v("gq")
                    p.mark(g.dma_start(out=sp_sz_r[:, :, ch * CH:(ch + 1) * CH],
                                       in_=sz_a[:, :, :]), "gq", 16)
                    szsp[ch] = p.v("gq")

            for t in range(NT_DI):
                isil = ch * NT_DI + t

                @p.blk.vector
                def _(ve, ch=ch, t=t, isil=isil):
                    if isil >= 2:
                        p.w(ve, "ac", silm[isil - 2])
                    p.w(ve, fill_s, fillm[(ch, t)])
                    p.w(ve, "dv", stm[ch])
                    cv = cvt[t % 2]
                    ve.tensor_scalar(cv[:], xzx[:, t, 0:CH], convw[:, 4 * t:4 * t + 1], None, OP.mult)
                    for kk in range(1, 4):
                        i = ve.scalar_tensor_tensor(cv[:], xzx[:, t, kk:CH + kk], convw[:, 4 * t + kk:4 * t + kk + 1],
                                                    cv[:], OP.mult, OP.add)
                    cvm[isil] = p.mark(i, "dv")

                @p.blk.scalar
                def _(sc, ch=ch, t=t, isil=isil):
                    p.w(sc, "dv", cvm[isil])
                    if not rev:
                        p.w(sc, "gq", spx[ch])
                    i = sc.activation(xzx[:, t, 3:CH + 3], cvt[t % 2][:], AF.Silu, bias=convb[:, t:t + 1])
                    silm[isil] = p.mark(i, "ac")

            @p.blk.vector
            def _(ve, ch=ch):
                for t in range(NT_DI):
                    p.w(ve, "ac", silm[ch * NT_DI + t])
                    i = ve.tensor_copy(xsil_bf[:, t, :], xzx[:, t, 3:CH + 3])
                    xsbm[(ch, t)] = p.mark(i, "dv")

            @p.blk.gpsimd
            def _(g, ch=ch):
                p.w(g, "ac", silm[ch * NT_DI + NT_DI - 1])
                p.mark(g.dma_start(out=sp_xsil_r[:, :, ch * CH:(ch + 1) * CH],
                                   in_=xzx[:, :, 3:CH + 3]), "gq", 16)
                sps[ch] = p.v("gq")

            @p.blk.tensor
            def _(te, ch=ch):
                p.w(te, "dv", xsbm[(ch, NT_DI - 1)])
                p.w(te, "gq", ld_wxp)
                if ch > 0:
                    p.w(te, "dv", xdm[ch - 1])
                wx = wxp
                for t in range(NT_DI):
                    i = te.matmul(ps_xd[:, :], wx[:, t, :], xsil_bf[:, t, :],
                                  start=(t == 0), stop=(t == NT_DI - 1))
                p.mark(i, "pe")
            pe_xd = p.v("pe")

            @p.blk.vector
            def _(ve, ch=ch, pe_xd=pe_xd):
                p.w(ve, "pe", pe_xd)
                i = ve.tensor_copy(xdbl[:, ch * CH:(ch + 1) * CH], ps_xd[:, :])
                xdm[ch] = p.mark(i, "dv")

        @p.blk.vector
        def _(ve):
            p.w(ve, "dv", xdm[NCH - 1])
            ve.tensor_copy(p.t[f"BCb_{tag}"][:, :], xdbl[0:32, :])
            p.mark(ve.tensor_copy(p.t[f"dtr_{tag}"][:, :], xdbl[64:128, :]), "dv")
    p.barrier()


def _mamba_stageB(p, dr, tag, rev):
    HF = S // 2
    NBK = NT_DI // 2
    with ExitStack() as es:
        dt_b = _sb(p, es, "mb_dt", [128, 2, S])
        dtx_b = _sb(p, es, "mb_dtx", [128, 2, S])
        xs_b = _sb(p, es, "mb_xs", [128, 2, S])
        szb = [_sb(p, es, f"mb_sz{i}", [128, 2, HF]) for i in range(2)]
        Bbc = [_sb(p, es, f"mb_Bbc{i}", [128, HF], BF16) for i in range(2)]
        Cbc = [_sb(p, es, f"mb_Cbc{i}", [128, HF], BF16) for i in range(2)]
        dA = [_sb(p, es, f"mb_dA{i}", [128, HF]) for i in range(2)]
        dp = [_sb(p, es, f"mb_dp{i}", [128, HF]) for i in range(2)]
        pnb = [_sb(p, es, f"mb_pn{i}", [128, HF], BF16) for i in range(2)]
        hsc = _sb(p, es, "mb_h", [128, HF], BF16)
        hl = _sb(p, es, "mb_hl", [128, 32])
        sp_full = _sb(p, es, "mb_spf", [128, S])
        t1 = _sb(p, es, "mb_t1", [128, HF])
        of_t = [_sb(p, es, f"mb_of{i}", [128, HF], BF16) for i in range(2)]
        WdtT = p.t[f"WdtT_{tag}"]
        ind32 = _sb(p, es, "mb_ind32", [32, 32 * 128], BF16)
        ps_y = [_ps(p, es, f"mb_psy{i}", [128, HF]) for i in range(2)]
        ps_dt = [_ps(p, es, f"mb_psdt{i}", [128, CH]) for i in range(2)]
        ps_bc = [_ps(p, es, f"mb_psbc{i}", [128, CH]) for i in range(2)]

        ld_wdt = p.t["ld_wdt"]

        @p.blk.sync
        def _(sync):
            p.mark(sync.dma_start(out=ind32[:], in_=dr["ind32"][:]), "sq", 16)
        ld_ind = p.v("sq")

        Aneg, Dv = p.t[f"Aneg_{tag}"], p.t[f"D_{tag}"]
        bdt = p.t[f"bdt_{tag}"]
        BCb, dtr = p.t[f"BCb_{tag}"], p.t[f"dtr_{tag}"]
        I128 = p.t["I128"]
        sp_xsil_r = dr["sp_xsil"].rearrange("(t p) s -> p t s", p=128)
        sp_sz_r = dr["sp_sz"].rearrange("(t p) s -> p t s", p=128)
        sp_of_r = dr["sp_of" if not rev else "sp_ofb"].rearrange("(t p) s -> p t s", p=128)

        def bcast_mm(te, st, jj, base):
            row = st if jj < 2 else 16 + st
            c0 = base + (jj % 2) * CH
            return te.matmul(ps_bc[jj % 2][:, :], ind32[:, row * 128:(row + 1) * 128],
                             BCb[:, c0:c0 + CH], start=True, stop=True)

        def bc_evac(sc, st, jj):
            dstt = Bbc if jj < 2 else Cbc
            c0 = (jj % 2) * CH
            return sc.copy(dstt[st % 2][:, c0:c0 + CH], ps_bc[jj % 2][:, :])

        bcev = {}
        xlm = {}
        dtln = {}
        dtxm = {}
        ofm = {}
        ofs = {}
        szm = {}
        scm = {}
        pnm = {}
        dbm = {}
        acm = {}
        damk = {}
        dtq_last = None
        ln_guard = {}

        ac_seed = p.v("ac")

        @p.blk.tensor
        def _(te):
            p.w(te, "gq", ld_wdt)
            p.w(te, "sq", ld_ind)
            for jj in range(4):
                if jj >= 2:
                    p.w(te, "ac", ac_seed + jj - 1)
                p.mark(bcast_mm(te, 0, jj, 0), "pe")
        seed_pe0 = p.v("pe") - 4

        @p.blk.scalar
        def _(sc):
            for jj in range(4):
                p.w(sc, "pe", seed_pe0 + jj + 1)
                p.mark(bc_evac(sc, 0, jj), "ac")
        bcev[0] = p.v("ac")
        assert bcev[0] == ac_seed + 4
        prev_e23 = (ac_seed + 3, ac_seed + 4)

        for blk in range(NBK):
            for t4 in range(2):
                t = blk * 2 + t4

                @p.blk.sync
                def _(sync, blk=blk, t4=t4, t=t):
                    if blk > 0:
                        p.w(sync, "dv", ofm[(blk - 1, 1, t4)])
                    p.mark(sync.dma_start(out=xs_b[:, t4, :], in_=sp_xsil_r[:, t, :]), "sq", 16)
                    xlm[(blk, t4)] = p.v("sq")

                pe0_dt = p.v("pe")
                ac0_dt = p.v("ac")

                @p.blk.tensor
                def _(te, blk=blk, t4=t4, t=t, ac0_dt=ac0_dt, dtq_last=dtq_last):
                    for q in range(NCH):
                        if q >= 2:
                            p.w(te, "ac", ac0_dt + q - 1)
                        elif dtq_last is not None:
                            p.w(te, "ac", dtq_last[q])
                        p.mark(te.matmul(ps_dt[q % 2][:, :], WdtT[:, t * 128:(t + 1) * 128],
                                         dtr[:, q * CH:(q + 1) * CH], start=True, stop=True), "pe")

                @p.blk.scalar
                def _(sc, blk=blk, t4=t4, t=t, pe0_dt=pe0_dt):
                    if blk > 0:
                        p.w(sc, "ac", ln_guard[t4])
                    for q in range(NCH):
                        p.w(sc, "pe", pe0_dt + q + 1)
                        i = sc.activation(sp_full[:, q * CH:(q + 1) * CH], ps_dt[q % 2][:, :],
                                          AF.Exp, bias=bdt[:, t:t + 1])
                        p.mark(i, "ac")
                    i = sc.activation(dt_b[:, t4, :], sp_full[:], AF.Ln, bias=1.0)
                    dtln[(blk, t4)] = p.mark(i, "ac")
                assert p.v("ac") == ac0_dt + NCH + 1
                dtq_last = (ac0_dt + 3, ac0_dt + 4)

                @p.blk.vector
                def _(ve, blk=blk, t4=t4):
                    p.w(ve, "ac", dtln[(blk, t4)])
                    p.w(ve, "sq", xlm[(blk, t4)])
                    if blk > 0:
                        p.w(ve, "dv", dbm[(NST - 1, t4)])
                    i = ve.tensor_tensor(dtx_b[:, t4, :], dt_b[:, t4, :], xs_b[:, t4, :], OP.mult)
                    dtxm[(blk, t4)] = p.mark(i, "dv")

            for h in range(2):
                base = h * HF
                fwd_off = base if not rev else S - base - HF

                @p.blk.sync
                def _(sync, blk=blk, h=h, fwd_off=fwd_off):
                    if blk > 0:
                        p.w(sync, "dv", ofm[(blk - 1, h, 1)])
                    for t4 in range(2):
                        t = blk * 2 + t4
                        p.mark(sync.dma_start(out=szb[h][:, t4, :],
                                              in_=sp_sz_r[:, t, fwd_off:fwd_off + HF]), "sq", 16)
                    szm[(blk, h)] = p.v("sq")

                for n in range(NST):
                    nn = (n + 1) % NST
                    last_gi = (blk == NBK - 1 and h == 1 and n == NST - 1)
                    if nn == 0:
                        if h == 0:
                            nbase = HF
                        else:
                            nbase = 0
                    else:
                        nbase = base

                    ac0 = p.v("ac")
                    dv0 = p.v("dv")

                    if not last_gi:
                        @p.blk.tensor
                        def _(te, nn=nn, nbase=nbase, ac0=ac0, prev_e23=prev_e23):
                            for jj in range(4):
                                if jj >= 2:
                                    p.w(te, "ac", ac0 + jj + 1)
                                else:
                                    p.w(te, "ac", prev_e23[jj])
                                p.mark(bcast_mm(te, nn, jj, nbase), "pe")
                    bc_pe0 = p.v("pe") - (0 if last_gi else 4)

                    @p.blk.scalar
                    def _(sc, blk=blk, h=h, n=n, nn=nn, last_gi=last_gi, bc_pe0=bc_pe0):
                        for t4 in range(2):
                            t = blk * 2 + t4
                            p.w(sc, "ac", dtln[(blk, t4)])
                            pk = (n - 1, t4) if n > 0 else (NST - 1, t4)
                            if pk in scm:
                                p.w(sc, "dv", scm[pk])
                            i = sc.activation(dA[t4][:], dt_b[:, t4, base:base + HF], AF.Exp,
                                              scale=Aneg[:, t * NST + n:t * NST + n + 1])
                            damk[(n, t4)] = p.mark(i, "ac")
                        if not last_gi:
                            pk = (n - 1, 1) if n > 0 else (NST - 1, 1)
                            if pk in pnm:
                                p.w(sc, "dv", pnm[pk])
                                p.w(sc, "dv", dbm[pk])
                            for jj in range(4):
                                p.w(sc, "pe", bc_pe0 + jj + 1)
                                p.mark(bc_evac(sc, nn, jj), "ac")
                    if not last_gi:
                        bcev[nn] = ac0 + 6
                        prev_e23 = (ac0 + 5, ac0 + 6)
                    assert p.v("ac") == ac0 + (2 if last_gi else 6)

                    @p.blk.vector
                    def _(ve, blk=blk, h=h, n=n, dv0=dv0):
                        nb = n % 2
                        for t4 in range(2):
                            p.w(ve, "ac", bcev[n])
                            p.w(ve, "ac", damk[(n, t4)])
                            pk = (n - 1, t4) if n > 0 else (NST - 1, t4)
                            if pk in acm:
                                p.w(ve, "pe", acm[pk])
                            i = ve.tensor_tensor(dp[t4][:], dtx_b[:, t4, base:base + HF],
                                                 Bbc[nb][:], OP.mult)
                            dbm[(n, t4)] = p.mark(i, "dv")
                            col = n * 2 + t4
                            init = 0.0 if h == 0 else hl[:, col:col + 1]
                            i = ve.tensor_tensor_scan(hsc[:], dA[t4][:], dp[t4][:], init,
                                                      OP.mult, OP.add)
                            scm[(n, t4)] = p.mark(i, "dv")
                            if h == 0:
                                p.mark(ve.tensor_copy(hl[:, col:col + 1], hsc[:, HF - 1:HF]), "dv")
                            i = ve.tensor_tensor(pnb[t4][:], hsc[:], Cbc[nb][:], OP.mult)
                            pnm[(n, t4)] = p.mark(i, "dv")

                    @p.blk.tensor
                    def _(te, blk=blk, h=h, n=n):
                        for t4 in range(2):
                            p.w(te, "dv", pnm[(n, t4)])
                            if n == 0 and (blk > 0 or h > 0):
                                ph = (blk, h - 1, t4) if h > 0 else (blk - 1, 1, t4)
                                p.w(te, "dv", ofm[ph])
                            for cq in range(2):
                                i = te.matmul(ps_y[t4][:, cq * CH:(cq + 1) * CH], I128[:, :],
                                              pnb[t4][:, cq * CH:(cq + 1) * CH],
                                              start=(n == 0), stop=(n == NST - 1))
                            acm[(n, t4)] = p.mark(i, "pe")

                for t4 in range(2):
                    t = blk * 2 + t4

                    @p.blk.vector
                    def _(ve, blk=blk, h=h, t4=t4, t=t, base=base):
                        p.w(ve, "pe", acm[(NST - 1, t4)])
                        p.w(ve, "sq", szm[(blk, h)])
                        pk = (blk, h - 1, t4) if h > 0 else \
                             ((blk - 1, 1, t4) if blk > 0 else None)
                        if pk is not None:
                            p.w(ve, "po", ofs[pk])
                        i = ve.scalar_tensor_tensor(t1[:], xs_b[:, t4, base:base + HF],
                                                    Dv[:, t:t + 1], ps_y[t4][:, :],
                                                    OP.mult, OP.add)
                        v = p.mark(i, "dv")
                        ve.wait_ge(p.sems["dv"], v)
                        if not rev:
                            i = ve.tensor_tensor(of_t[t4][:], t1[:], szb[h][:, t4, :], OP.mult)
                        else:
                            i = ve.tensor_tensor(of_t[t4][:], t1[:, ::-1], szb[h][:, t4, :], OP.mult)
                        ofm[(blk, h, t4)] = p.mark(i, "dv")

                    @p.blk.scalar
                    def _(ss, blk=blk, h=h, t4=t4, t=t, fwd_off=fwd_off):
                        p.w(ss, "dv", ofm[(blk, h, t4)])
                        p.mark(ss.dma_start(out=sp_of_r[:, t, fwd_off:fwd_off + HF],
                                            in_=of_t[t4][:]), "po", 16)
                        ofs[(blk, h, t4)] = p.v("po")
            for t4 in range(2):
                ln_guard[t4] = damk[(NST - 1, t4)]
    p.barrier()


def _mamba_outproj(p, dr, res_src, res_dst):
    with ExitStack() as es:
        wo_f = [_sb(p, es, f"mo_wf{i}", [128, DM]) for i in range(2)]
        wo_bf = [_sb(p, es, f"mo_wb{k}", [128, DM], BF16) for k in range(NT_DI)]
        of_f = [_sb(p, es, f"mo_off{i}", [128, NT_DI, CH], BF16) for i in range(2)]
        of_b = [_sb(p, es, f"mo_ofb{i}", [128, NT_DI, CH], BF16) for i in range(2)]
        res_t = [_sb(p, es, f"mo_res{k}", [128, S]) for k in range(NT_DM)]
        ps_o = [_ps(p, es, f"mo_ps{i}", [128, CH]) for i in range(2)]

        sp_of_r = dr["sp_of"].rearrange("(t p) s -> p t s", p=128)
        sp_ofb_r = dr["sp_ofb"].rearrange("(t p) s -> p t s", p=128)

        dv0 = p.v("dv")
        sq0 = p.v("sq")

        @p.blk.sync
        def _(sync):
            for k in range(NT_DI):
                if k >= 2:
                    p.w(sync, "dv", dv0 + k - 1)
                p.mark(sync.dma_start(out=wo_f[k % 2][:], in_=dr["W_outT"][k * 128:(k + 1) * 128, :]), "sq", 16)

        @p.blk.vector
        def _(ve):
            for k in range(NT_DI):
                p.w(ve, "sq", sq0 + 16 * (k + 1))
                p.mark(ve.tensor_scalar(wo_bf[k][:], wo_f[k % 2][:], 0.5, None, OP.mult), "dv")
        v_w = p.v("dv")

        pe0 = p.v("pe")
        dvA0 = p.v("dv")
        sq_r0 = p.v("sq")

        @p.blk.sync
        def _(sync):
            for k in range(NT_DM):
                p.mark(sync.dma_start(out=res_t[k][:], in_=res_src[k * 128:(k + 1) * 128, :]), "sq", 16)
            for chn in range(NCH):
                if chn >= 2:
                    p.w(sync, "pe", pe0 + (chn - 2) * NT_DM + NT_DM)
                p.mark(sync.dma_start(out=of_f[chn % 2][:], in_=sp_of_r[:, :, chn * CH:(chn + 1) * CH]), "sq", 16)
                p.mark(sync.dma_start(out=of_b[chn % 2][:], in_=sp_ofb_r[:, :, chn * CH:(chn + 1) * CH]), "sq", 16)

        @p.blk.tensor
        def _(te):
            p.w(te, "dv", v_w)
            for chn in range(NCH):
                p.w(te, "sq", sq_r0 + 16 * NT_DM + 32 * (chn + 1))
                for m in range(NT_DM):
                    fl = chn * NT_DM + m
                    if fl >= 2:
                        p.w(te, "dv", dvA0 + fl - 1)
                    for k in range(NT_DI):
                        te.matmul(ps_o[fl % 2][:, :], wo_bf[k][:, m * 128:(m + 1) * 128],
                                  of_f[chn % 2][:, k, :], start=(k == 0), stop=False)
                    for k in range(NT_DI):
                        i = te.matmul(ps_o[fl % 2][:, :], wo_bf[k][:, m * 128:(m + 1) * 128],
                                      of_b[chn % 2][:, k, :], start=False, stop=(k == NT_DI - 1))
                    p.mark(i, "pe")

        @p.blk.vector
        def _(ve):
            p.w(ve, "sq", sq_r0 + 16 * NT_DM)
            for chn in range(NCH):
                for m in range(NT_DM):
                    fl = chn * NT_DM + m
                    p.w(ve, "pe", pe0 + fl + 1)
                    i = ve.tensor_tensor(res_t[m][:, chn * CH:(chn + 1) * CH],
                                         res_t[m][:, chn * CH:(chn + 1) * CH], ps_o[fl % 2][:, :], OP.add)
                    p.mark(i, "dv")

        @p.blk.sync
        def _(sync):
            for k in range(NT_DM):
                p.w(sync, "dv", dvA0 + (NCH - 1) * NT_DM + k + 1)
                p.mark(sync.dma_start(out=res_dst[k * 128:(k + 1) * 128, :], in_=res_t[k][:]), "sq", 16)
    p.barrier()


def _ffn(p, dr, res_src, res_dst):
    p.barrier()
    with ExitStack() as es:
        mW1T = _sb(p, es, "ff_w1", [128, NT_DM, FF], BF16)
        xln_ch = _sb(p, es, "ff_xln", [128, NT_DM, CH], BF16)
        g_sb = _sb(p, es, "ff_g", [NE, 1])
        g_aug = _sb(p, es, "ff_gaug", [16, 1])
        g_rf = _sb(p, es, "ff_grf", [16, 512])
        g_hi = _sb(p, es, "ff_ghi", [16, 512], BF16)
        g_lo4 = _sb(p, es, "ff_glo4", [16, 512])
        g_lo = _sb(p, es, "ff_glo", [16, 512], BF16)
        g_pair = _sb(p, es, "ff_gpair", [128, 4])
        g_bc = _sb(p, es, "ff_gbc", [128, 9])
        TbT1 = _sb(p, es, "ff_tbt1", [128, 9, NT_F])
        TbT2 = _sb(p, es, "ff_tbt2", [128, 9, NT_DM])
        lhsp = _sb(p, es, "ff_lhsp", [128, 4, 64], BF16)
        mb1c = _sb(p, es, "ff_mb1c", [128, NT_F])
        mb2c = _sb(p, es, "ff_mb2c", [128, NT_DM])
        sp_xln_r = dr["sp_xln"].rearrange("(k p) s -> p k s", p=128)
        I64t2, I64, indp = p.t["I64t2"], p.t["I64"], p.t["indp"]

        ring1 = ExitStack()
        twb1 = _sb(p, ring1, "ff_twb1", [128, 6, FF], BF16)

        with ExitStack() as ges:
            gw1 = _sb(p, ges, "ff_gw1", [128, NT_DM, DM], BF16)
            gw2 = _sb(p, ges, "ff_gw2", [128, NT_DM, NE], BF16)
            hg_ch = _sb(p, ges, "ff_hg", [128, NT_DM, CH], BF16)
            gsum = _sb(p, ges, "ff_gsum", [NE, 1])
            ps_g1 = [_ps(p, ges, f"ff_psg{i}", [128, CH]) for i in range(2)]
            ps_g2 = _ps(p, ges, "ff_psg2", [NE, S])

            @p.blk.gpsimd
            def _(g):
                p.mark(g.dma_start(out=gw1[:], in_=dr["gw1T"].rearrange("(k p) d -> p k d", p=128)), "gq", 16)
                p.mark(g.dma_start(out=gw2[:], in_=dr["gw2T"].rearrange("(k p) e -> p k e", p=128)), "gq", 16)
            ld_gw = p.v("gq")

            g1pe = {}
            g1ac = {}
            g2pe = {}
            for ch in range(NCH):
                @p.blk.sync
                def _(sync, ch=ch):
                    if ch > 0:
                        p.w(sync, "pe", g2pe[ch - 1])
                    p.mark(sync.dma_start(out=xln_ch[:], in_=sp_xln_r[:, :, ch * CH:(ch + 1) * CH]), "sq", 16)
                ldx = p.v("sq")

                ac0_ch = p.v("ac")
                for o in range(NT_DM):
                    @p.blk.tensor
                    def _(te, ch=ch, o=o, ldx=ldx, ac0_ch=ac0_ch):
                        if o == 0:
                            p.w(te, "sq", ldx)
                            if ch == 0:
                                p.w(te, "gq", ld_gw)
                        if o >= 2:
                            p.w(te, "ac", ac0_ch + o - 1)
                        elif ch > 0:
                            p.w(te, "ac", g1ac[(ch - 1, NT_DM - 2 + o)])
                        for k in range(NT_DM):
                            i = te.matmul(ps_g1[o % 2][:, :], gw1[:, k, o * 128:(o + 1) * 128],
                                          xln_ch[:, k, :], start=(k == 0), stop=(k == NT_DM - 1))
                        g1pe[(ch, o)] = p.mark(i, "pe")

                @p.blk.scalar
                def _(sc, ch=ch):
                    for o in range(NT_DM):
                        p.w(sc, "pe", g1pe[(ch, o)])
                        i = sc.activation(hg_ch[:, o, :], ps_g1[o % 2][:, :],
                                          AF.Relu, bias=p.t["gb1"][:, o:o + 1])
                        g1ac[(ch, o)] = p.mark(i, "ac")
                assert p.v("ac") == ac0_ch + NT_DM

                if _DEBUG_FFN and ch == 0:
                    @p.blk.gpsimd
                    def _(g):
                        p.w(g, "ac", g1ac[(0, NT_DM - 1)])
                        p.mark(g.dma_start(out=dr["tq_hg"][:], in_=hg_ch[:].rearrange("p k c -> p (k c)")), "gq", 16)
                    hgtap = p.v("gq")

                    @p.blk.sync
                    def _(sync):
                        p.w(sync, "gq", hgtap)

                @p.blk.tensor
                def _(te, ch=ch):
                    p.w(te, "ac", g1ac[(ch, NT_DM - 1)])
                    for k in range(NT_DM):
                        i = te.matmul(ps_g2[:, ch * CH:(ch + 1) * CH], gw2[:, k, :],
                                      hg_ch[:, k, :], start=(k == 0), stop=(k == NT_DM - 1))
                    g2pe[ch] = p.mark(i, "pe")

            @p.blk.vector
            def _(ve):
                p.w(ve, "pe", g2pe[NCH - 1])
                v = p.mark(ve.tensor_reduce(gsum[:], ps_g2[:], AX.X, OP.add), "dv")
                ve.wait_ge(p.sems["dv"], v)
                p.mark(ve.tensor_scalar(g_sb[:], gsum[:], 1.0 / S, p.t["gb2"][:], OP.mult, OP.add), "dv")
            v_g = p.v("dv")
            gate_pe_end = p.v("pe")

        if _FFN_BARRIERS[0]:
            p.barrier()
        @p.blk.vector
        def _(ve):
            ve.memset(g_rf[:], 1.0)
            p.mark(ve.memset(g_aug[:], 1.0), "dv")
        v_ga0 = p.v("dv")

        @p.blk.gpsimd
        def _(g):
            p.w(g, "dv", v_g)
            p.w(g, "dv", v_ga0)
            p.mark(g.dma_start(out=g_aug[1:9, :], in_=g_sb[0:8, :]), "gq", 16)
        gq_ga = p.v("gq")

        @p.blk.vector
        def _(ve):
            p.w(ve, "gq", gq_ga)
            v = p.mark(ve.tensor_scalar(g_rf[:], g_rf[:], g_aug[:, 0:1], None, OP.mult), "dv")
            ve.wait_ge(p.sems["dv"], v)
            v = p.mark(ve.tensor_copy(g_hi[:], g_rf[:]), "dv")
            ve.wait_ge(p.sems["dv"], v)
            v = p.mark(ve.tensor_tensor(g_lo4[:], g_rf[:], g_hi[:], OP.subtract), "dv")
            ve.wait_ge(p.sems["dv"], v)
            p.mark(ve.tensor_copy(g_lo[:], g_lo4[:]), "dv")
        v_ghl = p.v("dv")

        @p.blk.sync
        def _(sync):
            p.mark(sync.dma_start(out=TbT1[:], in_=dr["TbT1"][:]), "sq", 16)
            p.mark(sync.dma_start(out=TbT2[:], in_=dr["TbT2"][:]), "sq", 16)
        ld_tb = p.v("sq")

        with ExitStack() as pes:
            ps_gp = _ps(p, pes, "ff_psgp", [128, 512])
            indsel = p.t["indsel"]

            @p.blk.tensor
            def _(te):
                p.w(te, "dv", v_ghl)
                for j in range(4):
                    te.matmul(ps_gp[:, j:j + 1], indp[:, j * 128:(j + 1) * 128],
                              g_hi[0:9, 0:1], start=True, stop=False)
                    te.matmul(ps_gp[:, j:j + 1], indp[:, j * 128:(j + 1) * 128],
                              g_lo[0:9, 0:1], start=False, stop=True)
                for e in range(9):
                    te.matmul(ps_gp[:, 4 + e:5 + e], indsel[:, e * 128:(e + 1) * 128],
                              g_hi[0:9, 0:1], start=True, stop=False)
                    i = te.matmul(ps_gp[:, 4 + e:5 + e], indsel[:, e * 128:(e + 1) * 128],
                                  g_lo[0:9, 0:1], start=False, stop=True)
                p.mark(i, "pe")
            gp_pe_end = p.v("pe")

            @p.blk.scalar
            def _(sc):
                p.w(sc, "pe", gp_pe_end)
                sc.copy(g_pair[:, :], ps_gp[:, 0:4])
                p.mark(sc.copy(g_bc[:, :], ps_gp[:, 4:13]), "ac")
            a_gp = p.v("ac")

        @p.blk.vector
        def _(ve):
            p.w(ve, "ac", a_gp)
            for j in range(4):
                i = ve.tensor_scalar(lhsp[:, j, :], I64t2[:], g_pair[:, j:j + 1], None, OP.mult)
            p.mark(i, "dv")
        v_lh = p.v("dv")

        @p.blk.vector
        def _(ve):
            p.w(ve, "ac", a_gp)
            p.w(ve, "sq", ld_tb)
            ve.tensor_scalar(mb1c[:], TbT1[:, 0, :], g_bc[:, 0:1], None, OP.mult)
            for e in range(1, 9):
                ve.scalar_tensor_tensor(mb1c[:], TbT1[:, e, :], g_bc[:, e:e + 1],
                                        mb1c[:], OP.mult, OP.add)
            ve.tensor_scalar(mb2c[:], TbT2[:, 0, :], g_bc[:, 0:1], None, OP.mult)
            for e in range(1, 9):
                i = ve.scalar_tensor_tensor(mb2c[:], TbT2[:, e, :], g_bc[:, e:e + 1],
                                            mb2c[:], OP.mult, OP.add)
            p.mark(i, "dv")
        mb_done = p.v("dv")
        pre_build = {"pe": gp_pe_end, "dv": v_lh, "ac": a_gp}

        def build(which, pre, twb, R):
            nstrip, W = (16, FF) if which == 1 else (64, DM)
            tw_r = dr["TW1bf" if which == 1 else "TW2bf"].rearrange(
                "s (j p) f -> s p j f", p=128)
            conv_done = p.t["conv1" if which == 1 else "conv2"]
            w_name = "W1_strips" if which == 1 else "W2_strips"
            dst = mW1T if which == 1 else mW2T
            hw_ = W // 2
            nck = hw_ // 512
            with ExitStack() as bes:
                ps_b = _ps(p, bes, f"ff_psb{which}", [64, W])
                cons = {}
                evac = {}
                ldm = {}
                for s in range(nstrip):
                    for j in range(5):
                        pi = s * 5 + j
                        slot = pi % R
                        if j == 4:
                            eng, sem = p.blk.gpsimd, "gq"
                        else:
                            eng = p.blk.gpsimd if pi % 2 == 0 else p.blk.sync
                            sem = "gq" if pi % 2 == 0 else "sq"

                        @eng
                        def _(g, s=s, j=j, pi=pi, slot=slot, sem=sem):
                            if pi <= 1:
                                p.w(g, "gq", conv_done)
                            if pi >= R:
                                p.w(g, "pe", cons[((pi - R) // 5, 1)])
                            if j < 4:
                                p.mark(g.dma_start(out=twb[:, slot, :], in_=tw_r[s, :, j, :]), sem, 16)
                            else:
                                p.mark(g.dma_start(out=twb[0:64, slot, :], in_=dr[w_name][s]), sem, 16)
                            ldm[pi] = (sem, p.v(sem))

                    for h in range(2):
                        @p.blk.tensor
                        def _(te, s=s, h=h):
                            if s == 0 and h == 0:
                                for sem, v in pre.items():
                                    p.w(te, sem, v)
                                p.w(te, "dv", v_lh)
                            if h == 0:
                                gmax = {}
                                for j in range(5):
                                    sm, vv = ldm[s * 5 + j]
                                    gmax[sm] = max(gmax.get(sm, 0), vv)
                                for sm, vv in gmax.items():
                                    p.w(te, sm, vv)
                            if s >= 1:
                                p.w(te, "dv", evac[(s - 1, h)])
                            for j in range(5):
                                slot = (s * 5 + j) % R
                                for c in range(nck):
                                    col = h * hw_ + c * 512
                                    if j < 4:
                                        i = te.matmul(ps_b[:, col:col + 512], lhsp[:, j, :],
                                                      twb[:, slot, col:col + 512],
                                                      start=(j == 0), stop=False)
                                    else:
                                        i = te.matmul(ps_b[:, col:col + 512], I64[:, :],
                                                      twb[0:64, slot, col:col + 512],
                                                      start=False, stop=True)
                            cons[(s, h)] = p.mark(i, "pe")

                        @p.blk.vector
                        def _(ve, s=s, h=h):
                            p.w(ve, "pe", cons[(s, h)])
                            t_i, pb = s // 2, (s % 2) * 64
                            for c in range(nck):
                                col = h * hw_ + c * 512
                                i = ve.tensor_copy(dst[pb:pb + 64, t_i, col:col + 512],
                                                   ps_b[:, col:col + 512])
                            evac[(s, h)] = p.mark(i, "dv")
                return evac[(nstrip - 1, 1)], {"pe": p.v("pe"), "dv": p.v("dv"),
                                               "ac": p.v("ac"), "gq": p.v("gq"), "sq": p.v("sq")}

        ev1, post1 = build(1, pre_build, twb1, 6)
        ring1.close()
        mW2T = _sb(p, es, "ff_w2", [128, NT_F, DM], BF16)
        twb2 = _sb(p, es, "ff_twb2", [128, 10, DM], BF16)
        ev2, post2 = build(2, post1, twb2, 10)
        if _FFN_BARRIERS[3]:
            p.barrier()
        if _DEBUG_FFN:
            @p.blk.gpsimd
            def _(g):
                p.w(g, "ac", ev2)
                p.mark(g.dma_start(out=dr["tq_w1"][:], in_=mW1T[0][:]), "gq", 16)
                p.mark(g.dma_start(out=dr["tq_w2"][:], in_=mW2T[0][:]), "gq", 16)
                for k in range(NT_DM):
                    p.mark(g.dma_start(out=dr["tq_w1f"][k * 128:(k + 1) * 128, :], in_=mW1T[k][:]), "gq", 16)
                for k in range(NT_F):
                    p.mark(g.dma_start(out=dr["tq_w2f"][k * 128:(k + 1) * 128, :], in_=mW2T[k][:]), "gq", 16)

        p.barrier()
        UC = 256
        with ExitStack() as ues:
            u_sb = _sb(p, ues, "ff_u", [128, NT_F, UC], BF16)
            res_c = _sb(p, ues, "ff_res", [128, NT_DM, UC])
            xln_u = _sb(p, ues, "ff_xlnu", [128, NT_DM, UC], BF16)
            ps_u = [_ps(p, ues, f"ff_psu{i}", [128, UC]) for i in range(4)]
            ps_d = [_ps(p, ues, f"ff_psd{i}", [128, UC]) for i in range(4)]
            res_r = res_src.rearrange("(k p) s -> p k s", p=128)
            dst_r = res_dst.rearrange("(k p) s -> p k s", p=128)
            upe = {}
            uac = {}
            dpe = {}
            adddv = {}
            dnlast = {}
            for c in range(S // UC):
                @p.blk.sync
                def _(sync, c=c):
                    if c > 0:
                        p.w(sync, "pe", dnlast[c - 1])
                        p.w(sync, "dv", adddv[(c - 1, NT_DM - 1)])
                    p.mark(sync.dma_start(out=xln_u[:], in_=sp_xln_r[:, :, c * UC:(c + 1) * UC]), "sq", 16)
                    p.mark(sync.dma_start(out=res_c[:], in_=res_r[:, :, c * UC:(c + 1) * UC]), "sq", 16)
                ld_u = p.v("sq")

                ac0_c = p.v("ac")
                for fg in range(NT_F // 4):
                    @p.blk.tensor
                    def _(te, c=c, fg=fg, ld_u=ld_u, ac0_c=ac0_c):
                        if fg == 0:
                            p.w(te, "sq", ld_u)
                            if c == 0:
                                p.w(te, "dv", ev1)
                            elif c > 0:
                                p.w(te, "ac", uac[(c - 1, NT_F - 1)])
                        else:
                            p.w(te, "ac", ac0_c + fg * 4)
                        for ft in range(fg * 4, fg * 4 + 4):
                            for k in range(NT_DM):
                                i = te.matmul(ps_u[ft % 4][:, :],
                                              mW1T[:, k, ft * 128:(ft + 1) * 128],
                                              xln_u[:, k, :], start=(k == 0), stop=(k == NT_DM - 1))
                            upe[(c, ft)] = p.mark(i, "pe")

                dv0_c = p.v("dv")
                for mg in range(NT_DM // 4):
                    @p.blk.tensor
                    def _(te, c=c, mg=mg, ac0_c=ac0_c, dv0_c=dv0_c):
                        if mg == 0:
                            p.w(te, "ac", ac0_c + NT_F)
                            if c == 0:
                                p.w(te, "dv", ev2)
                            elif c > 0:
                                p.w(te, "dv", adddv[(c - 1, NT_DM - 1)])
                        else:
                            p.w(te, "dv", dv0_c + mg * 4)
                        for m in range(mg * 4, mg * 4 + 4):
                            for k in range(NT_F):
                                i = te.matmul(ps_d[m % 4][:, :], mW2T[:, k, m * 128:(m + 1) * 128],
                                              u_sb[:, k, :], start=(k == 0), stop=(k == NT_F - 1))
                            dpe[(c, m)] = p.mark(i, "pe")
                dnlast[c] = dpe[(c, NT_DM - 1)]

                @p.blk.scalar
                def _(sc, c=c):
                    if c > 0:
                        p.w(sc, "pe", dnlast[c - 1])
                    else:
                        p.w(sc, "dv", mb_done)
                    for ft in range(NT_F):
                        p.w(sc, "pe", upe[(c, ft)])
                        i = sc.activation(u_sb[:, ft, :], ps_u[ft % 4][:, :],
                                          AF.Gelu_apprx_tanh, bias=mb1c[:, ft:ft + 1])
                        uac[(c, ft)] = p.mark(i, "ac")
                assert p.v("ac") == ac0_c + NT_F

                @p.blk.vector
                def _(ve, c=c, ld_u=ld_u):
                    for m in range(NT_DM):
                        p.w(ve, "pe", dpe[(c, m)])
                        if m == 0:
                            p.w(ve, "sq", ld_u)
                        i = ve.scalar_tensor_tensor(res_c[:, m, :], ps_d[m % 4][:, :],
                                                    mb2c[:, m:m + 1], res_c[:, m, :],
                                                    OP.add, OP.add)
                        adddv[(c, m)] = p.mark(i, "dv")
                assert p.v("dv") == dv0_c + NT_DM

                @p.blk.sync
                def _(sync, c=c):
                    p.w(sync, "dv", adddv[(c, NT_DM - 1)])
                    p.mark(sync.dma_start(out=dst_r[:, :, c * UC:(c + 1) * UC], in_=res_c[:]), "sq", 16)
    p.barrier()


def _attn(p, dr, res_src, res_dst):
    p.barrier()
    with ExitStack() as es:
        QT = _sb(p, es, "at_qt", [128, NT_DM, S], BF16)
        KT = _sb(p, es, "at_kt", [128, NT_DM, S], BF16)
        VT = _sb(p, es, "at_vt", [128, 16, DM], BF16)
        bvbc = _sb(p, es, "at_bvbc", [128, DM])
        sp_xln_r = dr["sp_xln"].rearrange("(k p) s -> p k s", p=128)
        res_r = res_src.rearrange("(k p) s -> p k s", p=128)
        dst_r = res_dst.rearrange("(k p) s -> p k s", p=128)
        I128, ones1 = p.t["I128"], p.t["ones1"]

        with ExitStack() as bes:
            @p.blk.sync
            def _(sync):
                p.mark(sync.dma_start(out=bvbc[:], in_=bass.AP(dr["bv_r"].tensor, 0, [[0, 128], [1, DM]])), "sq", 16)
            a_bv = ("sq", p.v("sq"))

            wq = [_sb(p, bes, f"at_w{i}", [128, NT_DM, DM], BF16) for i in range(2)]
            xln_ch = _sb(p, bes, "at_xln", [128, NT_DM, CH], BF16)
            ps_q = [_ps(p, bes, f"at_psq{i}", [128, CH]) for i in range(2)]

            @p.blk.gpsimd
            def _(g):
                p.mark(g.dma_start(out=wq[0][:], in_=dr["W_q"].rearrange("(k p) d -> p k d", p=128)), "gq", 16)
                p.mark(g.dma_start(out=wq[1][:], in_=dr["W_k"].rearrange("(k p) d -> p k d", p=128)), "gq", 16)
            ld_wq = p.v("gq")
            ld_wv = {}

            prpe = {}
            prac = {}
            for proj in range(3):
                wt = wq[proj % 2]
                for ch in range(NCH):
                    @p.blk.sync
                    def _(sync, proj=proj, ch=ch):
                        if proj > 0 or ch > 0:
                            prev = prpe.get((proj, ch - 1)) if ch > 0 else prpe[(proj - 1, NCH - 1)]
                            p.w(sync, "pe", prev)
                        p.mark(sync.dma_start(out=xln_ch[:], in_=sp_xln_r[:, :, ch * CH:(ch + 1) * CH]), "sq", 16)
                    ldx = p.v("sq")

                    for o in range(NT_DM):
                        @p.blk.tensor
                        def _(te, proj=proj, ch=ch, o=o, ldx=ldx, wt=wt):
                            if o == 0:
                                p.w(te, "sq", ldx)
                                if proj == 0 and ch == 0:
                                    p.w(te, "gq", ld_wq)
                                if proj == 2 and ch == 0:
                                    p.w(te, "gq", ld_wv[0])
                            if (ch * NT_DM + o) >= 2 or proj > 0:
                                prv = prac.get((proj, ch, o - 2)) or prac.get((proj, ch - 1, NT_DM - 2 + o)) \
                                    or (prac[(proj - 1, NCH - 1, NT_DM - 2 + o)] if proj > 0 and ch == 0 and o < 2 else None)
                                if prv is not None:
                                    p.w(te, prv[0], prv[1])
                            if proj < 2:
                                for k in range(NT_DM):
                                    i = te.matmul(ps_q[o % 2][:, :], wt[:, k, o * 128:(o + 1) * 128],
                                                  xln_ch[:, k, :], start=(k == 0), stop=(k == NT_DM - 1))
                            else:
                                for k in range(NT_DM):
                                    i = te.matmul(ps_q[o % 2][:, :],
                                                  xln_ch[:, k, (o % 4) * 128:(o % 4) * 128 + 128],
                                                  wt[:, k, (o // 4) * 512:(o // 4) * 512 + 512],
                                                  start=(k == 0), stop=(k == NT_DM - 1))
                            prpe[(proj, ch)] = p.mark(i, "pe")

                        if proj < 2:
                            @p.blk.scalar
                            def _(sc, proj=proj, ch=ch, o=o):
                                p.w(sc, "pe", prpe[(proj, ch)])
                                dst = QT if proj == 0 else KT
                                bias = p.t["bq"] if proj == 0 else p.t["bk"]
                                i = sc.activation(dst[:, o, ch * CH:(ch + 1) * CH], ps_q[o % 2][:, :],
                                                  AF.Identity, bias=bias[:, o:o + 1])
                                prac[(proj, ch, o)] = ("ac", p.mark(i, "ac"))
                        else:
                            @p.blk.vector
                            def _(ve, proj=proj, ch=ch, o=o):
                                p.w(ve, "pe", prpe[(proj, ch)])
                                p.w(ve, a_bv[0], a_bv[1])
                                ssub = ch * 4 + (o % 4)
                                dvc = o // 4
                                i = ve.tensor_tensor(VT[:, ssub, dvc * 512:(dvc + 1) * 512],
                                                     ps_q[o % 2][:, :], bvbc[:, dvc * 512:(dvc + 1) * 512],
                                                     OP.add)
                                prac[(proj, ch, o)] = ("dv", p.mark(i, "dv"))

                if proj == 0:
                    @p.blk.gpsimd
                    def _(g):
                        p.w(g, "pe", prpe[(0, NCH - 1)])
                        p.mark(g.dma_start(out=wq[0][:], in_=dr["W_v"].rearrange("(k p) d -> p k d", p=128)), "gq", 16)
                        ld_wv[0] = p.v("gq")

        with ExitStack() as s2:
          PT = _sb(p, s2, "at_pt", [128, 16, S], BF16)
          with ExitStack() as ses:
            Ssb = _sb(p, ses, "at_ssb", [128, S])
            Pf = _sb(p, ses, "at_pf", [128, S])
            nm32 = _sb(p, ses, "at_nm", [128, 1])
            rs = _sb(p, ses, "at_rs", [128, 1])
            rsr = _sb(p, ses, "at_rsr", [128, 1])
            Pbf = _sb(p, ses, "at_pbf", [128, S], BF16)
            ps_sc = _ps(p, ses, "at_pssc", [128, S])
            ps_tr = _ps(p, ses, "at_pstr", [128, 128], BF16)
            scpe = {}
            smdv = {}
            smac = {}
            trdv = {}
            qk_done = prac[(1, NCH - 1, NT_DM - 1)][1]
            evsc = {}
            for sq_t in range(16):
                @p.blk.tensor
                def _(te, sq_t=sq_t):
                    if sq_t == 0:
                        p.w(te, "ac", qk_done)
                        p.w(te, "dv", prac[(2, NCH - 1, NT_DM - 1)][1])
                    if sq_t > 0:
                        p.w(te, "ac", evsc[sq_t - 1])
                    for ch in range(NCH):
                        for k in range(NT_DM):
                            i = te.matmul(ps_sc[:, ch * CH:(ch + 1) * CH],
                                          QT[:, k, sq_t * 128:(sq_t + 1) * 128],
                                          KT[:, k, ch * CH:(ch + 1) * CH],
                                          start=(k == 0), stop=(k == NT_DM - 1))
                    scpe[sq_t] = p.mark(i, "pe")

                @p.blk.scalar
                def _(sc, sq_t=sq_t):
                    p.w(sc, "pe", scpe[sq_t])
                    if sq_t > 0:
                        p.w(sc, "ac", smac[sq_t - 1])
                    evsc[sq_t] = p.mark(sc.copy(Ssb[:], ps_sc[:]), "ac")

                @p.blk.vector
                def _(ve, sq_t=sq_t):
                    p.w(ve, "ac", evsc[sq_t])
                    v = p.mark(ve.tensor_reduce(nm32[:], Ssb[:], AX.X, OP.max, negate=True), "dv")
                    ve.wait_ge(p.sems["dv"], v)
                    p.mark(ve.tensor_scalar(nm32[:], nm32[:], 0.03125, None, OP.mult), "dv")
                smdv[sq_t] = p.v("dv")

                @p.blk.scalar
                def _(sc, sq_t=sq_t):
                    p.w(sc, "dv", smdv[sq_t])
                    i = sc.activation(Pf[:], Ssb[:], AF.Exp, scale=0.03125, bias=nm32[:],
                                      accum_out=rs[:])
                    smac[sq_t] = p.mark(i, "ac")

                @p.blk.vector
                def _(ve, sq_t=sq_t):
                    p.w(ve, "ac", smac[sq_t])
                    v = p.mark(ve.reciprocal(rsr[:], rs[:]), "dv")
                    ve.wait_ge(p.sems["dv"], v)
                    p.mark(ve.tensor_scalar(Pbf[:], Pf[:], rsr[:], None, OP.mult), "dv")
                v_pn = p.v("dv")

                if _DEBUG_ATT and sq_t == 0:
                    @p.blk.gpsimd
                    def _(g):
                        p.w(g, "dv", v_pn)
                        p.mark(g.dma_start(out=dr["ta_nm"][:], in_=nm32[:]), "gq", 16)
                        p.mark(g.dma_start(out=dr["ta_rs"][:], in_=rs[:]), "gq", 16)
                        p.mark(g.dma_start(out=dr["ta_rsr"][:], in_=rsr[:]), "gq", 16)
                        p.mark(g.dma_start(out=dr["ta_pbf"][:], in_=Pbf[:]), "gq", 16)
                    tapA = p.v("gq")

                    @p.blk.vector
                    def _(ve, tapA=tapA):
                        p.w(ve, "gq", tapA)

                for k in range(16):
                    @p.blk.tensor
                    def _(te, sq_t=sq_t, k=k, v_pn=v_pn):
                        if k == 0:
                            p.w(te, "dv", v_pn)
                        if (sq_t, k) != (0, 0):
                            prev = trdv.get((sq_t, k - 1)) or trdv[(sq_t - 1, 15)]
                            p.w(te, "dv", prev)
                        p.mark(te.transpose(ps_tr[:], Pbf[:, k * 128:(k + 1) * 128], I128[:]), "pe")

                    @p.blk.vector
                    def _(ve, sq_t=sq_t, k=k):
                        p.w(ve, "pe", p.v("pe"))
                        i = ve.tensor_copy(PT[:, k, sq_t * 128:(sq_t + 1) * 128], ps_tr[:])
                        trdv[(sq_t, k)] = p.mark(i, "dv")

          if _DEBUG_ATT:
            @p.blk.gpsimd
            def _(g):
                p.w(g, "dv", trdv[(15, 15)])
                p.mark(g.dma_start(out=dr["ta_pt"][:], in_=PT[:, 0, 0:128]), "gq", 16)
                p.mark(g.dma_start(out=dr["ta_qt"][:], in_=QT[:, 0, :]), "gq", 16)
                p.mark(g.dma_start(out=dr["ta_kt"][:], in_=KT[:, 0, :]), "gq", 16)
                p.mark(g.dma_start(out=dr["ta_vt"][:], in_=VT[:, 0, :]), "gq", 16)
          p.barrier()
          with ExitStack() as aes:
            res_c = _sb(p, aes, "at_res", [128, NT_DM, CH])
            tda = [_sb(p, aes, f"at_td{i}", [128, CH]) for i in range(2)]
            ps_av = [_ps(p, aes, f"at_psav{i}", [128, CH]) for i in range(2)]
            avpe = {}
            avac = {}
            avdv = {}
            for ch in range(NCH):
                @p.blk.sync
                def _(sync, ch=ch):
                    if ch > 0:
                        p.w(sync, "dv", avdv[(ch - 1, NT_DM - 1)])
                    p.mark(sync.dma_start(out=res_c[:], in_=res_r[:, :, ch * CH:(ch + 1) * CH]), "sq", 16)
                ld_r = p.v("sq")

                for m in range(NT_DM):
                    @p.blk.tensor
                    def _(te, ch=ch, m=m, ld_r=ld_r):
                        fl = ch * NT_DM + m
                        if fl >= 2:
                            p.w(te, "ac", avac[(ch, m - 2)] if m >= 2 else avac[(ch - 1, NT_DM - 2 + m)])
                        if fl == 0:
                            p.w(te, "dv", trdv[(15, 15)])
                            p.w(te, "ac", smac[15])
                            p.w(te, "dv", prac[(2, NCH - 1, NT_DM - 1)][1])
                        for k in range(16):
                            i = te.matmul(ps_av[m % 2][:, :], VT[:, k, m * 128:(m + 1) * 128],
                                          PT[:, k, ch * CH:(ch + 1) * CH],
                                          start=(k == 0), stop=(k == 15))
                        avpe[(ch, m)] = p.mark(i, "pe")

                    @p.blk.scalar
                    def _(sc, ch=ch, m=m):
                        p.w(sc, "pe", avpe[(ch, m)])
                        if (ch, m) >= (0, 2):
                            prv = avdv.get((ch, m - 2)) if m >= 2 else avdv.get((ch - 1, NT_DM - 2 + m))
                            p.w(sc, "dv", prv)
                        i = sc.copy(tda[m % 2][:], ps_av[m % 2][:])
                        avac[(ch, m)] = p.mark(i, "ac")

                    @p.blk.vector
                    def _(ve, ch=ch, m=m, ld_r=ld_r):
                        p.w(ve, "ac", avac[(ch, m)])
                        if m == 0:
                            p.w(ve, "sq", ld_r)
                        i = ve.tensor_tensor(res_c[:, m, :], res_c[:, m, :], tda[m % 2][:], OP.add)
                        avdv[(ch, m)] = p.mark(i, "dv")

                @p.blk.sync
                def _(sync, ch=ch):
                    p.w(sync, "dv", avdv[(ch, NT_DM - 1)])
                    p.mark(sync.dma_start(out=dst_r[:, :, ch * CH:(ch + 1) * CH], in_=res_c[:]), "sq", 16)
    p.barrier()


def kernel(**inputs):
    shared, xs = _host_prep(inputs)
    nc = _build_program()
    in_maps = []
    for b in range(NCORE):
        m = dict(shared)
        m["x_in"] = xs[b]
        in_maps.append(m)
    res = run_bass_kernel_spmd(nc, in_maps, core_ids=list(range(NCORE)))
    out = np.stack([np.asarray(r["out"], np.float32).T for r in res.results], 0)
    return np.ascontiguousarray(out)


# version 11
# speedup vs baseline: 1.0349x; 1.0297x over previous
import numpy as np
import ml_dtypes
from contextlib import ExitStack

import concourse.bass as bass
import concourse.mybir as mybir
from concourse.bass_utils import run_bass_kernel_spmd

F32 = mybir.dt.float32
BF16 = mybir.dt.bfloat16
DR = mybir.MatmulPerfMode.DoubleRow
OP = mybir.AluOpType
AF = mybir.ActivationFunctionType
AX = mybir.AxisListType

DM = 1024
S = 2048
DI = 2048
NST = 16
DTR = 64
FF = 4096
NE = 8
NCORE = 8
CH = 512
NCH = S // CH
NT_DM = DM // 128
NT_DI = DI // 128
NT_F = FF // 128

_N_SUB = 4
_DEBUG_TAPS = False
_DEBUG_FFN = False
_FFN_BARRIERS = (0, 0, 0, 0, 0)
_FFN_PARANOID = (0, 0, 0)
_DEBUG_ATT = False

_FCONSTS = [1e-5, -0.5, 0.03125, 1.0, 0.0, 0.5, -1.0]


def _host_prep(inp):
    d = {}
    f32 = np.float32

    def c(x):
        return np.ascontiguousarray(x, dtype=f32)

    d["W_inT"] = c(inp["L0_attn_a_W_in"].T)
    for sfx, tag in (("", "f"), ("_b", "b")):
        cw = np.asarray(inp[f"L0_attn_a_conv_w{sfx}"], f32)[:, 0, :]
        d[f"convw_{tag}"] = c(cw.reshape(NT_DI, 128, 4).transpose(1, 0, 2).reshape(128, NT_DI * 4))
        d[f"convb_{tag}"] = c(np.asarray(inp[f"L0_attn_a_conv_b{sfx}"], f32).reshape(NT_DI, 128).T)
        wx = np.asarray(inp[f"L0_attn_a_Wx{sfx}"], f32)
        wxp = np.zeros((DI, 128), f32)
        wxp[:, 0:16] = wx[DTR:DTR + 16].T
        wxp[:, 16:32] = wx[DTR + 16:DTR + 32].T
        wxp[:, 64:128] = wx[0:DTR].T
        d[f"WxTp_{tag}"] = c(wxp)
        d[f"WdtT_{tag}"] = c(np.asarray(inp[f"L0_attn_a_Wdt{sfx}"], f32).T)
        d[f"bdt_{tag}"] = c(np.asarray(inp[f"L0_attn_a_bdt{sfx}"], f32).reshape(NT_DI, 128).T)
        al = np.asarray(inp["L0_attn_a_A_log" if tag == "f" else "L0_attn_a_A_b_log"], f32)
        d[f"Alog_{tag}"] = c(al.reshape(NT_DI, 128, NST).transpose(1, 0, 2).reshape(128, NT_DI * NST))
        dd = np.asarray(inp["L0_attn_a_D" if tag == "f" else "L0_attn_a_D_b"], f32)
        d[f"D_{tag}"] = c(dd.reshape(NT_DI, 128).T)
    d["W_outT"] = c(inp["L0_attn_a_W_out"].T)

    d["W_q"] = c(inp["L0_attn_b_W_q"])
    d["W_k"] = c(inp["L0_attn_b_W_k"])
    d["W_v"] = c(inp["L0_attn_b_W_v"])
    d["bq"] = c(np.asarray(inp["L0_attn_b_b_q"], f32).reshape(NT_DM, 128).T)
    d["bk"] = c(np.asarray(inp["L0_attn_b_b_k"], f32).reshape(NT_DM, 128).T)
    d["bv_r"] = c(np.asarray(inp["L0_attn_b_b_v"], f32).reshape(1, DM))

    d["gw1T"] = c(inp["L0_ffn_gate_w1"].T)
    d["gb1"] = c(np.asarray(inp["L0_ffn_gate_b1"], f32).reshape(NT_DM, 128).T)
    d["gw2T"] = c(inp["L0_ffn_gate_w2"].T)
    d["gb2"] = c(np.asarray(inp["L0_ffn_gate_b2"], f32).reshape(NE, 1))

    W1 = np.asarray(inp["L0_ffn_W1"], f32)
    TW1 = np.asarray(inp["L0_ffn_TW1"], f32)
    t1 = TW1.transpose(2, 0, 1).reshape(16, 64, NE, FF)
    d["TW1_strips"] = c(t1.transpose(0, 2, 1, 3).reshape(16, NE * 64, FF))
    d["W1_strips"] = c(W1.T.reshape(16, 64, FF))
    W2 = np.asarray(inp["L0_ffn_W2"], f32)
    TW2 = np.asarray(inp["L0_ffn_TW2"], f32)
    t2 = TW2.transpose(2, 0, 1).reshape(64, 64, NE, DM)
    d["TW2_strips"] = c(t2.transpose(0, 2, 1, 3).reshape(64, NE * 64, DM))
    d["W2_strips"] = c(W2.T.reshape(64, 64, DM))
    tb1 = np.concatenate([np.asarray(inp["L0_ffn_b1"], f32)[None],
                          np.asarray(inp["L0_ffn_Tb1"], f32)], 0)
    tb2 = np.concatenate([np.asarray(inp["L0_ffn_b2"], f32)[None],
                          np.asarray(inp["L0_ffn_Tb2"], f32)], 0)
    d["TbT1"] = c(tb1.reshape(9, FF // 128, 128).transpose(2, 0, 1))
    d["TbT2"] = c(tb2.reshape(9, DM // 128, 128).transpose(2, 0, 1))

    d["I128"] = np.eye(128, dtype=f32)
    ind32 = np.zeros((32, 32 * 128), f32)
    for j in range(32):
        ind32[j, j * 128:(j + 1) * 128] = 1.0
    d["ind32"] = np.ascontiguousarray(ind32, dtype=ml_dtypes.bfloat16)
    d["I64t2"] = c(np.concatenate([np.eye(64), np.eye(64)], 0))
    d["I64"] = np.eye(64, dtype=f32)
    indp = np.zeros((9, 4 * 128), f32)
    for j in range(4):
        indp[1 + 2 * j, j * 128:j * 128 + 64] = 1.0
        indp[2 + 2 * j, j * 128 + 64:(j + 1) * 128] = 1.0
    d["indpair"] = indp
    indsel = np.zeros((9, 9 * 128), f32)
    for e in range(9):
        indsel[e, e * 128:(e + 1) * 128] = 1.0
    d["indsel"] = indsel
    d["onesLN"] = np.full((128, 128), 1.0 / DM, f32)
    d["ones1"] = np.ones((1, 128), f32)
    d["fconsts"] = np.tile(np.array(_FCONSTS, f32)[None, :], (128, 1))

    x = np.asarray(inp["L0_attn_a_hidden"], f32)
    xs = [np.ascontiguousarray(x[b].T) for b in range(NCORE)]
    return d, xs


class P:
    def __init__(self, nc, es):
        self.nc = nc
        self.es = es
        self.blk = es.enter_context(nc.Block())
        self.sems = {}
        self.cnt = {}
        for name in ("sq", "gq", "pe", "dv", "ac", "po"):
            self.sems[name] = es.enter_context(nc.semaphore(name))
            self.cnt[name] = 0
        self.t = {}

    def mark(self, instr, s, k=1):
        instr.then_inc(self.sems[s], k)
        self.cnt[s] += k
        return self.cnt[s]

    def v(self, s):
        return self.cnt[s]

    def w(self, eng, s, val):
        if val and val > 0:
            eng.wait_ge(self.sems[s], val)

    def barrier(self):
        tgt = {s: self.v(s) for s in self.cnt}
        for ename in ("tensor", "vector", "scalar", "gpsimd", "sync"):
            @getattr(self.blk, ename)
            def _(e, tgt=tgt):
                for s, v in tgt.items():
                    if v > 0:
                        e.wait_ge(self.sems[s], v)


_UID = [0]


def _sb(p, es, name, shape, dtype=F32):
    _UID[0] += 1
    return es.enter_context(p.nc.sbuf_tensor(f"{name}_{_UID[0]}", shape, dtype))


def _ps(p, es, name, shape, dtype=F32):
    _UID[0] += 1
    return es.enter_context(p.nc.psum_tensor(f"{name}_{_UID[0]}", shape, dtype))


def _build_program():
    nc = bass.Bass("TRN2", target_bir_lowering=False, debug=False)
    dr = {}

    def din(name, shape, dtype=F32):
        dr[name] = nc.dram_tensor(name, list(shape), dtype, kind="ExternalInput").ap()

    din("x_in", [DM, S])
    din("W_inT", [DM, 2 * DI])
    for t in ("f", "b"):
        din(f"convw_{t}", [128, NT_DI * 4]); din(f"convb_{t}", [128, NT_DI])
        din(f"WxTp_{t}", [DI, 128]); din(f"WdtT_{t}", [DTR, DI])
        din(f"bdt_{t}", [128, NT_DI]); din(f"Alog_{t}", [128, NT_DI * NST])
        din(f"D_{t}", [128, NT_DI])
    din("W_outT", [DI, DM])
    din("W_q", [DM, DM]); din("W_k", [DM, DM]); din("W_v", [DM, DM])
    din("bq", [128, NT_DM]); din("bk", [128, NT_DM]); din("bv_r", [1, DM])
    din("gw1T", [DM, DM]); din("gb1", [128, NT_DM]); din("gw2T", [DM, NE]); din("gb2", [NE, 1])
    din("TW1_strips", [16, NE * 64, FF]); din("W1_strips", [16, 64, FF])
    din("TW2_strips", [64, NE * 64, DM]); din("W2_strips", [64, 64, DM])
    din("TbT1", [128, 9, NT_F]); din("TbT2", [128, 9, NT_DM])
    din("I128", [128, 128]); din("ind32", [32, 32 * 128], BF16); din("I64t2", [128, 64])
    din("I64", [64, 64]); din("indpair", [9, 4 * 128]); din("indsel", [9, 9 * 128])
    din("onesLN", [128, 128]); din("ones1", [1, 128])
    din("fconsts", [128, len(_FCONSTS)])

    dr["out"] = nc.dram_tensor("out", [DM, S], F32, kind="ExternalOutput").ap()

    def scr(name, shape, dtype=F32):
        dr[name] = nc.dram_tensor(name, list(shape), dtype, kind="Internal").ap()

    scr("res_a", [DM, S]); scr("res_b", [DM, S])
    scr("sp_xzx", [DI, S]); scr("sp_sz", [DI, S]); scr("sp_xsil", [DI, S])
    scr("sp_of", [DI, S], BF16); scr("sp_ofb", [DI, S], BF16)
    scr("sp_xln", [DM, S], BF16)
    scr("g_scr", [16]); scr("mb1_scr", [FF]); scr("mb2_scr", [DM])
    scr("TW1bf", [16, NE * 64, FF], BF16); scr("TW2bf", [64, NE * 64, DM], BF16)
    if _DEBUG_TAPS:
        for nm in ("tp_dt", "tp_dtx", "tp_B", "tp_C", "tp_dA", "tp_h", "tp_pn", "tp_y0", "tp_yf"):
            dr[nm] = nc.dram_tensor(nm, [128, S], F32, kind="ExternalOutput").ap()
    if _DEBUG_FFN:
        dr["tq_g"] = nc.dram_tensor("tq_g", [16, 1], F32, kind="ExternalOutput").ap()
        dr["tq_gp"] = nc.dram_tensor("tq_gp", [128, 4], F32, kind="ExternalOutput").ap()
        dr["tq_mb1"] = nc.dram_tensor("tq_mb1", [128, NT_F], F32, kind="ExternalOutput").ap()
        dr["tq_mb2"] = nc.dram_tensor("tq_mb2", [128, NT_DM], F32, kind="ExternalOutput").ap()
        dr["tq_w1"] = nc.dram_tensor("tq_w1", [128, FF], BF16, kind="ExternalOutput").ap()
        dr["tq_w2"] = nc.dram_tensor("tq_w2", [128, DM], BF16, kind="ExternalOutput").ap()
        dr["tq_hg"] = nc.dram_tensor("tq_hg", [128, NT_DM * CH], BF16, kind="ExternalOutput").ap()
        dr["tq_u"] = nc.dram_tensor("tq_u", [128, 256], BF16, kind="ExternalOutput").ap()
        dr["tq_w1f"] = nc.dram_tensor("tq_w1f", [DM, FF], BF16, kind="ExternalOutput").ap()
        dr["tq_w2f"] = nc.dram_tensor("tq_w2f", [FF, DM], BF16, kind="ExternalOutput").ap()
    if _DEBUG_ATT:
        dr["ta_nm"] = nc.dram_tensor("ta_nm", [128, 1], F32, kind="ExternalOutput").ap()
        dr["ta_rs"] = nc.dram_tensor("ta_rs", [128, 1], F32, kind="ExternalOutput").ap()
        dr["ta_rsr"] = nc.dram_tensor("ta_rsr", [128, 1], F32, kind="ExternalOutput").ap()
        dr["ta_pbf"] = nc.dram_tensor("ta_pbf", [128, S], BF16, kind="ExternalOutput").ap()
        dr["ta_pt"] = nc.dram_tensor("ta_pt", [128, 128], BF16, kind="ExternalOutput").ap()
        dr["ta_qt"] = nc.dram_tensor("ta_qt", [128, S], BF16, kind="ExternalOutput").ap()
        dr["ta_kt"] = nc.dram_tensor("ta_kt", [128, S], BF16, kind="ExternalOutput").ap()
        dr["ta_vt"] = nc.dram_tensor("ta_vt", [128, DM], BF16, kind="ExternalOutput").ap()

    with ExitStack() as es:
        p = P(nc, es)
        _emit(p, dr, es)
    return nc


def _emit(p, dr, es):
    T = p.t

    T["I128"] = _sb(p, es, "c_I128", [128, 128], BF16)
    T["I64t2"] = _sb(p, es, "c_I64t2", [128, 64], F32)
    T["I64"] = _sb(p, es, "c_I64", [64, 64], BF16)
    T["indp"] = _sb(p, es, "c_indp", [9, 4 * 128], BF16)
    T["indsel"] = _sb(p, es, "c_indsel", [9, 9 * 128], BF16)
    T["onesLN"] = _sb(p, es, "c_onesLN", [128, 128], BF16)
    T["ones1"] = _sb(p, es, "c_ones1", [1, 128], BF16)
    for t in ("f", "b"):
        T[f"convw_{t}"] = _sb(p, es, f"m_convw_{t}", [128, NT_DI * 4])
        T[f"convb_{t}"] = _sb(p, es, f"m_convb_{t}", [128, NT_DI])
        T[f"bdt_{t}"] = _sb(p, es, f"m_bdt_{t}", [128, NT_DI])
        T[f"Aneg_{t}"] = _sb(p, es, f"m_Aneg_{t}", [128, NT_DI * NST])
        T[f"D_{t}"] = _sb(p, es, f"m_D_{t}", [128, NT_DI])
    T["fconsts"] = _sb(p, es, "c_fconsts", [128, len(_FCONSTS)])
    for j, val in enumerate(_FCONSTS):
        p.nc.const_aps.register(F32, float(val), T["fconsts"][:, j:j + 1])
    T["WdtT_f"] = _sb(p, es, "wdtT_f", [DTR, DI], BF16)
    T["WdtT_b"] = _sb(p, es, "wdtT_b", [DTR, DI], BF16)
    T["gb2"] = _sb(p, es, "a_gb2", [NE, 1])
    T["bq"] = _sb(p, es, "a_bq", [128, NT_DM])
    T["bk"] = _sb(p, es, "a_bk", [128, NT_DM])
    T["gb1"] = _sb(p, es, "a_gb1", [128, NT_DM])

    @p.blk.sync
    def _(sync):
        for name in ("fconsts", "convw_f", "convw_b", "convb_f", "convb_b", "bdt_f", "bdt_b",
                     "D_f", "D_b", "gb2", "bq", "bk", "gb1"):
            p.mark(sync.dma_start(out=T[name][:], in_=dr[name][:]), "sq", 16)
        p.mark(sync.dma_start(out=T["Aneg_f"][:], in_=dr["Alog_f"][:]), "sq", 16)
        p.mark(sync.dma_start(out=T["Aneg_b"][:], in_=dr["Alog_b"][:]), "sq", 16)

    @p.blk.gpsimd
    def _(g):
        for name, src in (("I128", "I128"), ("I64", "I64"), ("indp", "indpair"),
                          ("indsel", "indsel"), ("onesLN", "onesLN"), ("ones1", "ones1")):
            p.mark(g.dma_start(out=T[name][:], in_=dr[src][:]), "gq", 16)
        p.mark(g.dma_start(out=T["I64t2"][:], in_=dr["I64t2"][:]), "gq", 16)

    ld_small = p.v("sq")

    @p.blk.scalar
    def _(sc):
        p.w(sc, "sq", ld_small)
        sc.activation(T["Aneg_f"][:], T["Aneg_f"][:], AF.Exp)
        p.mark(sc.activation(T["Aneg_b"][:], T["Aneg_b"][:], AF.Exp), "ac")

    @p.blk.vector
    def _(ve):
        p.w(ve, "ac", p.v("ac"))
        ve.tensor_scalar(T["Aneg_f"][:], T["Aneg_f"][:], -1.0, None, OP.mult)
        p.mark(ve.tensor_scalar(T["Aneg_b"][:], T["Aneg_b"][:], -1.0, None, OP.mult), "dv")

    res = {0: dr["x_in"], 1: dr["res_a"], 2: dr["res_b"], 3: dr["res_a"]}

    def dst(k):
        return dr["out"] if k == _N_SUB else res[k]

    _ln(p, dr, res[0])
    _mamba(p, dr, res[0], dst(1))
    if _N_SUB >= 2:
        _ln(p, dr, res[1])
        _ffn(p, dr, res[1], dst(2))
    if _N_SUB >= 3:
        _ln(p, dr, res[2])
        _attn(p, dr, res[2], dst(3))
    if _N_SUB >= 4:
        _ln(p, dr, res[3])
        _ffn(p, dr, res[3], dr["out"])

    @p.blk.sync
    def _(sync):
        sync.wait_ge(p.sems["sq"], p.v("sq"))
        sync.wait_ge(p.sems["gq"], p.v("gq"))


def _ln(p, dr, res_src):
    p.barrier()
    with ExitStack() as es:
        x_f = [_sb(p, es, f"ln_x{k}", [128, S]) for k in range(NT_DM)]
        x_bf = [_sb(p, es, f"ln_xb{k}", [128, S], BF16) for k in range(NT_DM)]
        sq_bf = [_sb(p, es, f"ln_sq{k}", [128, S], BF16) for k in range(NT_DM)]
        mm_sq = _sb(p, es, "ln_mmsq", [128, S])
        rstd = _sb(p, es, "ln_rstd", [128, S])
        mr = _sb(p, es, "ln_mr", [128, S])
        t1 = _sb(p, es, "ln_t1", [128, S])
        xo = [_sb(p, es, f"ln_xo{i}", [128, S], BF16) for i in range(2)]
        p_m = _ps(p, es, "ln_pm", [128, S])
        p_s = _ps(p, es, "ln_ps", [128, S])

        @p.blk.sync
        def _(sync):
            for k in range(NT_DM):
                p.mark(sync.dma_start(out=x_f[k][:], in_=res_src[k * 128:(k + 1) * 128, :]), "sq", 16)
        ld = p.v("sq")

        @p.blk.vector
        def _(ve):
            p.w(ve, "sq", ld)
            for k in range(NT_DM):
                i = ve.tensor_copy(x_bf[k][:], x_f[k][:])
            p.mark(i, "dv")
        v_cast = p.v("dv")

        @p.blk.scalar
        def _(sc):
            p.w(sc, "sq", ld)
            for k in range(NT_DM):
                i = sc.activation(sq_bf[k][:], x_f[k][:], AF.Square)
            p.mark(i, "ac")
        a_sq = p.v("ac")

        @p.blk.tensor
        def _(te):
            p.w(te, "dv", v_cast)
            p.w(te, "ac", a_sq)
            ones = p.t["onesLN"]
            for ch in range(NCH):
                for k in range(NT_DM):
                    te.matmul(p_m[:, ch * CH:(ch + 1) * CH], ones[:], x_bf[k][:, ch * CH:(ch + 1) * CH],
                              start=(k == 0), stop=(k == NT_DM - 1))
            for ch in range(NCH):
                for k in range(NT_DM):
                    i = te.matmul(p_s[:, ch * CH:(ch + 1) * CH], ones[:], sq_bf[k][:, ch * CH:(ch + 1) * CH],
                                  start=(k == 0), stop=(k == NT_DM - 1))
            p.mark(i, "pe")
        pe_st = p.v("pe")

        @p.blk.scalar
        def _(sc):
            p.w(sc, "pe", pe_st)
            p.mark(sc.activation(mm_sq[:], p_m[:], AF.Square), "ac")
        a_mm = p.v("ac")

        @p.blk.vector
        def _(ve):
            p.w(ve, "ac", a_mm)
            p.mark(ve.scalar_tensor_tensor(t1[:], p_s[:], 1.0, mm_sq[:], OP.mult, OP.subtract), "dv")
        v_var = p.v("dv")

        @p.blk.scalar
        def _(sc):
            p.w(sc, "dv", v_var)
            v = p.mark(sc.activation(t1[:], t1[:], AF.Ln, bias=1e-5), "ac")
            sc.wait_ge(p.sems["ac"], v)
            p.mark(sc.activation(rstd[:], t1[:], AF.Exp, scale=-0.5), "ac")
        a_rs = p.v("ac")

        dv0 = p.v("dv")
        sq0 = p.v("sq")

        @p.blk.vector
        def _(ve):
            p.w(ve, "ac", a_rs)
            ve.tensor_tensor(mr[:], p_m[:], rstd[:], OP.mult)
            for k in range(NT_DM):
                if k >= 2:
                    p.w(ve, "sq", sq0 + 16 * (k - 1))
                ve.tensor_tensor(t1[:], x_f[k][:], rstd[:], OP.mult)
                p.mark(ve.tensor_tensor(xo[k % 2][:], t1[:], mr[:], OP.subtract), "dv")

        @p.blk.sync
        def _(sync):
            for k in range(NT_DM):
                p.w(sync, "dv", dv0 + k + 1)
                p.mark(sync.dma_start(out=dr["sp_xln"][k * 128:(k + 1) * 128, :], in_=xo[k % 2][:]), "sq", 16)
    p.barrier()


def _mamba(p, dr, res_src, res_dst):
    with ExitStack() as mes:
        for tg in ("f", "b"):
            p.t[f"BCb_{tg}"] = _sb(p, mes, f"BCb_{tg}", [32, S], BF16)
            p.t[f"dtr_{tg}"] = _sb(p, mes, f"dtr_{tg}", [DTR, S], BF16)
        _mamba_stageA(p, dr, "f", rev=False)
        _mamba_stageB(p, dr, "f", rev=False)
        _mamba_stageA(p, dr, "b", rev=True)
        _mamba_stageB(p, dr, "b", rev=True)
        _mamba_outproj(p, dr, res_src, res_dst)
    for tg in ("f", "b"):
        del p.t[f"BCb_{tg}"], p.t[f"dtr_{tg}"]


def _mamba_stageA(p, dr, tag, rev):
    p.barrier()
    if f"BCb_{tag}" not in p.t:
        p.t[f"BCb_{tag}"] = _sb(p, p.es, f"BCb_{tag}", [32, S], BF16)
        p.t[f"dtr_{tag}"] = _sb(p, p.es, f"dtr_{tag}", [DTR, S], BF16)
    with ExitStack() as es:
        xzx = _sb(p, es, "ma_xzx", [128, NT_DI, CH + 3])
        stash = _sb(p, es, "ma_stash", [128, NT_DI, 3])
        sz_a = _sb(p, es, "ma_sz", [128, NT_DI, CH])
        xsil_bf = _sb(p, es, "ma_xsb", [128, NT_DI, CH], BF16)
        xdbl = _sb(p, es, "ma_xdbl", [128, S])
        cvt = [_sb(p, es, f"ma_cvt{i}", [128, CH]) for i in range(2)]
        wxp = _sb(p, es, "ma_wxp", [128, NT_DI, 128], BF16)

        @p.blk.gpsimd
        def _(g):
            p.mark(g.dma_start(out=wxp[:], in_=dr[f"WxTp_{tag}"].rearrange("(t p) c -> p t c", p=128)), "gq", 16)
        ld_wxp = p.v("gq")
        ps_in = [_ps(p, es, f"ma_pin{i}", [128, CH]) for i in range(2)]
        ps_xd = _ps(p, es, "ma_pxd", [128, CH])
        if not rev:
            w_in = [_sb(p, es, f"ma_win{k}", [128, 2 * DI], BF16) for k in range(NT_DM)]
            xln_ch = [_sb(p, es, f"ma_xln{i}", [128, NT_DM, CH], BF16) for i in range(2)]

            @p.blk.gpsimd
            def _(g):
                for k in range(NT_DM):
                    p.mark(g.dma_start(out=w_in[k][:], in_=dr["W_inT"][k * 128:(k + 1) * 128, :]), "gq", 16)
            ld_w = p.v("gq")

            @p.blk.gpsimd
            def _(g):
                p.mark(g.dma_start(out=p.t["WdtT_f"][:], in_=dr["WdtT_f"][:]), "gq", 16)
                p.mark(g.dma_start(out=p.t["WdtT_b"][:], in_=dr["WdtT_b"][:]), "gq", 16)
            p.t["ld_wdt"] = p.v("gq")

            @p.blk.gpsimd
            def _(g):
                for s in range(16):
                    p.mark(g.dma_start(out=dr["TW1bf"][s], in_=dr["TW1_strips"][s]), "gq", 16)
            p.t["conv1"] = p.v("gq")

            @p.blk.gpsimd
            def _(g):
                for s in range(64):
                    p.mark(g.dma_start(out=dr["TW2bf"][s], in_=dr["TW2_strips"][s]), "gq", 16)
            p.t["conv2"] = p.v("gq")
        else:
            ld_x = [_sb(p, es, f"ma_ldx{i}", [128, NT_DI, CH]) for i in range(2)]

        convw, convb = p.t[f"convw_{tag}"], p.t[f"convb_{tag}"]
        sp_xln_r = dr["sp_xln"].rearrange("(k p) s -> p k s", p=128)
        sp_xzx_r = dr["sp_xzx"].rearrange("(t p) s -> p t s", p=128)
        sp_sz_r = dr["sp_sz"].rearrange("(t p) s -> p t s", p=128)
        sp_xsil_r = dr["sp_xsil"].rearrange("(t p) s -> p t s", p=128)

        fillm = {}
        fill_s = "ac" if not rev else "dv"
        silm = {}
        cvm = {}
        xsbm = {}
        inpe = {}
        spx = {}
        szsp = {}
        sps = {}
        xdm = {}
        pech = {}
        stm = {}

        for ch in range(NCH):
            if not rev:
                @p.blk.sync
                def _(sync, ch=ch):
                    if ch >= 2:
                        p.w(sync, "pe", pech[ch - 2])
                    p.mark(sync.dma_start(out=xln_ch[ch % 2][:],
                                          in_=sp_xln_r[:, :, ch * CH:(ch + 1) * CH]), "sq", 16)
                ld_ch = p.v("sq")

                ac0 = p.v("ac")

                @p.blk.tensor
                def _(te, ch=ch, ld_ch=ld_ch, ac0=ac0):
                    p.w(te, "sq", ld_ch)
                    if ch == 0:
                        p.w(te, "gq", ld_w)
                    for dz in range(2 * NT_DI):
                        ig = ch * 2 * NT_DI + dz
                        if dz < 2:
                            if ig >= 2:
                                p.w(te, "ac", fillm[("ig", ig - 2)])
                        else:
                            p.w(te, "ac", ac0 + dz - 1)
                        for kk in range(NT_DM):
                            i = te.matmul(ps_in[dz % 2][:, :], w_in[kk][:, dz * 128:(dz + 1) * 128],
                                          xln_ch[ch % 2][:, kk, :], start=(kk == 0), stop=(kk == NT_DM - 1))
                        inpe[ig] = p.mark(i, "pe")
                    pech[ch] = p.v("pe")

                @p.blk.scalar
                def _(sc, ch=ch):
                    if ch > 0:
                        p.w(sc, "gq", sps[ch - 1])
                        p.w(sc, "dv", stm[ch - 1])
                    for dz in range(2 * NT_DI):
                        ig = ch * 2 * NT_DI + dz
                        p.w(sc, "pe", inpe[ig])
                        if dz < NT_DI:
                            i = sc.copy(xzx[:, dz, 3:CH + 3], ps_in[dz % 2][:, :])
                        else:
                            i = sc.activation(sz_a[:, dz - NT_DI, :], ps_in[dz % 2][:, :], AF.Silu)
                        fillm[("ig", ig)] = p.mark(i, "ac")
                        if dz < NT_DI:
                            fillm[(ch, dz)] = fillm[("ig", ig)]
                fill_done = p.v("ac")
            else:
                o0 = S - (ch + 1) * CH

                @p.blk.sync
                def _(sync, ch=ch, o0=o0):
                    if ch >= 2:
                        p.w(sync, "dv", fillm[(ch - 2, NT_DI - 1)])
                    p.mark(sync.dma_start(out=ld_x[ch % 2][:], in_=sp_xzx_r[:, :, o0:o0 + CH]), "sq", 16)
                ld_ch = p.v("sq")

                @p.blk.vector
                def _(ve, ch=ch, ld_ch=ld_ch):
                    p.w(ve, "sq", ld_ch)
                    if ch > 0:
                        p.w(ve, "ac", silm[ch * NT_DI - 1])
                        p.w(ve, "gq", sps[ch - 1])
                    for t in range(NT_DI):
                        i = ve.tensor_copy(xzx[:, t, 3:CH + 3], ld_x[ch % 2][:, t, ::-1])
                        fillm[(ch, t)] = p.mark(i, "dv")
                fill_done = p.v("dv")

            @p.blk.vector
            def _(ve, ch=ch, fill_done=fill_done):
                if ch == 0:
                    v = p.mark(ve.memset(xzx[:, :, 0:3], 0.0), "dv")
                else:
                    v = p.mark(ve.tensor_copy(xzx[:, :, 0:3], stash[:]), "dv")
                ve.wait_ge(p.sems["dv"], v)
                p.w(ve, fill_s, fill_done)
                i = ve.tensor_copy(stash[:], xzx[:, :, CH:CH + 3])
                stm[ch] = p.mark(i, "dv")

            if not rev:
                @p.blk.gpsimd
                def _(g, ch=ch, fill_done=fill_done):
                    p.w(g, "ac", fill_done)
                    p.mark(g.dma_start(out=sp_xzx_r[:, :, ch * CH:(ch + 1) * CH],
                                       in_=xzx[:, :, 3:CH + 3]), "gq", 16)
                    spx[ch] = p.v("gq")
                    p.mark(g.dma_start(out=sp_sz_r[:, :, ch * CH:(ch + 1) * CH],
                                       in_=sz_a[:, :, :]), "gq", 16)
                    szsp[ch] = p.v("gq")

            for t in range(NT_DI):
                isil = ch * NT_DI + t

                @p.blk.vector
                def _(ve, ch=ch, t=t, isil=isil):
                    if isil >= 2:
                        p.w(ve, "ac", silm[isil - 2])
                    p.w(ve, fill_s, fillm[(ch, t)])
                    p.w(ve, "dv", stm[ch])
                    cv = cvt[t % 2]
                    ve.tensor_scalar(cv[:], xzx[:, t, 0:CH], convw[:, 4 * t:4 * t + 1], None, OP.mult)
                    for kk in range(1, 4):
                        i = ve.scalar_tensor_tensor(cv[:], xzx[:, t, kk:CH + kk], convw[:, 4 * t + kk:4 * t + kk + 1],
                                                    cv[:], OP.mult, OP.add)
                    cvm[isil] = p.mark(i, "dv")

                @p.blk.scalar
                def _(sc, ch=ch, t=t, isil=isil):
                    p.w(sc, "dv", cvm[isil])
                    if not rev:
                        p.w(sc, "gq", spx[ch])
                    i = sc.activation(xzx[:, t, 3:CH + 3], cvt[t % 2][:], AF.Silu, bias=convb[:, t:t + 1])
                    silm[isil] = p.mark(i, "ac")

            @p.blk.vector
            def _(ve, ch=ch):
                for t in range(NT_DI):
                    p.w(ve, "ac", silm[ch * NT_DI + t])
                    i = ve.tensor_copy(xsil_bf[:, t, :], xzx[:, t, 3:CH + 3])
                    xsbm[(ch, t)] = p.mark(i, "dv")

            @p.blk.gpsimd
            def _(g, ch=ch):
                p.w(g, "ac", silm[ch * NT_DI + NT_DI - 1])
                p.mark(g.dma_start(out=sp_xsil_r[:, :, ch * CH:(ch + 1) * CH],
                                   in_=xzx[:, :, 3:CH + 3]), "gq", 16)
                sps[ch] = p.v("gq")

            @p.blk.tensor
            def _(te, ch=ch):
                p.w(te, "dv", xsbm[(ch, NT_DI - 1)])
                p.w(te, "gq", ld_wxp)
                if ch > 0:
                    p.w(te, "dv", xdm[ch - 1])
                wx = wxp
                for t in range(NT_DI):
                    i = te.matmul(ps_xd[:, :], wx[:, t, :], xsil_bf[:, t, :],
                                  start=(t == 0), stop=(t == NT_DI - 1))
                p.mark(i, "pe")
            pe_xd = p.v("pe")

            @p.blk.vector
            def _(ve, ch=ch, pe_xd=pe_xd):
                p.w(ve, "pe", pe_xd)
                i = ve.tensor_copy(xdbl[:, ch * CH:(ch + 1) * CH], ps_xd[:, :])
                xdm[ch] = p.mark(i, "dv")

        @p.blk.vector
        def _(ve):
            p.w(ve, "dv", xdm[NCH - 1])
            ve.tensor_copy(p.t[f"BCb_{tag}"][:, :], xdbl[0:32, :])
            p.mark(ve.tensor_copy(p.t[f"dtr_{tag}"][:, :], xdbl[64:128, :]), "dv")
    p.barrier()


def _mamba_stageB(p, dr, tag, rev):
    HF = S // 2
    NBK = NT_DI // 2
    with ExitStack() as es:
        dt_b = _sb(p, es, "mb_dt", [128, 2, S])
        dtx_b = _sb(p, es, "mb_dtx", [128, 2, S])
        xs_b = _sb(p, es, "mb_xs", [128, 2, S])
        szb = [_sb(p, es, f"mb_sz{i}", [128, 2, HF]) for i in range(2)]
        Bbc = [_sb(p, es, f"mb_Bbc{i}", [128, HF], BF16) for i in range(2)]
        Cbc = [_sb(p, es, f"mb_Cbc{i}", [128, HF], BF16) for i in range(2)]
        dA = [_sb(p, es, f"mb_dA{i}", [128, HF]) for i in range(2)]
        dp = [_sb(p, es, f"mb_dp{i}", [128, HF]) for i in range(2)]
        pnb = [_sb(p, es, f"mb_pn{i}", [128, HF], BF16) for i in range(2)]
        hsc = _sb(p, es, "mb_h", [128, HF], BF16)
        hl = _sb(p, es, "mb_hl", [128, 32])
        sp_full = _sb(p, es, "mb_spf", [128, S])
        t1 = _sb(p, es, "mb_t1", [128, HF])
        of_t = [_sb(p, es, f"mb_of{i}", [128, HF], BF16) for i in range(2)]
        WdtT = p.t[f"WdtT_{tag}"]
        ind32 = _sb(p, es, "mb_ind32", [32, 32 * 128], BF16)
        ps_y = [_ps(p, es, f"mb_psy{i}", [128, HF]) for i in range(2)]
        ps_dt = [_ps(p, es, f"mb_psdt{i}", [128, CH]) for i in range(2)]
        ps_bc = [_ps(p, es, f"mb_psbc{i}", [128, CH]) for i in range(2)]

        ld_wdt = p.t["ld_wdt"]

        @p.blk.sync
        def _(sync):
            p.mark(sync.dma_start(out=ind32[:], in_=dr["ind32"][:]), "sq", 16)
        ld_ind = p.v("sq")

        Aneg, Dv = p.t[f"Aneg_{tag}"], p.t[f"D_{tag}"]
        bdt = p.t[f"bdt_{tag}"]
        BCb, dtr = p.t[f"BCb_{tag}"], p.t[f"dtr_{tag}"]
        I128 = p.t["I128"]
        sp_xsil_r = dr["sp_xsil"].rearrange("(t p) s -> p t s", p=128)
        sp_sz_r = dr["sp_sz"].rearrange("(t p) s -> p t s", p=128)
        sp_of_r = dr["sp_of" if not rev else "sp_ofb"].rearrange("(t p) s -> p t s", p=128)

        def bcast_mm(te, st, jj, base):
            row = st if jj < 2 else 16 + st
            c0 = base + (jj % 2) * CH
            return te.matmul(ps_bc[jj % 2][:, :], ind32[:, row * 128:(row + 1) * 128],
                             BCb[:, c0:c0 + CH], start=True, stop=True)

        def bc_evac(sc, st, jj):
            dstt = Bbc if jj < 2 else Cbc
            c0 = (jj % 2) * CH
            return sc.copy(dstt[st % 2][:, c0:c0 + CH], ps_bc[jj % 2][:, :])

        bcev = {}
        xlm = {}
        dtln = {}
        dtxm = {}
        ofm = {}
        ofs = {}
        szm = {}
        scm = {}
        pnm = {}
        dbm = {}
        acm = {}
        damk = {}
        dtq_last = None
        ln_guard = {}

        ac_seed = p.v("ac")

        @p.blk.tensor
        def _(te):
            p.w(te, "gq", ld_wdt)
            p.w(te, "sq", ld_ind)
            for jj in range(4):
                if jj >= 2:
                    p.w(te, "ac", ac_seed + jj - 1)
                p.mark(bcast_mm(te, 0, jj, 0), "pe")
        seed_pe0 = p.v("pe") - 4

        @p.blk.scalar
        def _(sc):
            for jj in range(4):
                p.w(sc, "pe", seed_pe0 + jj + 1)
                p.mark(bc_evac(sc, 0, jj), "ac")
        bcev[0] = p.v("ac")
        assert bcev[0] == ac_seed + 4
        prev_e23 = (ac_seed + 3, ac_seed + 4)

        for blk in range(NBK):
            for t4 in range(2):
                t = blk * 2 + t4

                @p.blk.sync
                def _(sync, blk=blk, t4=t4, t=t):
                    if blk > 0:
                        p.w(sync, "dv", ofm[(blk - 1, 1, t4)])
                    p.mark(sync.dma_start(out=xs_b[:, t4, :], in_=sp_xsil_r[:, t, :]), "sq", 16)
                    xlm[(blk, t4)] = p.v("sq")

                pe0_dt = p.v("pe")
                ac0_dt = p.v("ac")

                @p.blk.tensor
                def _(te, blk=blk, t4=t4, t=t, ac0_dt=ac0_dt, dtq_last=dtq_last):
                    for q in range(NCH):
                        if q >= 2:
                            p.w(te, "ac", ac0_dt + q - 1)
                        elif dtq_last is not None:
                            p.w(te, "ac", dtq_last[q])
                        p.mark(te.matmul(ps_dt[q % 2][:, :], WdtT[:, t * 128:(t + 1) * 128],
                                         dtr[:, q * CH:(q + 1) * CH], start=True, stop=True), "pe")

                @p.blk.scalar
                def _(sc, blk=blk, t4=t4, t=t, pe0_dt=pe0_dt):
                    if blk > 0:
                        p.w(sc, "ac", ln_guard[t4])
                    for q in range(NCH):
                        p.w(sc, "pe", pe0_dt + q + 1)
                        i = sc.activation(sp_full[:, q * CH:(q + 1) * CH], ps_dt[q % 2][:, :],
                                          AF.Exp, bias=bdt[:, t:t + 1])
                        p.mark(i, "ac")
                    i = sc.activation(dt_b[:, t4, :], sp_full[:], AF.Ln, bias=1.0)
                    dtln[(blk, t4)] = p.mark(i, "ac")
                assert p.v("ac") == ac0_dt + NCH + 1
                dtq_last = (ac0_dt + 3, ac0_dt + 4)

                @p.blk.vector
                def _(ve, blk=blk, t4=t4):
                    p.w(ve, "ac", dtln[(blk, t4)])
                    p.w(ve, "sq", xlm[(blk, t4)])
                    if blk > 0:
                        p.w(ve, "dv", dbm[(NST - 1, t4)])
                    i = ve.tensor_tensor(dtx_b[:, t4, :], dt_b[:, t4, :], xs_b[:, t4, :], OP.mult)
                    dtxm[(blk, t4)] = p.mark(i, "dv")

            for h in range(2):
                base = h * HF
                fwd_off = base if not rev else S - base - HF

                @p.blk.sync
                def _(sync, blk=blk, h=h, fwd_off=fwd_off):
                    if blk > 0:
                        p.w(sync, "dv", ofm[(blk - 1, h, 1)])
                    for t4 in range(2):
                        t = blk * 2 + t4
                        p.mark(sync.dma_start(out=szb[h][:, t4, :],
                                              in_=sp_sz_r[:, t, fwd_off:fwd_off + HF]), "sq", 16)
                    szm[(blk, h)] = p.v("sq")

                for n in range(NST):
                    nn = (n + 1) % NST
                    last_gi = (blk == NBK - 1 and h == 1 and n == NST - 1)
                    if nn == 0:
                        if h == 0:
                            nbase = HF
                        else:
                            nbase = 0
                    else:
                        nbase = base

                    ac0 = p.v("ac")
                    dv0 = p.v("dv")

                    if not last_gi:
                        @p.blk.tensor
                        def _(te, nn=nn, nbase=nbase, ac0=ac0, prev_e23=prev_e23):
                            for jj in range(4):
                                if jj >= 2:
                                    p.w(te, "ac", ac0 + jj + 1)
                                else:
                                    p.w(te, "ac", prev_e23[jj])
                                p.mark(bcast_mm(te, nn, jj, nbase), "pe")
                    bc_pe0 = p.v("pe") - (0 if last_gi else 4)

                    @p.blk.scalar
                    def _(sc, blk=blk, h=h, n=n, nn=nn, last_gi=last_gi, bc_pe0=bc_pe0):
                        for t4 in range(2):
                            t = blk * 2 + t4
                            p.w(sc, "ac", dtln[(blk, t4)])
                            pk = (n - 1, t4) if n > 0 else (NST - 1, t4)
                            if pk in scm:
                                p.w(sc, "dv", scm[pk])
                            i = sc.activation(dA[t4][:], dt_b[:, t4, base:base + HF], AF.Exp,
                                              scale=Aneg[:, t * NST + n:t * NST + n + 1])
                            damk[(n, t4)] = p.mark(i, "ac")
                        if not last_gi:
                            pk = (n - 1, 1) if n > 0 else (NST - 1, 1)
                            if pk in pnm:
                                p.w(sc, "dv", pnm[pk])
                                p.w(sc, "dv", dbm[pk])
                            for jj in range(4):
                                p.w(sc, "pe", bc_pe0 + jj + 1)
                                p.mark(bc_evac(sc, nn, jj), "ac")
                    if not last_gi:
                        bcev[nn] = ac0 + 6
                        prev_e23 = (ac0 + 5, ac0 + 6)
                    assert p.v("ac") == ac0 + (2 if last_gi else 6)

                    @p.blk.vector
                    def _(ve, blk=blk, h=h, n=n, dv0=dv0):
                        nb = n % 2
                        for t4 in range(2):
                            p.w(ve, "ac", bcev[n])
                            p.w(ve, "ac", damk[(n, t4)])
                            pk = (n - 1, t4) if n > 0 else (NST - 1, t4)
                            if pk in acm:
                                p.w(ve, "pe", acm[pk])
                            i = ve.tensor_tensor(dp[t4][:], dtx_b[:, t4, base:base + HF],
                                                 Bbc[nb][:], OP.mult)
                            dbm[(n, t4)] = p.mark(i, "dv")
                            col = n * 2 + t4
                            init = 0.0 if h == 0 else hl[:, col:col + 1]
                            i = ve.tensor_tensor_scan(hsc[:], dA[t4][:], dp[t4][:], init,
                                                      OP.mult, OP.add)
                            scm[(n, t4)] = p.mark(i, "dv")
                            if h == 0:
                                p.mark(ve.tensor_copy(hl[:, col:col + 1], hsc[:, HF - 1:HF]), "dv")
                            i = ve.tensor_tensor(pnb[t4][:], hsc[:], Cbc[nb][:], OP.mult)
                            pnm[(n, t4)] = p.mark(i, "dv")

                    @p.blk.tensor
                    def _(te, blk=blk, h=h, n=n):
                        for t4 in range(2):
                            p.w(te, "dv", pnm[(n, t4)])
                            if n == 0 and (blk > 0 or h > 0):
                                ph = (blk, h - 1, t4) if h > 0 else (blk - 1, 1, t4)
                                p.w(te, "dv", ofm[ph])
                            for cq in range(2):
                                i = te.matmul(ps_y[t4][:, cq * CH:(cq + 1) * CH], I128[:, :],
                                              pnb[t4][:, cq * CH:(cq + 1) * CH],
                                              start=(n == 0), stop=(n == NST - 1))
                            acm[(n, t4)] = p.mark(i, "pe")

                for t4 in range(2):
                    t = blk * 2 + t4

                    @p.blk.vector
                    def _(ve, blk=blk, h=h, t4=t4, t=t, base=base):
                        p.w(ve, "pe", acm[(NST - 1, t4)])
                        p.w(ve, "sq", szm[(blk, h)])
                        pk = (blk, h - 1, t4) if h > 0 else \
                             ((blk - 1, 1, t4) if blk > 0 else None)
                        if pk is not None:
                            p.w(ve, "po", ofs[pk])
                        i = ve.scalar_tensor_tensor(t1[:], xs_b[:, t4, base:base + HF],
                                                    Dv[:, t:t + 1], ps_y[t4][:, :],
                                                    OP.mult, OP.add)
                        v = p.mark(i, "dv")
                        ve.wait_ge(p.sems["dv"], v)
                        if not rev:
                            i = ve.tensor_tensor(of_t[t4][:], t1[:], szb[h][:, t4, :], OP.mult)
                        else:
                            i = ve.tensor_tensor(of_t[t4][:], t1[:, ::-1], szb[h][:, t4, :], OP.mult)
                        ofm[(blk, h, t4)] = p.mark(i, "dv")

                    @p.blk.scalar
                    def _(ss, blk=blk, h=h, t4=t4, t=t, fwd_off=fwd_off):
                        p.w(ss, "dv", ofm[(blk, h, t4)])
                        p.mark(ss.dma_start(out=sp_of_r[:, t, fwd_off:fwd_off + HF],
                                            in_=of_t[t4][:]), "po", 16)
                        ofs[(blk, h, t4)] = p.v("po")
            for t4 in range(2):
                ln_guard[t4] = damk[(NST - 1, t4)]
    p.barrier()


def _mamba_outproj(p, dr, res_src, res_dst):
    with ExitStack() as es:
        wo_f = [_sb(p, es, f"mo_wf{i}", [128, DM]) for i in range(2)]
        wo_bf = [_sb(p, es, f"mo_wb{k}", [128, DM], BF16) for k in range(NT_DI)]
        of_f = [_sb(p, es, f"mo_off{i}", [128, NT_DI, CH], BF16) for i in range(2)]
        of_b = [_sb(p, es, f"mo_ofb{i}", [128, NT_DI, CH], BF16) for i in range(2)]
        res_t = [_sb(p, es, f"mo_res{k}", [128, S]) for k in range(NT_DM)]
        ps_o = [_ps(p, es, f"mo_ps{i}", [128, CH]) for i in range(2)]

        sp_of_r = dr["sp_of"].rearrange("(t p) s -> p t s", p=128)
        sp_ofb_r = dr["sp_ofb"].rearrange("(t p) s -> p t s", p=128)

        dv0 = p.v("dv")
        sq0 = p.v("sq")

        @p.blk.sync
        def _(sync):
            for k in range(NT_DI):
                if k >= 2:
                    p.w(sync, "dv", dv0 + k - 1)
                p.mark(sync.dma_start(out=wo_f[k % 2][:], in_=dr["W_outT"][k * 128:(k + 1) * 128, :]), "sq", 16)

        @p.blk.vector
        def _(ve):
            for k in range(NT_DI):
                p.w(ve, "sq", sq0 + 16 * (k + 1))
                p.mark(ve.tensor_scalar(wo_bf[k][:], wo_f[k % 2][:], 0.5, None, OP.mult), "dv")
        v_w = p.v("dv")

        pe0 = p.v("pe")
        dvA0 = p.v("dv")
        sq_r0 = p.v("sq")

        @p.blk.sync
        def _(sync):
            for k in range(NT_DM):
                p.mark(sync.dma_start(out=res_t[k][:], in_=res_src[k * 128:(k + 1) * 128, :]), "sq", 16)
            for chn in range(NCH):
                if chn >= 2:
                    p.w(sync, "pe", pe0 + (chn - 2) * NT_DM + NT_DM)
                p.mark(sync.dma_start(out=of_f[chn % 2][:], in_=sp_of_r[:, :, chn * CH:(chn + 1) * CH]), "sq", 16)
                p.mark(sync.dma_start(out=of_b[chn % 2][:], in_=sp_ofb_r[:, :, chn * CH:(chn + 1) * CH]), "sq", 16)

        @p.blk.tensor
        def _(te):
            p.w(te, "dv", v_w)
            for chn in range(NCH):
                p.w(te, "sq", sq_r0 + 16 * NT_DM + 32 * (chn + 1))
                for m in range(NT_DM):
                    fl = chn * NT_DM + m
                    if fl >= 2:
                        p.w(te, "dv", dvA0 + fl - 1)
                    for k in range(NT_DI):
                        te.matmul(ps_o[fl % 2][:, :], wo_bf[k][:, m * 128:(m + 1) * 128],
                                  of_f[chn % 2][:, k, :], start=(k == 0), stop=False)
                    for k in range(NT_DI):
                        i = te.matmul(ps_o[fl % 2][:, :], wo_bf[k][:, m * 128:(m + 1) * 128],
                                      of_b[chn % 2][:, k, :], start=False, stop=(k == NT_DI - 1))
                    p.mark(i, "pe")

        @p.blk.vector
        def _(ve):
            p.w(ve, "sq", sq_r0 + 16 * NT_DM)
            for chn in range(NCH):
                for m in range(NT_DM):
                    fl = chn * NT_DM + m
                    p.w(ve, "pe", pe0 + fl + 1)
                    i = ve.tensor_tensor(res_t[m][:, chn * CH:(chn + 1) * CH],
                                         res_t[m][:, chn * CH:(chn + 1) * CH], ps_o[fl % 2][:, :], OP.add)
                    p.mark(i, "dv")

        @p.blk.sync
        def _(sync):
            for k in range(NT_DM):
                p.w(sync, "dv", dvA0 + (NCH - 1) * NT_DM + k + 1)
                p.mark(sync.dma_start(out=res_dst[k * 128:(k + 1) * 128, :], in_=res_t[k][:]), "sq", 16)
    p.barrier()


def _ffn(p, dr, res_src, res_dst):
    p.barrier()
    with ExitStack() as es:
        mW1T = _sb(p, es, "ff_w1", [128, NT_DM, FF], BF16)
        xln_ch = _sb(p, es, "ff_xln", [128, NT_DM, CH], BF16)
        g_sb = _sb(p, es, "ff_g", [NE, 1])
        g_aug = _sb(p, es, "ff_gaug", [16, 1])
        g_rf = _sb(p, es, "ff_grf", [16, 512])
        g_hi = _sb(p, es, "ff_ghi", [16, 512], BF16)
        g_lo4 = _sb(p, es, "ff_glo4", [16, 512])
        g_lo = _sb(p, es, "ff_glo", [16, 512], BF16)
        g_pair = _sb(p, es, "ff_gpair", [128, 4])
        g_bc = _sb(p, es, "ff_gbc", [128, 9])
        TbT1 = _sb(p, es, "ff_tbt1", [128, 9, NT_F])
        TbT2 = _sb(p, es, "ff_tbt2", [128, 9, NT_DM])
        lhsp = _sb(p, es, "ff_lhsp", [128, 4, 64], BF16)
        mb1c = _sb(p, es, "ff_mb1c", [128, NT_F])
        mb2c = _sb(p, es, "ff_mb2c", [128, NT_DM])
        sp_xln_r = dr["sp_xln"].rearrange("(k p) s -> p k s", p=128)
        I64t2, I64, indp = p.t["I64t2"], p.t["I64"], p.t["indp"]

        ring1 = ExitStack()
        twb1 = _sb(p, ring1, "ff_twb1", [128, 6, FF], BF16)

        with ExitStack() as ges:
            gw1 = _sb(p, ges, "ff_gw1", [128, NT_DM, DM], BF16)
            gw2 = _sb(p, ges, "ff_gw2", [128, NT_DM, NE], BF16)
            hg_ch = _sb(p, ges, "ff_hg", [128, NT_DM, CH], BF16)
            gsum = _sb(p, ges, "ff_gsum", [NE, 1])
            ps_g1 = [_ps(p, ges, f"ff_psg{i}", [128, CH]) for i in range(2)]
            ps_g2 = _ps(p, ges, "ff_psg2", [NE, S])

            @p.blk.gpsimd
            def _(g):
                p.mark(g.dma_start(out=gw1[:], in_=dr["gw1T"].rearrange("(k p) d -> p k d", p=128)), "gq", 16)
                p.mark(g.dma_start(out=gw2[:], in_=dr["gw2T"].rearrange("(k p) e -> p k e", p=128)), "gq", 16)
            ld_gw = p.v("gq")

            g1pe = {}
            g1ac = {}
            g2pe = {}
            for ch in range(NCH):
                @p.blk.sync
                def _(sync, ch=ch):
                    if ch > 0:
                        p.w(sync, "pe", g2pe[ch - 1])
                    p.mark(sync.dma_start(out=xln_ch[:], in_=sp_xln_r[:, :, ch * CH:(ch + 1) * CH]), "sq", 16)
                ldx = p.v("sq")

                ac0_ch = p.v("ac")
                for o in range(NT_DM):
                    @p.blk.tensor
                    def _(te, ch=ch, o=o, ldx=ldx, ac0_ch=ac0_ch):
                        if o == 0:
                            p.w(te, "sq", ldx)
                            if ch == 0:
                                p.w(te, "gq", ld_gw)
                        if o >= 2:
                            p.w(te, "ac", ac0_ch + o - 1)
                        elif ch > 0:
                            p.w(te, "ac", g1ac[(ch - 1, NT_DM - 2 + o)])
                        for k in range(NT_DM):
                            i = te.matmul(ps_g1[o % 2][:, :], gw1[:, k, o * 128:(o + 1) * 128],
                                          xln_ch[:, k, :], start=(k == 0), stop=(k == NT_DM - 1))
                        g1pe[(ch, o)] = p.mark(i, "pe")

                @p.blk.scalar
                def _(sc, ch=ch):
                    for o in range(NT_DM):
                        p.w(sc, "pe", g1pe[(ch, o)])
                        i = sc.activation(hg_ch[:, o, :], ps_g1[o % 2][:, :],
                                          AF.Relu, bias=p.t["gb1"][:, o:o + 1])
                        g1ac[(ch, o)] = p.mark(i, "ac")
                assert p.v("ac") == ac0_ch + NT_DM

                if _DEBUG_FFN and ch == 0:
                    @p.blk.gpsimd
                    def _(g):
                        p.w(g, "ac", g1ac[(0, NT_DM - 1)])
                        p.mark(g.dma_start(out=dr["tq_hg"][:], in_=hg_ch[:].rearrange("p k c -> p (k c)")), "gq", 16)
                    hgtap = p.v("gq")

                    @p.blk.sync
                    def _(sync):
                        p.w(sync, "gq", hgtap)

                @p.blk.tensor
                def _(te, ch=ch):
                    p.w(te, "ac", g1ac[(ch, NT_DM - 1)])
                    for k in range(NT_DM):
                        i = te.matmul(ps_g2[:, ch * CH:(ch + 1) * CH], gw2[:, k, :],
                                      hg_ch[:, k, :], start=(k == 0), stop=(k == NT_DM - 1))
                    g2pe[ch] = p.mark(i, "pe")

            @p.blk.vector
            def _(ve):
                p.w(ve, "pe", g2pe[NCH - 1])
                v = p.mark(ve.tensor_reduce(gsum[:], ps_g2[:], AX.X, OP.add), "dv")
                ve.wait_ge(p.sems["dv"], v)
                p.mark(ve.tensor_scalar(g_sb[:], gsum[:], 1.0 / S, p.t["gb2"][:], OP.mult, OP.add), "dv")
            v_g = p.v("dv")
            gate_pe_end = p.v("pe")

        if _FFN_BARRIERS[0]:
            p.barrier()
        @p.blk.vector
        def _(ve):
            ve.memset(g_rf[:], 1.0)
            p.mark(ve.memset(g_aug[:], 1.0), "dv")
        v_ga0 = p.v("dv")

        @p.blk.gpsimd
        def _(g):
            p.w(g, "dv", v_g)
            p.w(g, "dv", v_ga0)
            p.mark(g.dma_start(out=g_aug[1:9, :], in_=g_sb[0:8, :]), "gq", 16)
        gq_ga = p.v("gq")

        @p.blk.vector
        def _(ve):
            p.w(ve, "gq", gq_ga)
            v = p.mark(ve.tensor_scalar(g_rf[:], g_rf[:], g_aug[:, 0:1], None, OP.mult), "dv")
            ve.wait_ge(p.sems["dv"], v)
            v = p.mark(ve.tensor_copy(g_hi[:], g_rf[:]), "dv")
            ve.wait_ge(p.sems["dv"], v)
            v = p.mark(ve.tensor_tensor(g_lo4[:], g_rf[:], g_hi[:], OP.subtract), "dv")
            ve.wait_ge(p.sems["dv"], v)
            p.mark(ve.tensor_copy(g_lo[:], g_lo4[:]), "dv")
        v_ghl = p.v("dv")

        @p.blk.sync
        def _(sync):
            p.mark(sync.dma_start(out=TbT1[:], in_=dr["TbT1"][:]), "sq", 16)
            p.mark(sync.dma_start(out=TbT2[:], in_=dr["TbT2"][:]), "sq", 16)
        ld_tb = p.v("sq")

        with ExitStack() as pes:
            ps_gp = _ps(p, pes, "ff_psgp", [128, 512])
            indsel = p.t["indsel"]

            @p.blk.tensor
            def _(te):
                p.w(te, "dv", v_ghl)
                for j in range(4):
                    te.matmul(ps_gp[:, j:j + 1], indp[:, j * 128:(j + 1) * 128],
                              g_hi[0:9, 0:1], start=True, stop=False)
                    te.matmul(ps_gp[:, j:j + 1], indp[:, j * 128:(j + 1) * 128],
                              g_lo[0:9, 0:1], start=False, stop=True)
                for e in range(9):
                    te.matmul(ps_gp[:, 4 + e:5 + e], indsel[:, e * 128:(e + 1) * 128],
                              g_hi[0:9, 0:1], start=True, stop=False)
                    i = te.matmul(ps_gp[:, 4 + e:5 + e], indsel[:, e * 128:(e + 1) * 128],
                                  g_lo[0:9, 0:1], start=False, stop=True)
                p.mark(i, "pe")
            gp_pe_end = p.v("pe")

            @p.blk.scalar
            def _(sc):
                p.w(sc, "pe", gp_pe_end)
                sc.copy(g_pair[:, :], ps_gp[:, 0:4])
                p.mark(sc.copy(g_bc[:, :], ps_gp[:, 4:13]), "ac")
            a_gp = p.v("ac")

        @p.blk.vector
        def _(ve):
            p.w(ve, "ac", a_gp)
            for j in range(4):
                i = ve.tensor_scalar(lhsp[:, j, :], I64t2[:], g_pair[:, j:j + 1], None, OP.mult)
            p.mark(i, "dv")
        v_lh = p.v("dv")

        @p.blk.vector
        def _(ve):
            p.w(ve, "ac", a_gp)
            p.w(ve, "sq", ld_tb)
            ve.tensor_scalar(mb1c[:], TbT1[:, 0, :], g_bc[:, 0:1], None, OP.mult)
            for e in range(1, 9):
                ve.scalar_tensor_tensor(mb1c[:], TbT1[:, e, :], g_bc[:, e:e + 1],
                                        mb1c[:], OP.mult, OP.add)
            ve.tensor_scalar(mb2c[:], TbT2[:, 0, :], g_bc[:, 0:1], None, OP.mult)
            for e in range(1, 9):
                i = ve.scalar_tensor_tensor(mb2c[:], TbT2[:, e, :], g_bc[:, e:e + 1],
                                            mb2c[:], OP.mult, OP.add)
            p.mark(i, "dv")
        mb_done = p.v("dv")
        pre_build = {"pe": gp_pe_end, "dv": v_lh, "ac": a_gp}

        def build(which, pre, twb, R):
            nstrip, W = (16, FF) if which == 1 else (64, DM)
            tw_r = dr["TW1bf" if which == 1 else "TW2bf"].rearrange(
                "s (j p) f -> s p j f", p=128)
            conv_done = p.t["conv1" if which == 1 else "conv2"]
            w_name = "W1_strips" if which == 1 else "W2_strips"
            dst = mW1T if which == 1 else mW2T
            hw_ = W // 2
            nck = hw_ // 512
            with ExitStack() as bes:
                ps_b = _ps(p, bes, f"ff_psb{which}", [64, W])
                cons = {}
                evac = {}
                ldm = {}
                for s in range(nstrip):
                    for j in range(5):
                        pi = s * 5 + j
                        slot = pi % R
                        if j == 4:
                            eng, sem = p.blk.gpsimd, "gq"
                        else:
                            eng = p.blk.gpsimd if pi % 2 == 0 else p.blk.sync
                            sem = "gq" if pi % 2 == 0 else "sq"

                        @eng
                        def _(g, s=s, j=j, pi=pi, slot=slot, sem=sem):
                            if pi <= 1:
                                p.w(g, "gq", conv_done)
                            if pi >= R:
                                p.w(g, "pe", cons[((pi - R) // 5, 1)])
                            if j < 4:
                                p.mark(g.dma_start(out=twb[:, slot, :], in_=tw_r[s, :, j, :]), sem, 16)
                            else:
                                p.mark(g.dma_start(out=twb[0:64, slot, :], in_=dr[w_name][s]), sem, 16)
                            ldm[pi] = (sem, p.v(sem))

                    for h in range(2):
                        @p.blk.tensor
                        def _(te, s=s, h=h):
                            if s == 0 and h == 0:
                                for sem, v in pre.items():
                                    p.w(te, sem, v)
                                p.w(te, "dv", v_lh)
                            if h == 0:
                                gmax = {}
                                for j in range(5):
                                    sm, vv = ldm[s * 5 + j]
                                    gmax[sm] = max(gmax.get(sm, 0), vv)
                                for sm, vv in gmax.items():
                                    p.w(te, sm, vv)
                            if s >= 1:
                                p.w(te, "dv", evac[(s - 1, h)])
                            for j in range(5):
                                slot = (s * 5 + j) % R
                                for c in range(nck):
                                    col = h * hw_ + c * 512
                                    if j < 4:
                                        i = te.matmul(ps_b[:, col:col + 512], lhsp[:, j, :],
                                                      twb[:, slot, col:col + 512],
                                                      start=(j == 0), stop=False)
                                    else:
                                        i = te.matmul(ps_b[:, col:col + 512], I64[:, :],
                                                      twb[0:64, slot, col:col + 512],
                                                      start=False, stop=True)
                            cons[(s, h)] = p.mark(i, "pe")

                        @p.blk.vector
                        def _(ve, s=s, h=h):
                            p.w(ve, "pe", cons[(s, h)])
                            t_i, pb = s // 2, (s % 2) * 64
                            for c in range(nck):
                                col = h * hw_ + c * 512
                                i = ve.tensor_copy(dst[pb:pb + 64, t_i, col:col + 512],
                                                   ps_b[:, col:col + 512])
                            evac[(s, h)] = p.mark(i, "dv")
                return evac[(nstrip - 1, 1)], {"pe": p.v("pe"), "dv": p.v("dv"),
                                               "ac": p.v("ac"), "gq": p.v("gq"), "sq": p.v("sq")}

        ev1, post1 = build(1, pre_build, twb1, 6)
        ring1.close()
        mW2T = _sb(p, es, "ff_w2", [128, NT_F, DM], BF16)
        twb2 = _sb(p, es, "ff_twb2", [128, 10, DM], BF16)
        ev2, post2 = build(2, post1, twb2, 10)
        if _FFN_BARRIERS[3]:
            p.barrier()
        if _DEBUG_FFN:
            @p.blk.gpsimd
            def _(g):
                p.w(g, "ac", ev2)
                p.mark(g.dma_start(out=dr["tq_w1"][:], in_=mW1T[0][:]), "gq", 16)
                p.mark(g.dma_start(out=dr["tq_w2"][:], in_=mW2T[0][:]), "gq", 16)
                for k in range(NT_DM):
                    p.mark(g.dma_start(out=dr["tq_w1f"][k * 128:(k + 1) * 128, :], in_=mW1T[k][:]), "gq", 16)
                for k in range(NT_F):
                    p.mark(g.dma_start(out=dr["tq_w2f"][k * 128:(k + 1) * 128, :], in_=mW2T[k][:]), "gq", 16)

        p.barrier()
        UC = 256
        with ExitStack() as ues:
            u_sb = _sb(p, ues, "ff_u", [128, NT_F, UC], BF16)
            res_c = _sb(p, ues, "ff_res", [128, NT_DM, UC])
            xln_u = _sb(p, ues, "ff_xlnu", [128, NT_DM, UC], BF16)
            ps_u = [_ps(p, ues, f"ff_psu{i}", [128, UC]) for i in range(4)]
            ps_d = [_ps(p, ues, f"ff_psd{i}", [128, UC]) for i in range(4)]
            res_r = res_src.rearrange("(k p) s -> p k s", p=128)
            dst_r = res_dst.rearrange("(k p) s -> p k s", p=128)
            upe = {}
            uac = {}
            dpe = {}
            adddv = {}
            dnlast = {}
            for c in range(S // UC):
                @p.blk.sync
                def _(sync, c=c):
                    if c > 0:
                        p.w(sync, "pe", dnlast[c - 1])
                        p.w(sync, "dv", adddv[(c - 1, NT_DM - 1)])
                    p.mark(sync.dma_start(out=xln_u[:], in_=sp_xln_r[:, :, c * UC:(c + 1) * UC]), "sq", 16)
                    p.mark(sync.dma_start(out=res_c[:], in_=res_r[:, :, c * UC:(c + 1) * UC]), "sq", 16)
                ld_u = p.v("sq")

                ac0_c = p.v("ac")
                for fg in range(NT_F // 4):
                    @p.blk.tensor
                    def _(te, c=c, fg=fg, ld_u=ld_u, ac0_c=ac0_c):
                        if fg == 0:
                            p.w(te, "sq", ld_u)
                            if c == 0:
                                p.w(te, "dv", ev1)
                            elif c > 0:
                                p.w(te, "ac", uac[(c - 1, NT_F - 1)])
                        else:
                            p.w(te, "ac", ac0_c + fg * 4)
                        for ft in range(fg * 4, fg * 4 + 4):
                            for k in range(NT_DM):
                                i = te.matmul(ps_u[ft % 4][:, :],
                                              mW1T[:, k, ft * 128:(ft + 1) * 128],
                                              xln_u[:, k, :], start=(k == 0), stop=(k == NT_DM - 1))
                            upe[(c, ft)] = p.mark(i, "pe")

                dv0_c = p.v("dv")
                for mg in range(NT_DM // 4):
                    @p.blk.tensor
                    def _(te, c=c, mg=mg, ac0_c=ac0_c, dv0_c=dv0_c):
                        if mg == 0:
                            p.w(te, "ac", ac0_c + NT_F)
                            if c == 0:
                                p.w(te, "dv", ev2)
                            elif c > 0:
                                p.w(te, "dv", adddv[(c - 1, NT_DM - 1)])
                        else:
                            p.w(te, "dv", dv0_c + mg * 4)
                        for m in range(mg * 4, mg * 4 + 4):
                            for k in range(NT_F):
                                i = te.matmul(ps_d[m % 4][:, :], mW2T[:, k, m * 128:(m + 1) * 128],
                                              u_sb[:, k, :], start=(k == 0), stop=(k == NT_F - 1))
                            dpe[(c, m)] = p.mark(i, "pe")
                dnlast[c] = dpe[(c, NT_DM - 1)]

                @p.blk.scalar
                def _(sc, c=c):
                    if c > 0:
                        p.w(sc, "pe", dnlast[c - 1])
                    else:
                        p.w(sc, "dv", mb_done)
                    for ft in range(NT_F):
                        p.w(sc, "pe", upe[(c, ft)])
                        i = sc.activation(u_sb[:, ft, :], ps_u[ft % 4][:, :],
                                          AF.Gelu_apprx_tanh, bias=mb1c[:, ft:ft + 1])
                        uac[(c, ft)] = p.mark(i, "ac")
                assert p.v("ac") == ac0_c + NT_F

                @p.blk.vector
                def _(ve, c=c, ld_u=ld_u):
                    for m in range(NT_DM):
                        p.w(ve, "pe", dpe[(c, m)])
                        if m == 0:
                            p.w(ve, "sq", ld_u)
                        i = ve.scalar_tensor_tensor(res_c[:, m, :], ps_d[m % 4][:, :],
                                                    mb2c[:, m:m + 1], res_c[:, m, :],
                                                    OP.add, OP.add)
                        adddv[(c, m)] = p.mark(i, "dv")
                assert p.v("dv") == dv0_c + NT_DM

                @p.blk.sync
                def _(sync, c=c):
                    p.w(sync, "dv", adddv[(c, NT_DM - 1)])
                    p.mark(sync.dma_start(out=dst_r[:, :, c * UC:(c + 1) * UC], in_=res_c[:]), "sq", 16)
    p.barrier()


def _attn(p, dr, res_src, res_dst):
    p.barrier()
    with ExitStack() as es:
        QT = _sb(p, es, "at_qt", [128, NT_DM, S], BF16)
        KT = _sb(p, es, "at_kt", [128, NT_DM, S], BF16)
        VT = _sb(p, es, "at_vt", [128, 16, DM], BF16)
        bvbc = _sb(p, es, "at_bvbc", [128, DM])
        sp_xln_r = dr["sp_xln"].rearrange("(k p) s -> p k s", p=128)
        res_r = res_src.rearrange("(k p) s -> p k s", p=128)
        dst_r = res_dst.rearrange("(k p) s -> p k s", p=128)
        I128, ones1 = p.t["I128"], p.t["ones1"]

        with ExitStack() as bes:
            @p.blk.sync
            def _(sync):
                p.mark(sync.dma_start(out=bvbc[:], in_=bass.AP(dr["bv_r"].tensor, 0, [[0, 128], [1, DM]])), "sq", 16)
            a_bv = ("sq", p.v("sq"))

            wq = [_sb(p, bes, f"at_w{i}", [128, NT_DM, DM], BF16) for i in range(2)]
            xln_ch = _sb(p, bes, "at_xln", [128, NT_DM, CH], BF16)
            ps_q = [_ps(p, bes, f"at_psq{i}", [128, CH]) for i in range(2)]

            @p.blk.gpsimd
            def _(g):
                p.mark(g.dma_start(out=wq[0][:], in_=dr["W_q"].rearrange("(k p) d -> p k d", p=128)), "gq", 16)
                p.mark(g.dma_start(out=wq[1][:], in_=dr["W_k"].rearrange("(k p) d -> p k d", p=128)), "gq", 16)
            ld_wq = p.v("gq")
            ld_wv = {}

            prpe = {}
            prac = {}
            for proj in range(3):
                wt = wq[proj % 2]
                for ch in range(NCH):
                    @p.blk.sync
                    def _(sync, proj=proj, ch=ch):
                        if proj > 0 or ch > 0:
                            prev = prpe.get((proj, ch - 1)) if ch > 0 else prpe[(proj - 1, NCH - 1)]
                            p.w(sync, "pe", prev)
                        p.mark(sync.dma_start(out=xln_ch[:], in_=sp_xln_r[:, :, ch * CH:(ch + 1) * CH]), "sq", 16)
                    ldx = p.v("sq")

                    for o in range(NT_DM):
                        @p.blk.tensor
                        def _(te, proj=proj, ch=ch, o=o, ldx=ldx, wt=wt):
                            if o == 0:
                                p.w(te, "sq", ldx)
                                if proj == 0 and ch == 0:
                                    p.w(te, "gq", ld_wq)
                                if proj == 2 and ch == 0:
                                    p.w(te, "gq", ld_wv[0])
                            if (ch * NT_DM + o) >= 2 or proj > 0:
                                prv = prac.get((proj, ch, o - 2)) or prac.get((proj, ch - 1, NT_DM - 2 + o)) \
                                    or (prac[(proj - 1, NCH - 1, NT_DM - 2 + o)] if proj > 0 and ch == 0 and o < 2 else None)
                                if prv is not None:
                                    p.w(te, prv[0], prv[1])
                            if proj < 2:
                                for k in range(NT_DM):
                                    i = te.matmul(ps_q[o % 2][:, :], wt[:, k, o * 128:(o + 1) * 128],
                                                  xln_ch[:, k, :], start=(k == 0), stop=(k == NT_DM - 1))
                            else:
                                for k in range(NT_DM):
                                    i = te.matmul(ps_q[o % 2][:, :],
                                                  xln_ch[:, k, (o % 4) * 128:(o % 4) * 128 + 128],
                                                  wt[:, k, (o // 4) * 512:(o // 4) * 512 + 512],
                                                  start=(k == 0), stop=(k == NT_DM - 1))
                            prpe[(proj, ch)] = p.mark(i, "pe")

                        if proj < 2:
                            @p.blk.scalar
                            def _(sc, proj=proj, ch=ch, o=o):
                                p.w(sc, "pe", prpe[(proj, ch)])
                                dst = QT if proj == 0 else KT
                                bias = p.t["bq"] if proj == 0 else p.t["bk"]
                                i = sc.activation(dst[:, o, ch * CH:(ch + 1) * CH], ps_q[o % 2][:, :],
                                                  AF.Identity, bias=bias[:, o:o + 1])
                                prac[(proj, ch, o)] = ("ac", p.mark(i, "ac"))
                        else:
                            @p.blk.vector
                            def _(ve, proj=proj, ch=ch, o=o):
                                p.w(ve, "pe", prpe[(proj, ch)])
                                p.w(ve, a_bv[0], a_bv[1])
                                ssub = ch * 4 + (o % 4)
                                dvc = o // 4
                                i = ve.tensor_tensor(VT[:, ssub, dvc * 512:(dvc + 1) * 512],
                                                     ps_q[o % 2][:, :], bvbc[:, dvc * 512:(dvc + 1) * 512],
                                                     OP.add)
                                prac[(proj, ch, o)] = ("dv", p.mark(i, "dv"))

                if proj == 0:
                    @p.blk.gpsimd
                    def _(g):
                        p.w(g, "pe", prpe[(0, NCH - 1)])
                        p.mark(g.dma_start(out=wq[0][:], in_=dr["W_v"].rearrange("(k p) d -> p k d", p=128)), "gq", 16)
                        ld_wv[0] = p.v("gq")

        with ExitStack() as s2:
          PT = _sb(p, s2, "at_pt", [128, 16, S], BF16)
          with ExitStack() as ses:
            Ssb = _sb(p, ses, "at_ssb", [128, S])
            Pf = _sb(p, ses, "at_pf", [128, S])
            nm32 = _sb(p, ses, "at_nm", [128, 1])
            rs = _sb(p, ses, "at_rs", [128, 1])
            rsr = _sb(p, ses, "at_rsr", [128, 1])
            Pbf = _sb(p, ses, "at_pbf", [128, S], BF16)
            ps_sc = _ps(p, ses, "at_pssc", [128, S])
            ps_tr = [_ps(p, ses, f"at_pstr{i}", [128, 128], BF16) for i in range(2)]
            scpe = {}
            smdv = {}
            smac = {}
            trdv = {}
            qk_done = prac[(1, NCH - 1, NT_DM - 1)][1]
            evsc = {}
            for sq_t in range(16):
                @p.blk.tensor
                def _(te, sq_t=sq_t):
                    if sq_t == 0:
                        p.w(te, "ac", qk_done)
                        p.w(te, "dv", prac[(2, NCH - 1, NT_DM - 1)][1])
                    if sq_t > 0:
                        p.w(te, "ac", evsc[sq_t - 1])
                    for ch in range(NCH):
                        for k in range(NT_DM):
                            i = te.matmul(ps_sc[:, ch * CH:(ch + 1) * CH],
                                          QT[:, k, sq_t * 128:(sq_t + 1) * 128],
                                          KT[:, k, ch * CH:(ch + 1) * CH],
                                          start=(k == 0), stop=(k == NT_DM - 1))
                    scpe[sq_t] = p.mark(i, "pe")

                @p.blk.scalar
                def _(sc, sq_t=sq_t):
                    p.w(sc, "pe", scpe[sq_t])
                    if sq_t > 0:
                        p.w(sc, "ac", smac[sq_t - 1])
                    evsc[sq_t] = p.mark(sc.copy(Ssb[:], ps_sc[:]), "ac")

                @p.blk.vector
                def _(ve, sq_t=sq_t):
                    p.w(ve, "ac", evsc[sq_t])
                    v = p.mark(ve.tensor_reduce(nm32[:], Ssb[:], AX.X, OP.max, negate=True), "dv")
                    ve.wait_ge(p.sems["dv"], v)
                    p.mark(ve.tensor_scalar(nm32[:], nm32[:], 0.03125, None, OP.mult), "dv")
                smdv[sq_t] = p.v("dv")

                @p.blk.scalar
                def _(sc, sq_t=sq_t):
                    p.w(sc, "dv", smdv[sq_t])
                    i = sc.activation(Pf[:], Ssb[:], AF.Exp, scale=0.03125, bias=nm32[:],
                                      accum_out=rs[:])
                    smac[sq_t] = p.mark(i, "ac")

                @p.blk.vector
                def _(ve, sq_t=sq_t):
                    p.w(ve, "ac", smac[sq_t])
                    v = p.mark(ve.reciprocal(rsr[:], rs[:]), "dv")
                    ve.wait_ge(p.sems["dv"], v)
                    p.mark(ve.tensor_scalar(Pbf[:], Pf[:], rsr[:], None, OP.mult), "dv")
                v_pn = p.v("dv")

                if _DEBUG_ATT and sq_t == 0:
                    @p.blk.gpsimd
                    def _(g):
                        p.w(g, "dv", v_pn)
                        p.mark(g.dma_start(out=dr["ta_nm"][:], in_=nm32[:]), "gq", 16)
                        p.mark(g.dma_start(out=dr["ta_rs"][:], in_=rs[:]), "gq", 16)
                        p.mark(g.dma_start(out=dr["ta_rsr"][:], in_=rsr[:]), "gq", 16)
                        p.mark(g.dma_start(out=dr["ta_pbf"][:], in_=Pbf[:]), "gq", 16)
                    tapA = p.v("gq")

                    @p.blk.vector
                    def _(ve, tapA=tapA):
                        p.w(ve, "gq", tapA)

                for k in range(16):
                    @p.blk.tensor
                    def _(te, sq_t=sq_t, k=k, v_pn=v_pn):
                        if k == 0:
                            p.w(te, "dv", v_pn)
                        pk = (sq_t, k - 2) if k >= 2 else (sq_t - 1, 14 + k)
                        if pk in trdv:
                            p.w(te, "dv", trdv[pk])
                        p.mark(te.transpose(ps_tr[k % 2][:], Pbf[:, k * 128:(k + 1) * 128], I128[:]), "pe")

                    @p.blk.vector
                    def _(ve, sq_t=sq_t, k=k):
                        p.w(ve, "pe", p.v("pe"))
                        i = ve.tensor_copy(PT[:, k, sq_t * 128:(sq_t + 1) * 128], ps_tr[k % 2][:])
                        trdv[(sq_t, k)] = p.mark(i, "dv")

          if _DEBUG_ATT:
            @p.blk.gpsimd
            def _(g):
                p.w(g, "dv", trdv[(15, 15)])
                p.mark(g.dma_start(out=dr["ta_pt"][:], in_=PT[:, 0, 0:128]), "gq", 16)
                p.mark(g.dma_start(out=dr["ta_qt"][:], in_=QT[:, 0, :]), "gq", 16)
                p.mark(g.dma_start(out=dr["ta_kt"][:], in_=KT[:, 0, :]), "gq", 16)
                p.mark(g.dma_start(out=dr["ta_vt"][:], in_=VT[:, 0, :]), "gq", 16)
          p.barrier()
          with ExitStack() as aes:
            res_c = _sb(p, aes, "at_res", [128, NT_DM, CH])
            ps_av = [_ps(p, aes, f"at_psav{i}", [128, CH]) for i in range(2)]
            avpe = {}
            avdv = {}
            for ch in range(NCH):
                @p.blk.sync
                def _(sync, ch=ch):
                    if ch > 0:
                        p.w(sync, "dv", avdv[(ch - 1, NT_DM - 1)])
                    p.mark(sync.dma_start(out=res_c[:], in_=res_r[:, :, ch * CH:(ch + 1) * CH]), "sq", 16)
                ld_r = p.v("sq")

                for m in range(NT_DM):
                    @p.blk.tensor
                    def _(te, ch=ch, m=m, ld_r=ld_r):
                        fl = ch * NT_DM + m
                        if fl >= 2:
                            p.w(te, "dv", avdv[(ch, m - 2)] if m >= 2 else avdv[(ch - 1, NT_DM - 2 + m)])
                        if fl == 0:
                            p.w(te, "dv", trdv[(15, 15)])
                            p.w(te, "ac", smac[15])
                            p.w(te, "dv", prac[(2, NCH - 1, NT_DM - 1)][1])
                        for k in range(16):
                            i = te.matmul(ps_av[m % 2][:, :], VT[:, k, m * 128:(m + 1) * 128],
                                          PT[:, k, ch * CH:(ch + 1) * CH],
                                          start=(k == 0), stop=(k == 15))
                        avpe[(ch, m)] = p.mark(i, "pe")

                    @p.blk.vector
                    def _(ve, ch=ch, m=m, ld_r=ld_r):
                        p.w(ve, "pe", avpe[(ch, m)])
                        if m == 0:
                            p.w(ve, "sq", ld_r)
                        i = ve.tensor_tensor(res_c[:, m, :], res_c[:, m, :], ps_av[m % 2][:], OP.add)
                        avdv[(ch, m)] = p.mark(i, "dv")

                @p.blk.sync
                def _(sync, ch=ch):
                    p.w(sync, "dv", avdv[(ch, NT_DM - 1)])
                    p.mark(sync.dma_start(out=dst_r[:, :, ch * CH:(ch + 1) * CH], in_=res_c[:]), "sq", 16)
    p.barrier()


def kernel(**inputs):
    shared, xs = _host_prep(inputs)
    nc = _build_program()
    in_maps = []
    for b in range(NCORE):
        m = dict(shared)
        m["x_in"] = xs[b]
        in_maps.append(m)
    res = run_bass_kernel_spmd(nc, in_maps, core_ids=list(range(NCORE)))
    out = np.stack([np.asarray(r["out"], np.float32).T for r in res.results], 0)
    return np.ascontiguousarray(out)
